# version 07
# baseline (speedup 1.0000x reference)
import sys
sys.path.insert(0, "/opt/trn_rl_repo")
from contextlib import ExitStack
import numpy as np
import concourse.bass as bass
import concourse.mybir as mybir
from concourse.bass_utils import run_bass_kernel_spmd

F32 = mybir.dt.float32
BF16 = mybir.dt.bfloat16
AF = mybir.ActivationFunctionType
ALU = mybir.AluOpType
AX = mybir.AxisListType

N_CORES = 8
SEQ = 4096
D = 1024
NTOK = 2 * SEQ
GRP = 256
NGRP = NTOK // GRP
GPB = SEQ // GRP
NSUB = GRP // 128
NCH = D // 128
C_RND = 12582912.0
LN_EPS = 1e-5
NH, DH, CHUNK = 16, 64, 128
NCHK = GRP // CHUNK
DFF = 3072
NFF = DFF // 128
QMAX = 32767.0
CLAMP_MAX = float(np.log(1.0 / 0.01))
ENGS = ["pe", "dve", "act", "pool", "qs", "qg", "qc"]
MAXG_C = 999
MAXG_AB = 999
AB_F32GA = True
AB_GA_N = 8
AB_GA_MM = True
AB_GA_EVAC = True
TR_DUAL = True
TR_EVAC = "dve"
AB_STAGE = 99
C_STAGE = 99
DEV_NO_CC = False


def grows(g):
    b, t = g // GPB, g % GPB
    r0 = b * SEQ + t * GRP
    return r0, r0 + GRP


def prepare_shared_inputs(w):
    f = np.float32
    C = np.ascontiguousarray
    out = {}
    out["w_in_h"] = C(np.asarray(w["L0_attn_w_in"], f))
    out["w_a_h"] = C(np.asarray(w["L0_attn_w_a"], f))
    out["bin_t"] = C(np.asarray(w["L0_attn_b_in"], f).T)
    out["ba_t"] = C(np.asarray(w["L0_attn_b_a"], f).T)
    out["ap_t"] = C(np.asarray(w["L0_attn_a_param"], f).reshape(8, 128).T)
    out["WtT"] = C(np.asarray(w["L0_ffn_W"], f).T)
    out["WpT"] = C(np.asarray(w["L0_ffn_Wp"], f).T)
    out["bp_t"] = C(np.asarray(w["L0_ffn_bp"], f).reshape(2, 128).T)
    out["simc"] = C(np.asarray(w["L0_ffn_sim"], f))
    out["simT"] = C(np.asarray(w["L0_ffn_sim"], f).T)
    out["temp"] = C(np.asarray(w["L0_ffn_temp"], f).reshape(1, 1))
    out["WdallT"] = C(np.concatenate(
        [np.asarray(w[f"L0_ffn_Wd{i}"], f) for i in range(4)], 0).T)
    out["WuallT"] = C(np.concatenate(
        [np.asarray(w[f"L0_ffn_Wu{i}"], f) for i in range(4)], 1).T)
    M = np.zeros((4, 15), dtype=f)
    r0 = 0
    for e, r in enumerate([1, 2, 4, 8]):
        M[e, r0:r0 + r] = 1.0
        r0 += r
    out["Mmap"] = M
    for nm in ["Wq", "Wk", "Wv", "Wg", "Wog", "Wo"]:
        out[nm + "T"] = C(np.asarray(w[f"L1_attn_{nm}"], f).T)
    out["WdT"] = C(np.asarray(w["L1_attn_Wd"], f).T)
    out["bd_c"] = C(np.asarray(w["L1_attn_bd"], f).reshape(16, 1))
    out["normw_t"] = C(np.asarray(w["L1_attn_norm_w"], f).reshape(8, 128).T)
    out["w_gate"] = C(np.asarray(w["L1_ffn_w_gate"], f))
    out["w_up"] = C(np.asarray(w["L1_ffn_w_up"], f))
    out["w_down"] = C(np.asarray(w["L1_ffn_w_down"], f))
    out["ident"] = np.eye(128, dtype=f)
    fc = np.zeros((128, 8), dtype=f)
    fc[:, 0] = 1e-5
    fc[:, 1] = 1.0
    fc[:, 2] = 12582912.0
    fc[:, 3] = -12582912.0
    fc[:, 4] = 65535.0 / 65536.0
    fc[:, 5] = np.float32(np.finfo(np.float32).eps)
    out["fconst"] = fc
    out["ones_row"] = np.ones((1, 128), dtype=f)
    out["ones_col"] = np.ones((128, 1), dtype=f)
    out["triu"] = np.triu(np.ones((128, 128), dtype=f))
    lt = np.zeros((128, 128), dtype=f)
    lt[np.tril_indices(128, -1)] = -1e30
    out["ltneg"] = lt
    return out


SHARED_SPECS = {
    "w_in_h": [8, 128, 128], "w_a_h": [8, 128, 128],
    "bin_t": [128, 8], "ba_t": [128, 8], "ap_t": [128, 8],
    "WtT": [1024, 1024], "WpT": [1024, 256], "bp_t": [128, 2],
    "simc": [256, 4], "simT": [4, 256], "temp": [1, 1],
    "WdallT": [1024, 15], "WuallT": [15, 1024], "Mmap": [4, 15],
    "WqT": [1024, 1024], "WkT": [1024, 1024], "WvT": [1024, 1024],
    "WgT": [1024, 1024], "WogT": [1024, 1024], "WoT": [1024, 1024],
    "WdT": [1024, 16], "bd_c": [16, 1], "normw_t": [128, 8],
    "w_gate": [3072, 1024], "w_up": [3072, 1024], "w_down": [1024, 3072],
    "ident": [128, 128], "ones_row": [1, 128], "ones_col": [128, 1],
    "triu": [128, 128], "fconst": [128, 8], "ltneg": [128, 128],
}


class Prog:
    def __init__(self, nc, es, phases, dev_taps=()):
        self.nc = nc
        self.es = es
        self.phases = list(phases)
        self.dev_taps = set(dev_taps)
        self.t = {}
        self.d = {}
        self.S = {}
        self.C = {}
        self.blk = None
        self.bank_tok = {}

    def dram_in(self, name, shape, dtype=F32):
        self.d[name] = self.nc.dram_tensor(name, list(shape), dtype, kind="ExternalInput")
        return self.d[name]

    def dram_out(self, name, shape, dtype=F32):
        self.d[name] = self.nc.dram_tensor(name, list(shape), dtype, kind="ExternalOutput")
        return self.d[name]

    def dram_tmp(self, name, shape, dtype=F32):
        kind = "ExternalOutput" if name in self.dev_taps else "Internal"
        self.d[name] = self.nc.dram_tensor(name, list(shape), dtype, kind=kind)
        return self.d[name]

    _uid = 0

    def sb(self, name, shape, dtype=F32, es=None):
        Prog._uid += 1
        self.t[name] = (es or self.es).enter_context(
            self.nc.sbuf_tensor(f"sb{Prog._uid}_{name}", list(shape), dtype))
        return self.t[name]

    def ps(self, name, shape, dtype=F32, es=None):
        self.t[name] = (es or self.es).enter_context(
            self.nc.psum_tensor("ps_" + name, list(shape), dtype))
        return self.t[name]

    def sem(self, name):
        if name not in self.S:
            self.S[name] = self.es.enter_context(self.nc.semaphore(name))
            self.C[name] = 0
        return self.S[name]

    def inc(self, instr, name, amt=1):
        instr.then_inc(self.sem(name), amt)
        self.C[name] += amt
        return self.C[name]

    def val(self, name):
        self.sem(name)
        return self.C[name]

    def snap(self):
        return {k: self.val(k) for k in ENGS}

    def mmdt(self, te, d):
        if getattr(self, "_mmdt", None) != d:
            if getattr(self, "_mmdt", None) is not None:
                te.wait_ge(self.S["pe"], self.C["pe"])
            self._mmdt = d

    def wait(self, eng, tok, only=None):
        if tok is None:
            return
        items = tok.items() if isinstance(tok, dict) else tok
        for k, v in items:
            if only is not None and k not in only:
                continue
            if v > 0:
                eng.wait_ge(self.S[k], v)

    def tt(self, name):
        k = name + getattr(self, "sfx", "")
        return self.t[k] if k in self.t else self.t[name]


class FlatBlk:

    def __init__(self, nc):
        self._nc = nc

    def sync(self, f):
        f(self._nc.sync)

    def tensor(self, f):
        f(self._nc.tensor)

    def vector(self, f):
        f(self._nc.vector)

    def scalar(self, f):
        f(self._nc.scalar)

    def gpsimd(self, f):
        f(self._nc.gpsimd)


def dma(p, eng, q, out, in_):
    return p.inc(eng.dma_start(out=out, in_=in_), q, 16)


def emit_barrier(p):
    bar = p.snap()
    blk = p.blk

    @blk.sync
    def _(e):
        p.wait(e, bar)

    @blk.tensor
    def _(e):
        p.wait(e, bar)

    @blk.vector
    def _(e):
        p.wait(e, bar)

    @blk.scalar
    def _(e):
        p.wait(e, bar)

    @blk.gpsimd
    def _(e):
        p.wait(e, bar)
    return bar



def emit_ln(p, src, dst, wait0=None, extra_apply_wait=None, nsub=NSUB):
    blk, t = p.blk, p.t
    st = p.tt("stats")

    @blk.vector
    def _(v):
        p.wait(v, wait0)
        for s in range(nsub):
            ii = v.reduce_sum(st[:, s:s + 1], src[:, s, :], axis=AX.X)
        p.inc(ii, "dve")
    tv1 = p.snap()

    @blk.scalar
    def _(a):
        p.wait(a, wait0)
        p.wait(a, tv1, only=("dve",))
        p.wait(a, extra_apply_wait)
        for s in range(nsub):
            ii = a.activation(dst[:, s, :], src[:, s, :],
                              AF.Square, accum_out=st[:, 4 + s:5 + s])
        p.inc(ii, "act")
    ta1 = p.snap()

    @blk.vector
    def _(v):
        p.wait(v, ta1, only=("act",))
        v.tensor_scalar_mul(st[:, 8:8 + nsub], st[:, 0:nsub], 1.0 / D)
        p.inc(v.tensor_scalar_mul(st[:, 12:12 + nsub], st[:, 4:4 + nsub], 1.0 / D),
              "dve")
        v.wait_ge(p.S["dve"], p.C["dve"])
        p.inc(v.tensor_mul(st[:, 16:16 + nsub], st[:, 8:8 + nsub], st[:, 8:8 + nsub]),
              "dve")
        v.wait_ge(p.S["dve"], p.C["dve"])
        p.inc(v.tensor_sub(st[:, 20:20 + nsub], st[:, 16:16 + nsub],
                           st[:, 12:12 + nsub]), "dve")
    tv2 = p.snap()

    @blk.scalar
    def _(a):
        p.wait(a, tv2, only=("dve",))
        p.inc(a.activation(st[:, 24:24 + nsub], st[:, 20:20 + nsub], AF.Sqrt,
                           scale=-1.0, bias=t["fconst"][:, 0:1]), "act")
    ta2 = p.snap()

    @blk.vector
    def _(v):
        p.wait(v, ta2, only=("act",))
        p.inc(v.reciprocal(st[:, 28:28 + nsub], st[:, 24:24 + nsub]), "dve")
        v.wait_ge(p.S["dve"], p.C["dve"])
        p.inc(v.scalar_tensor_tensor(st[:, 32:32 + nsub], st[:, 8:8 + nsub], -1.0,
                                     st[:, 28:28 + nsub], ALU.mult, ALU.mult), "dve")
    tv3 = p.snap()

    @blk.scalar
    def _(a):
        p.wait(a, tv3, only=("dve",))
        p.wait(a, extra_apply_wait)
        for s in range(nsub):
            ii = a.activation(dst[:, s, :], src[:, s, :], AF.Identity,
                              bias=st[:, 32 + s:33 + s], scale=st[:, 28 + s:29 + s])
        p.inc(ii, "act")
    return p.snap()


def emit_tr_fm(p, src_tm, dst_fm, wait_tok, dtype_note="bf16", banks=(0, 1, 2, 3),
               evac_eng="dve", dual_f32=None, nsub=NSUB):
    blk, t = p.blk, p.t
    fills = [(s, cq) for s in range(nsub) for cq in range(2)]
    for fi, (s, cq) in enumerate(fills):
        @blk.tensor
        def _(te, s=s, cq=cq):
            p.wait(te, wait_tok)
            p.mmdt(te, "f")
            for j in range(4):
                bn = f"pb{banks[j]}"
                p.wait(te, p.bank_tok.get(bn))
                ch = cq * 4 + j
                p.inc(te.transpose(t[bn][:, 0:128],
                                   src_tm[:, s, ch * 128:(ch + 1) * 128],
                                   t["ident"][:]), "pe")
        tpe = p.snap()

        if evac_eng == "dve":
            @blk.vector
            def _(v, s=s, cq=cq):
                p.wait(v, tpe, only=("pe",))
                for j in range(4):
                    ii = v.tensor_copy(
                        dst_fm[:, cq * 4 + j, s * 128:(s + 1) * 128],
                        t[f"pb{banks[j]}"][:, 0:128])
                    if dual_f32 is not None:
                        ii = v.tensor_copy(
                            dual_f32[:, cq * 4 + j, s * 128:(s + 1) * 128],
                            t[f"pb{banks[j]}"][:, 0:128])
                p.inc(ii, "dve")
        else:
            @blk.scalar
            def _(a, s=s, cq=cq):
                p.wait(a, tpe, only=("pe",))
                for j in range(4):
                    ii = a.copy(dst_fm[:, cq * 4 + j, s * 128:(s + 1) * 128],
                                t[f"pb{banks[j]}"][:, 0:128])
                    if dual_f32 is not None:
                        ii = a.copy(dual_f32[:, cq * 4 + j, s * 128:(s + 1) * 128],
                                    t[f"pb{banks[j]}"][:, 0:128])
                p.inc(ii, "act")
        tokc = p.snap()
        for j in range(4):
            p.bank_tok[f"pb{banks[j]}"] = tokc
    return p.snap()



def alloc_common(p):
    p.sb("stats", [128, 40], F32)
    p.sb("tiny", [128, 16], F32)
    p.sb("sq_scr", [128, NSUB, 1024], F32)
    p.sb("state0", [128, NCH], F32)
    p.sb("msp8", [128, 8], F32)
    p.sb("gmax", [128, 8], F32)
    p.sb("glob", [16, 128], F32)
    for nm in ["ident", "ones_row", "ones_col", "triu", "bin_t", "ba_t",
               "bp_t", "bd_c", "normw_t", "fconst", "ltneg"]:
        p.sb(nm, SHARED_SPECS[nm], F32)
    p.sb("ones16", [16, 128], F32)
    p.sb("mtop", [1, 128], F32)
    p.sb("mbot", [1, 128], F32)
    p.sb("identb", [128, 128], BF16)
    p.sb("Mmap", [4, 15], F32)
    p.sb("rstd_es", [4, 1], F32)
    p.sb("ap_s", [128, 8], F32)
    p.sb("ab_s", [128, 8], F32)
    p.sb("sim_s", [4, 256], F32)
    p.sb("sc_s", [4, 8], F32)
    for i in range(7):
        p.ps(f"pb{i}", [128, 512], F32)
    p.ps("pbt", [128, 1024], BF16)


def emit_setup(p):
    blk, t = p.blk, p.t

    @blk.sync
    def _(sync):
        for nm in ["ident", "ones_row", "ones_col", "triu", "bin_t", "ba_t",
                   "bp_t", "Mmap", "bd_c", "normw_t", "fconst", "ltneg"]:
            dma(p, sync, "qs", t[nm][:], p.d[nm].ap())
        dma(p, sync, "qs", t["ap_s"][:], p.d["ap_t"].ap())
        dma(p, sync, "qs", t["sim_s"][:], p.d["simT"].ap())
        dma(p, sync, "qs", t["sc_s"][0:1, 0:1], p.d["temp"].ap())
    tok_l = p.snap()

    @blk.scalar
    def _(a):
        p.wait(a, tok_l, only=("qs",))
        a.copy(t["identb"][:], t["ident"][:])
        p.inc(a.activation(t["ab_s"][:], t["ap_s"][:], AF.Abs), "act")
        a.wait_ge(p.S["act"], p.C["act"])
        p.inc(a.activation(t["msp8"][:], t["ab_s"][:], AF.Exp, scale=-1.0), "act")
        a.wait_ge(p.S["act"], p.C["act"])
        p.inc(a.activation(t["ab_s"][:], t["msp8"][:], AF.Ln,
                           bias=t["fconst"][:, 1:2]), "act")
        p.inc(a.activation(t["sim_s"][:], t["sim_s"][:], AF.Square,
                           accum_out=t["sc_s"][0:4, 1:2]), "act")
    ta = p.snap()

    @blk.vector
    def _(v):
        p.wait(v, ta, only=("act",))
        p.inc(v.tensor_scalar_max(t["msp8"][:], t["ap_s"][:], 0.0), "dve")
        v.wait_ge(p.S["dve"], p.C["dve"])
        p.inc(v.tensor_add(t["msp8"][:], t["msp8"][:], t["ab_s"][:]), "dve")
        v.wait_ge(p.S["dve"], p.C["dve"])
        p.inc(v.tensor_scalar_mul(t["msp8"][:], t["msp8"][:], -8.0), "dve")
        v.memset(t["state0"][:], 0.0)
        v.memset(t["gmax"][:], 0.0)
        v.memset(t["glob"][:], 0.0)
        v.memset(t["ones16"][:], 1.0)
        v.memset(t["mtop"][0:1, 0:64], 1.0)
        v.memset(t["mtop"][0:1, 64:128], 0.0)
        v.memset(t["mbot"][0:1, 0:64], 0.0)
        v.memset(t["mbot"][0:1, 64:128], 1.0)
        p.inc(v.tensor_scalar_min(t["sc_s"][0:1, 0:1], t["sc_s"][0:1, 0:1], CLAMP_MAX),
              "dve")
    tv = p.snap()

    @blk.scalar
    def _(a):
        p.wait(a, tv, only=("dve",))
        p.inc(a.activation(t["sc_s"][0:1, 2:3], t["sc_s"][0:1, 0:1], AF.Exp), "act")
        p.inc(a.activation(t["sc_s"][0:4, 3:4], t["sc_s"][0:4, 1:2], AF.Sqrt), "act")
    ta = p.snap()

    @blk.tensor
    def _(te):
        p.wait(te, ta, only=("act",))
        p.mmdt(te, "f")
        p.inc(te.matmul(t["pb6"][0:4, 0:1], t["ones_row"][0:1, 0:4],
                        t["sc_s"][0:1, 2:3], start=True, stop=True), "pe")
    tpe = p.snap()

    @blk.vector
    def _(v):
        p.wait(v, tpe, only=("pe",))
        p.inc(v.tensor_scalar_max(t["sc_s"][0:4, 5:6], t["sc_s"][0:4, 3:4], 1e-12),
              "dve")
        v.wait_ge(p.S["dve"], p.C["dve"])
        p.inc(v.reciprocal(t["sc_s"][0:4, 4:5], t["sc_s"][0:4, 5:6]), "dve")
        v.wait_ge(p.S["dve"], p.C["dve"])
        p.inc(v.tensor_mul(t["rstd_es"][:], t["sc_s"][0:4, 4:5], t["pb6"][0:4, 0:1]),
              "dve")
    p.bank_tok["pb6"] = p.snap()


def emit_l0_weights(p, es):
    p.sb("w_in_b", [128, 8, 128], BF16, es=es)
    p.sb("w_a_f", [128, 8, 128], F32, es=es)
    p.sb("WtT_b", [128, NCH, 1024], BF16, es=es)
    p.sb("WpT_b", [128, NCH, 256], BF16, es=es)
    p.sb("WdallT_b", [128, NCH, 15], BF16, es=es)
    p.sb("WuallT_b", [15, 1024], BF16, es=es)
    p.sb("simc_t", [128, 2, 4], F32, es=es)
    blk, t = p.blk, p.t

    @blk.gpsimd
    def _(g):
        for h in range(8):
            dma(p, g, "qg", t["w_in_b"][:, h, :], p.d["w_in_h"].ap()[h])
            dma(p, g, "qg", t["w_a_f"][:, h, :], p.d["w_a_h"].ap()[h])
        dma(p, g, "qg", t["WtT_b"][:],
            p.d["WtT"].ap().rearrange("(c q) o -> q c o", q=128))
        dma(p, g, "qg", t["WpT_b"][:],
            p.d["WpT"].ap().rearrange("(c q) o -> q c o", q=128))
        dma(p, g, "qg", t["WdallT_b"][:],
            p.d["WdallT"].ap().rearrange("(c q) o -> q c o", q=128))
        dma(p, g, "qg", t["WuallT_b"][:], p.d["WuallT"].ap())
        dma(p, g, "qg", t["simc_t"][:],
            p.d["simc"].ap().rearrange("(c q) o -> q c o", q=128))
    return p.snap()



def alloc_ab(p, es):
    p.sb("sx0", [128, NSUB, 1024], F32, es=es)
    p.sb("sx1", [128, NSUB, 1024], F32, es=es)
    p.sb("tmw", [128, NSUB, 1024], F32, es=es)
    for nm in ["gx", "ga", "a_t", "mlt", "u_t"]:
        p.sb(nm, [128, NCH, GRP], F32, es=es)
    p.t["h_t"] = p.t["gx"]
    p.t["ob"] = p.t["a_t"][:].rearrange("q (s x) g -> q s (x g)", s=2)
    p.sb("xnb", [128, NCH, GRP], BF16, es=es)
    p.sb("xnf", [128, NCH, GRP], F32, es=es)
    p.sb("projf", [128, 2, GRP], F32, es=es)
    p.sb("lg", [4, GRP], F32, es=es)
    p.sb("rnorm", [1, GRP], F32, es=es)
    p.sb("rnorm2", [1, GRP], F32, es=es)
    p.sb("lw_tm", [128, NSUB * 4], F32, es=es)
    p.sb("ret_tm", [128, NSUB * 4], F32, es=es)
    p.sb("ret_fm", [4, GRP], F32, es=es)
    p.sb("tmpf", [15, GRP], F32, es=es)
    p.sb("tmp2b", [15, GRP], BF16, es=es)
    p.sb("tmwB", [128, NSUB, 1024], F32, es=es)
    for nm in ["gx", "ga", "a_t", "mlt", "u_t"]:
        p.sb(nm + "B", [128, NCH, GRP], F32, es=es)
    p.t["h_tB"] = p.t["gxB"]
    p.t["obB"] = p.t["a_tB"][:].rearrange("q (s x) g -> q s (x g)", s=2)
    p.sb("xnbB", [128, NCH, GRP], BF16, es=es)
    p.sb("xnfB", [128, NCH, GRP], F32, es=es)
    p.sb("projfB", [128, 2, GRP], F32, es=es)
    p.sb("lgB", [4, GRP], F32, es=es)
    p.sb("rnormB", [1, GRP], F32, es=es)
    p.sb("rnorm2B", [1, GRP], F32, es=es)
    p.sb("lw_tmB", [128, NSUB * 4], F32, es=es)
    p.sb("ret_tmB", [128, NSUB * 4], F32, es=es)
    p.sb("ret_fmB", [4, GRP], F32, es=es)
    p.sb("tmpfB", [15, GRP], F32, es=es)
    p.sb("tmp2bB", [15, GRP], BF16, es=es)
    p.sb("statsB", [128, 40], F32, es=es)
    p.sb("tinyB", [128, 16], F32, es=es)
    p.sb("sq_scrB", [128, 1, 512], F32, es=es)
    p.sb("state0B", [128, NCH], F32, es=es)


def emit_ab_group(p, g, prev, nxt=None):
    blk, t = p.blk, p.t
    r0, r1 = grows(g)
    sx = t["sx0"] if p.sfx == "" else t["sx1"]
    is_seq_start = (g % GPB == 0)

    if nxt is not None:
        nxt_g, nxt_prev = nxt
        nx = t["sx1"] if p.sfx == "" else t["sx0"]
        nr0, nr1 = grows(nxt_g)

        @blk.sync
        def _(sync):
            p.wait(sync, nxt_prev.get("slot_free"))
            dma(p, sync, "qs", nx[:],
                p.d["x"].ap()[nr0:nr1, :].rearrange("(s q) d -> q s d", q=128))
            prev["load_tok"][nxt_g] = {"qs": p.val("qs")}

    tok_x = prev["load_tok"][g]
    w0 = dict(tok_x)
    w0.update(prev.get("xgrp", {}))

    ta_apply = emit_ln(p, sx, p.tt("tmw"), wait0=w0)
    dual_f32 = p.tt("xnf")
    if AB_STAGE < 0.5:
        prev["xgrp"] = {"dve": p.val("dve"), "act": p.val("act")}
        prev["slot_free"] = {"dve": p.val("dve")}
        prev["ob_store"] = prev.get("ob_store")
        return
    if g == 0 and "dev_dbg" in p.dev_taps and AB_STAGE != 0.55:
        @blk.gpsimd
        def _(gp):
            p.wait(gp, ta_apply, only=("act",))
            dma(p, gp, "qg", p.d["dbg_xn"].ap(), p.tt("tmw")[:])
            dma(p, gp, "qg", p.d["dbg_sx"].ap(), sx[:])
            dma(p, gp, "qg", p.d["dbg_st"].ap(), t["stats"][:])
    tok_xnb = emit_tr_fm(p, p.tt("tmw"), p.tt("xnb"), ta_apply,
                         evac_eng=TR_EVAC,
                         dual_f32=p.tt("xnf") if TR_DUAL else None)
    if AB_STAGE < 0.8:
        prev["xgrp"] = {"dve": p.val("dve"), "act": p.val("act")}
        prev["slot_free"] = {"dve": p.val("dve")}
        return

    gate_list = [("w_in_b", "bin_t", "gx"), ("w_a_f", "ba_t", "ga")]
    if AB_STAGE < 0.9:
        gate_list = gate_list[:1]
    for wi, (wt, bt, dst) in enumerate(gate_list):
        for h in range(8 if wi == 0 else AB_GA_N):
            bn = f"pb{2 + 2 * wi + h % 2}"
            bank = t[bn]

            if wi == 0 or AB_GA_MM:
                @blk.tensor
                def _(te, h=h, wt=wt, bank=bank, bn=bn):
                    p.wait(te, tok_xnb, only=("dve", "act"))
                    p.wait(te, prev.get("wtok"), only=("qg",))
                    p.wait(te, p.bank_tok.get(bn))
                    use_f = (wt != "w_in_b") and AB_F32GA
                    p.mmdt(te, "f" if use_f else "b")
                    rhs = p.tt("xnf") if use_f else p.tt("xnb")
                    lhs = t[wt] if use_f or wt == "w_in_b" else t["w_in_b"]
                    if wt == "w_a_f" and not AB_F32GA:
                        lhs = t["w_in_b"]
                    p.inc(te.matmul(bank[:, 0:GRP], lhs[:, h, :], rhs[:, h, :],
                                    start=True, stop=True), "pe")
            tpe = p.snap()

            if wi == 0 or AB_GA_EVAC:
                @blk.scalar
                def _(a, h=h, bt=bt, dst=dst, bank=bank, bn=bn):
                    p.wait(a, tpe, only=("pe",))
                    p.inc(a.activation(p.tt(dst)[:, h, :], bank[:, 0:GRP], AF.Sigmoid,
                                       bias=t[bt][:, h:h + 1]), "act")
            p.bank_tok[bn] = p.snap()
    tok_gates = p.snap()
    if AB_STAGE < 2:
        prev["xgrp"] = {"dve": p.val("dve"), "act": p.val("act")}
        prev["slot_free"] = {"dve": p.val("dve")}
        return

    @blk.vector
    def _(v):
        p.wait(v, tok_gates, only=("act",))
        msp = t["msp8"][:].unsqueeze(2).broadcast_to([128, 8, GRP])
        p.inc(v.tensor_tensor(p.tt("mlt")[:], p.tt("ga")[:], msp, ALU.mult), "dve")
    tv = p.snap()

    @blk.scalar
    def _(a):
        p.wait(a, tv, only=("dve",))
        p.wait(a, prev.get("ob_store"), only=("qg",))
        a.activation(p.tt("a_t")[:], p.tt("mlt")[:], AF.Exp)
        a.activation(p.tt("ga")[:], p.tt("mlt")[:], AF.Exp, scale=2.0)
        p.inc(a.activation(p.tt("mlt")[:], p.tt("ga")[:], AF.Sqrt, scale=-1.0,
                           bias=t["fconst"][:, 1:2]), "act")
    ta = p.snap()

    @blk.vector
    def _(v):
        p.wait(v, ta, only=("act",))
        v.tensor_mul(p.tt("u_t")[:], p.tt("xnf")[:], p.tt("gx")[:])
        p.inc(v.tensor_mul(p.tt("u_t")[:], p.tt("u_t")[:], p.tt("mlt")[:]), "dve")
        if is_seq_start:
            v.tensor_mul(p.tt("u_t")[:, :, 0:1], p.tt("xnf")[:, :, 0:1], p.tt("gx")[:, :, 0:1])
            p.inc(v.memset(p.tt("state0")[:], 0.0), "dve")
        v.wait_ge(p.S["dve"], p.C["dve"])
        last = None
        for ch in range(NCH):
            last = v.tensor_tensor_scan(p.tt("h_t")[:, ch, :], p.tt("a_t")[:, ch, :],
                                        p.tt("u_t")[:, ch, :], p.tt("state0")[:, ch:ch + 1],
                                        ALU.mult, ALU.add)
        p.inc(last, "dve")
        v.wait_ge(p.S["dve"], p.C["dve"])
        for ch in range(NCH):
            last = v.tensor_copy(p.tt("state0")[:, ch:ch + 1], p.tt("h_t")[:, ch, GRP - 1:GRP])
        p.inc(last, "dve")
    tok_scan = p.snap()
    if g == 0 and "dev_dbg" in p.dev_taps:
        @blk.gpsimd
        def _(gp):
            p.wait(gp, tok_scan, only=("dve",))
            for nm in ["gx", "ga", "at", "ut", "ht"]:
                key = {"at": "a_t", "ut": "u_t", "ht": "h_t"}.get(nm, nm)
                dma(p, gp, "qg", p.d["dbg_" + nm].ap(), p.tt(key)[:])

    fills = [(s, cq) for s in range(NSUB) for cq in range(2)]
    for fi, (s, cq) in enumerate(fills):
        @blk.tensor
        def _(te, s=s, cq=cq):
            p.wait(te, tok_scan, only=("dve",))
            p.mmdt(te, "f")
            for j in range(4):
                bn = f"pb{j}"
                p.wait(te, p.bank_tok.get(bn))
                ch = cq * 4 + j
                ii = te.transpose(t[bn][:, 0:128],
                                  p.tt("h_t")[:, ch, s * 128:(s + 1) * 128], t["ident"][:])
            p.inc(ii, "pe")
        tpe = p.snap()

        @blk.scalar
        def _(a, s=s, cq=cq):
            p.wait(a, tpe, only=("pe",))
            for j in range(4):
                ii = a.copy(p.tt("tmw")[:, s, cq * 512 + j * 128:cq * 512 + (j + 1) * 128],
                            t[f"pb{j}"][:, 0:128])
            p.inc(ii, "act")
        tokc = p.snap()
        for j in range(4):
            p.bank_tok[f"pb{j}"] = tokc
    tok_y = p.snap()

    if g == 0 and "dev_dbg" in p.dev_taps:
        @blk.gpsimd
        def _(gp):
            p.wait(gp, tok_y, only=("act",))
            dma(p, gp, "qg", p.d["dbg_y"].ap(), p.tt("tmw")[:])

    ta = emit_ln(p, p.tt("tmw"), p.tt("ob"), wait0={"act": tok_y["act"]},
                 extra_apply_wait=prev.get("ob_store"))

    @blk.vector
    def _(v):
        p.wait(v, ta, only=("act",))
        p.inc(v.tensor_add(sx[:], p.tt("ob")[:], sx[:]), "dve")
    tok_x1 = p.snap()

    if "dev_x1" in p.dev_taps:
        @blk.gpsimd
        def _(gp):
            p.wait(gp, tok_x1, only=("dve",))
            dma(p, gp, "qg",
                p.d["dev_x1"].ap()[r0:r1, :].rearrange("(s q) d -> q s d", q=128),
                sx[:])

    if AB_STAGE < 3:
        prev["xgrp"] = {"dve": p.val("dve"), "act": p.val("act")}
        prev["slot_free"] = {"dve": p.val("dve")}
        return

    ta_apply2 = emit_ln(p, sx, p.tt("tmw"), wait0={"dve": tok_x1["dve"]})
    tok_xnb2 = emit_tr_fm(p, p.tt("tmw"), p.tt("xnb"), ta_apply2, evac_eng="dve")

    for oc in range(2):
        bn = f"pb{2 + oc}"
        bank = t[bn]

        @blk.tensor
        def _(te, oc=oc, bank=bank, bn=bn):
            p.wait(te, tok_xnb2, only=("dve",))
            p.wait(te, p.bank_tok.get(bn))
            p.mmdt(te, "b")
            for kc in range(NCH):
                ii = te.matmul(bank[:, 0:GRP],
                               t["WpT_b"][:, kc, oc * 128:(oc + 1) * 128],
                               p.tt("xnb")[:, kc, :], start=(kc == 0), stop=(kc == NCH - 1))
            p.inc(ii, "pe")
        tpe = p.snap()

        @blk.scalar
        def _(a, oc=oc, bank=bank, bn=bn):
            p.wait(a, tpe, only=("pe",))
            p.inc(a.activation(p.tt("projf")[:, oc, :], bank[:, 0:GRP], AF.Identity,
                               bias=t["bp_t"][:, oc:oc + 1]), "act")
        p.bank_tok[bn] = p.snap()
    tok_proj = p.snap()

    @blk.scalar
    def _(a):
        p.wait(a, tok_proj, only=("act",))
        p.inc(a.activation(p.tt("sq_scr")[:, 0, 0:2 * GRP]
                           .rearrange("q (c n) -> q c n", c=2),
                           p.tt("projf")[:], AF.Square), "act")
    ta_sq = p.snap()

    @blk.tensor
    def _(te):
        p.wait(te, ta_sq, only=("act",))
        p.wait(te, p.bank_tok.get("pb4"))
        p.mmdt(te, "f")
        bank = t["pb4"]
        for c in range(2):
            ii = te.matmul(bank[0:1, 0:GRP], t["ones_col"][:, 0:1],
                           p.tt("sq_scr")[:, 0, c * GRP:(c + 1) * GRP],
                           start=(c == 0), stop=(c == 1))
        p.inc(ii, "pe")
        te.wait_ge(p.S["pe"], p.C["pe"])
        for c in range(2):
            ii = te.matmul(bank[0:4, 256:256 + GRP], t["simc_t"][:, c, :],
                           p.tt("projf")[:, c, :], start=(c == 0), stop=(c == 1))
        p.inc(ii, "pe")
    tok_nrm = p.snap()

    @blk.scalar
    def _(a):
        p.wait(a, tok_nrm, only=("pe",))
        p.inc(a.activation(p.tt("rnorm")[:], t["pb4"][0:1, 0:GRP], AF.Sqrt), "act")
    ta = p.snap()

    @blk.vector
    def _(v):
        p.wait(v, ta, only=("act",))
        p.inc(v.tensor_scalar_max(p.tt("rnorm2")[:], p.tt("rnorm")[:], 1e-12), "dve")
        v.wait_ge(p.S["dve"], p.C["dve"])
        p.inc(v.reciprocal(p.tt("rnorm")[:], p.tt("rnorm2")[:]), "dve")
        v.wait_ge(p.S["dve"], p.C["dve"])
        p.inc(v.tensor_scalar(p.tt("lg")[:], t["pb4"][0:4, 256:256 + GRP],
                              t["rstd_es"][:], None, ALU.mult), "dve")
    tok_lg = p.snap()

    @blk.tensor
    def _(te):
        p.wait(te, tok_lg, only=("dve",))
        p.mmdt(te, "f")
        p.inc(te.matmul(t["pb4"][0:4, 0:GRP], t["ones_row"][0:1, 0:4],
                        p.tt("rnorm")[:], start=True, stop=True), "pe")
    tpe = p.snap()

    @blk.vector
    def _(v):
        p.wait(v, tpe, only=("pe",))
        p.inc(v.tensor_mul(p.tt("lg")[:], p.tt("lg")[:], t["pb4"][0:4, 0:GRP]), "dve")
    p.bank_tok["pb4"] = p.snap()
    tok_lg2 = p.snap()

    @blk.tensor
    def _(te):
        p.wait(te, tok_lg2, only=("dve",))
        p.mmdt(te, "f")
        for s in range(NSUB):
            bn = f"pb{s}"
            p.wait(te, p.bank_tok.get(bn))
            ii = te.transpose(t[bn][:, 0:4],
                              p.tt("lg")[0:4, s * 128:(s + 1) * 128], t["ident"][0:4, 0:4])
        p.inc(ii, "pe")
    tpe = p.snap()

    @blk.vector
    def _(v):
        p.wait(v, tpe, only=("pe",))
        for s in range(NSUB):
            ii = v.tensor_copy(p.tt("lw_tm")[:, s * 4:(s + 1) * 4], t[f"pb{s}"][:, 0:4])
        p.inc(ii, "dve")
    tokc = p.snap()
    for s in range(NSUB):
        p.bank_tok[f"pb{s}"] = tokc

    @blk.vector
    def _(v):
        v.wait_ge(p.S["dve"], p.C["dve"])
        lw3 = p.tt("lw_tm")[:].rearrange("q (s e) -> q s e", e=4)
        st = p.tt("tiny")
        p.inc(v.reduce_max(st[:, 0:NSUB], lw3, axis=AX.X), "dve")
        v.wait_ge(p.S["dve"], p.C["dve"])
        p.inc(v.tensor_scalar_mul(st[:, 2:2 + NSUB], st[:, 0:NSUB], -1.0), "dve")
    tv = p.snap()

    @blk.scalar
    def _(a):
        p.wait(a, tv, only=("dve",))
        for s in range(NSUB):
            ii = a.activation(p.tt("lw_tm")[:, s * 4:(s + 1) * 4],
                              p.tt("lw_tm")[:, s * 4:(s + 1) * 4],
                              AF.Exp, bias=p.tt("tiny")[:, 2 + s:3 + s])
        p.inc(ii, "act")
    ta = p.snap()

    @blk.vector
    def _(v):
        p.wait(v, ta, only=("act",))
        lw3 = p.tt("lw_tm")[:].rearrange("q (s e) -> q s e", e=4)
        ret3 = p.tt("ret_tm")[:].rearrange("q (s e) -> q s e", e=4)
        st = p.tt("tiny")
        p.inc(v.reduce_sum(st[:, 4:4 + NSUB], lw3, axis=AX.X), "dve")
        v.wait_ge(p.S["dve"], p.C["dve"])
        p.inc(v.reciprocal(st[:, 6:6 + NSUB], st[:, 4:4 + NSUB]), "dve")
        v.wait_ge(p.S["dve"], p.C["dve"])
        rs = st[:, 6:6 + NSUB].unsqueeze(2).broadcast_to([128, NSUB, 4])
        p.inc(v.tensor_tensor(lw3, lw3, rs, ALU.mult), "dve")
        v.wait_ge(p.S["dve"], p.C["dve"])
        p.inc(v.reduce_max(st[:, 8:8 + NSUB], lw3, axis=AX.X), "dve")
        v.wait_ge(p.S["dve"], p.C["dve"])
        rm = st[:, 8:8 + NSUB].unsqueeze(2).broadcast_to([128, NSUB, 4])
        p.inc(v.tensor_tensor(ret3, lw3, rm, ALU.is_ge), "dve")
        v.wait_ge(p.S["dve"], p.C["dve"])
        p.inc(v.tensor_tensor(ret3, ret3, lw3, ALU.mult), "dve")
    tok_ret = p.snap()

    @blk.tensor
    def _(te):
        p.wait(te, tok_ret, only=("dve",))
        p.mmdt(te, "f")
        for s in range(NSUB):
            bn = f"pb{s}"
            p.wait(te, p.bank_tok.get(bn))
            ii = te.transpose(t[bn][0:4, 0:128],
                              p.tt("ret_tm")[:, s * 4:(s + 1) * 4], t["ident"][:])
        p.inc(ii, "pe")
    tpe = p.snap()

    @blk.vector
    def _(v):
        p.wait(v, tpe, only=("pe",))
        for s in range(NSUB):
            ii = v.tensor_copy(p.tt("ret_fm")[0:4, s * 128:(s + 1) * 128],
                               t[f"pb{s}"][0:4, 0:128])
        p.inc(ii, "dve")
    tokc = p.snap()
    for s in range(NSUB):
        p.bank_tok[f"pb{s}"] = tokc
    tok_retfm = p.snap()

    @blk.tensor
    def _(te):
        p.wait(te, tok_retfm, only=("dve",))
        p.wait(te, p.bank_tok.get("pb5"))
        p.mmdt(te, "f")
        p.inc(te.matmul(t["pb5"][0:15, 0:GRP], t["Mmap"][:], p.tt("ret_fm")[:],
                        start=True, stop=True), "pe")
        te.wait_ge(p.S["pe"], p.C["pe"])
        p.mmdt(te, "b")
        for kc in range(NCH):
            ii = te.matmul(t["pb5"][0:15, 256:256 + GRP], t["WdallT_b"][:, kc, :],
                           p.tt("xnb")[:, kc, :], start=(kc == 0), stop=(kc == NCH - 1))
        p.inc(ii, "pe")
    tok_exp = p.snap()

    @blk.scalar
    def _(a):
        p.wait(a, tok_exp, only=("pe",))
        p.inc(a.copy(p.tt("tmpf")[:], t["pb5"][0:15, 256:256 + GRP]), "act")
    ta = p.snap()

    @blk.vector
    def _(v):
        p.wait(v, ta, only=("act",))
        p.inc(v.tensor_mul(p.tt("tmp2b")[:], p.tt("tmpf")[:], t["pb5"][0:15, 0:GRP]), "dve")
    p.bank_tok["pb5"] = p.snap()
    tok_tmp2 = p.snap()

    if g == 0 and "dev_dbg" in p.dev_taps:
        @blk.gpsimd
        def _(gp):
            p.wait(gp, tok_tmp2, only=("dve",))
            dma(p, gp, "qg", p.d["dbg_projf"].ap(), p.tt("projf")[:])
            dma(p, gp, "qg", p.d["dbg_lg"].ap(), p.tt("lg")[:])
            dma(p, gp, "qg", p.d["dbg_ret"].ap(), p.tt("ret_tm")[:])
            dma(p, gp, "qg", p.d["dbg_tmpf"].ap(), p.tt("tmpf")[:])

    if AB_STAGE < 4:
        prev["xgrp"] = {"dve": p.val("dve"), "act": p.val("act")}
        prev["slot_free"] = {"dve": p.val("dve")}
        return

    for s in range(NSUB):
        for oh in range(2):
            i = s * 2 + oh
            bn = f"pb{2 + i % 2}"
            bank = t[bn]

            @blk.tensor
            def _(te, s=s, oh=oh, bank=bank, bn=bn):
                p.wait(te, tok_tmp2, only=("dve",))
                p.wait(te, p.bank_tok.get(bn))
                p.mmdt(te, "b")
                for kc in range(NCH):
                    te.matmul(bank[:], p.tt("xnb")[:, kc, s * 128:(s + 1) * 128],
                              t["WtT_b"][:, kc, oh * 512:(oh + 1) * 512],
                              start=(kc == 0), stop=False)
                p.inc(te.matmul(bank[:], p.tt("tmp2b")[0:15, s * 128:(s + 1) * 128],
                                t["WuallT_b"][0:15, oh * 512:(oh + 1) * 512],
                                start=False, stop=True), "pe")
            tpe = p.snap()

            @blk.vector
            def _(v, s=s, oh=oh, bank=bank, bn=bn):
                p.wait(v, tpe, only=("pe",))
                p.inc(v.tensor_copy(p.tt("tmw")[:, s, oh * 512:(oh + 1) * 512], bank[:]),
                      "dve")
            p.bank_tok[bn] = p.snap()
    tok_y2 = p.snap()

    if g == 0 and "dev_dbg" in p.dev_taps:
        @blk.gpsimd
        def _(gp):
            p.wait(gp, tok_y2, only=("dve",))
            dma(p, gp, "qg", p.d["dbg_y2"].ap(), p.tt("tmw")[:])

    ta = emit_ln(p, p.tt("tmw"), p.tt("ob"), wait0={"dve": tok_y2["dve"]},
                 extra_apply_wait=prev.get("ob_store"))

    @blk.vector
    def _(v):
        p.wait(v, ta, only=("act",))
        p.inc(v.tensor_add(p.tt("ob")[:], p.tt("ob")[:], sx[:]), "dve")
    tok_x2 = p.snap()
    prev["slot_free"] = {"dve": tok_x2["dve"]}

    @blk.gpsimd
    def _(gp):
        p.wait(gp, tok_x2, only=("dve",))
        dma(p, gp, "qg",
            p.d["x2"].ap()[r0:r1, :].rearrange("(s q) d -> q s d", q=128), p.tt("ob")[:])
    prev["ob_store"] = {"qg": p.val("qg")}
    prev["xgrp"] = {"dve": p.val("dve"), "act": p.val("act")}


def emit_ab(p, es, wtok):
    alloc_ab(p, es)
    blk = p.blk
    load_tok = {}
    prevs = [{"load_tok": load_tok, "wtok": wtok},
             {"load_tok": load_tok, "wtok": wtok}]
    order = []
    for i in range(GPB):
        order.append(i)
        order.append(GPB + i)
    order = order[:min(NGRP, MAXG_AB)]

    @blk.sync
    def _(sync):
        r0, r1 = grows(order[0])
        dma(p, sync, "qs", p.t["sx0"][:],
            p.d["x"].ap()[r0:r1, :].rearrange("(s q) d -> q s d", q=128))
        load_tok[order[0]] = {"qs": p.val("qs")}

    for ei, g in enumerate(order):
        p.sfx = "" if ei % 2 == 0 else "B"
        nxt = (order[ei + 1], prevs[(ei + 1) % 2]) if ei + 1 < len(order) else None
        emit_ab_group(p, g, prevs[ei % 2], nxt=nxt)
    p.sfx = ""
    return p.snap()





def alloc_c(p, es):
    p.sb("sx0", [128, NSUB, 1024], F32, es=es)
    p.sb("tmw", [128, NSUB, 1024], F32, es=es)
    p.sb("ob", [128, NSUB, 1024], F32, es=es)
    p.sb("xnb", [128, NCH, GRP], BF16, es=es)
    p.sb("xnf", [128, NCH, GRP], F32, es=es)
    p.sb("fmb1", [128, NCH, GRP], BF16, es=es)
    p.sb("fmb2", [128, NCH, GRP], BF16, es=es)
    p.sb("sgf", [128, GRP], F32, es=es)
    p.sb("tmb1", [128, NSUB, 1024], BF16, es=es)
    p.sb("tmb2", [128, NSUB, 1024], BF16, es=es)
    p.sb("dd", [16, GRP], F32, es=es)
    p.sb("lnp", [16, GRP], F32, es=es)
    p.sb("t1d", [16, GRP], F32, es=es)
    p.sb("p_fm", [16, GRP], F32, es=es)
    p.sb("eq_fm", [16, GRP], F32, es=es)
    p.sb("ek_fm", [16, GRP], F32, es=es)
    p.sb("etf", [16, 8], F32, es=es)
    p.sb("eqtm", [128, 16], F32, es=es)
    p.sb("ektm", [128, 16], F32, es=es)
    p.sb("etrow", [1, 16], F32, es=es)
    p.sb("etr", [128, 8], F32, es=es)
    p.sb("prow", [1, 16, 128], F32, es=es)
    p.sb("nprow", [1, 16, 128], F32, es=es)
    p.sb("dt", [128, NH * 128], F32, es=es)
    p.sb("stb", [128, NH * 128], BF16, es=es)
    p.sb("o3f", [128, 8, 128], BF16, es=es)
    p.sb("kgt", [128, 8, 128], BF16, es=es)
    p.sb("sstate", [128, 512], F32, es=es)
    p.sb("ssb", [128, 512], BF16, es=es)


def emit_l1_weights(p, es):
    for nm in ["WqT_b", "WkT_b", "WvT_b", "WgT_b", "WogT_b", "WoT_b"]:
        p.sb(nm, [128, NCH, 1024], BF16, es=es)
    p.sb("WdT_f", [128, NCH, 16], F32, es=es)
    blk, t = p.blk, p.t

    @blk.gpsimd
    def _(g):
        for nm in ["Wq", "Wk", "Wv", "Wg", "Wog", "Wo"]:
            dma(p, g, "qg", t[nm + "T_b"][:],
                p.d[nm + "T"].ap().rearrange("(c q) o -> q c o", q=128))
        dma(p, g, "qg", t["WdT_f"][:],
            p.d["WdT"].ap().rearrange("(c q) o -> q c o", q=128))
    tok_w = p.snap()

    @blk.vector
    def _(v):
        p.wait(v, tok_w, only=("qg",))
        for kc in range(NCH):
            ii = v.tensor_scalar_mul(t["WoT_b"][:, kc, :], t["WoT_b"][:, kc, :],
                                     t["normw_t"][:, kc:kc + 1])
        p.inc(ii, "dve")
    return p.snap()


def emit_c_group(p, g, prev):
    blk, t = p.blk, p.t
    r0, r1 = grows(g)
    sx = t["sx0"]
    is_seq_start = (g % GPB == 0)
    o_s = t["xnf"][:].rearrange("q a b -> q (a b)")[:, 0:NH * DH]
    o3b = t["stb"][:, 0:NH * DH]
    vekb = t["stb"][:, NH * DH:2 * NH * DH]
    stmp = t["sq_scr"][:, 1, 0:512]

    if g > 0:
        pass

    @blk.sync
    def _(sync):
        p.wait(sync, prev.get("slot_free"))
        dma(p, sync, "qs", sx[:],
            p.d["x2"].ap()[r0:r1, :].rearrange("(s q) d -> q s d", q=128))
        prev["load_tok"][g] = {"qs": p.val("qs")}

    tok_x = dict(prev["load_tok"][g])
    tok_x.update(prev.get("xgrp", {}))

    STAGE = C_STAGE
    ta_apply = emit_ln(p, sx, t["tmw"], wait0=tok_x)
    tok_xnb = emit_tr_fm(p, t["tmw"], t["xnb"], ta_apply,
                         evac_eng=TR_EVAC,
                         dual_f32=t["xnf"] if TR_DUAL else None)
    if STAGE < 1:
        prev["xgrp"] = {"dve": p.val("dve"), "act": p.val("act"), "qg": p.val("qg")}
        prev["slot_free"] = {"dve": p.val("dve")}
        return

    for hc in range(NCH):
        bn = f"pb{hc % 2}"
        bank = t[bn]

        @blk.tensor
        def _(te, hc=hc, bank=bank, bn=bn):
            p.wait(te, tok_xnb, only=("dve",))
            p.wait(te, prev.get("wtok"))
            p.wait(te, p.bank_tok.get(bn))
            p.mmdt(te, "b")
            for kc in range(NCH):
                ii = te.matmul(bank[:, 0:GRP],
                               t["WqT_b"][:, kc, hc * 128:(hc + 1) * 128],
                               t["xnb"][:, kc, :], start=(kc == 0), stop=(kc == NCH - 1))
            p.inc(ii, "pe")
        tpe = p.snap()

        @blk.vector
        def _(v, hc=hc, bank=bank, bn=bn):
            p.wait(v, tpe, only=("pe",))
            p.inc(v.tensor_copy(t["fmb1"][:, hc, :], bank[:, 0:GRP]), "dve")
        p.bank_tok[bn] = p.snap()

    for hc in range(NCH):
        bnk, bng = "pb2", "pb3"

        @blk.tensor
        def _(te, hc=hc):
            p.wait(te, tok_xnb, only=("dve",))
            p.wait(te, p.bank_tok.get(bnk))
            p.wait(te, p.bank_tok.get(bng))
            p.mmdt(te, "b")
            for kc in range(NCH):
                te.matmul(t[bnk][:, 0:GRP], t["WkT_b"][:, kc, hc * 128:(hc + 1) * 128],
                          t["xnb"][:, kc, :], start=(kc == 0), stop=(kc == NCH - 1))
            for kc in range(NCH):
                ii = te.matmul(t[bng][:, 0:GRP],
                               t["WgT_b"][:, kc, hc * 128:(hc + 1) * 128],
                               t["xnb"][:, kc, :], start=(kc == 0), stop=(kc == NCH - 1))
            p.inc(ii, "pe")
        tpe = p.snap()

        @blk.scalar
        def _(a, hc=hc):
            p.wait(a, tpe, only=("pe",))
            p.wait(a, prev.get("sg_free"))
            p.inc(a.activation(t["sgf"][:], t[bng][:, 0:GRP], AF.Sigmoid), "act")
        ta = p.snap()

        @blk.vector
        def _(v, hc=hc):
            p.wait(v, ta, only=("act",))
            p.inc(v.tensor_mul(t["fmb2"][:, hc, :], t[bnk][:, 0:GRP],
                               t["sgf"][:]), "dve")
        p.bank_tok[bnk] = p.snap()
        p.bank_tok[bng] = p.snap()
        prev["sg_free"] = {"dve": p.val("dve")}

    for dst, wname, sig in [("tmb1", "WvT_b", False), ("tmb2", "WogT_b", True)]:
        for c in range(NSUB):
            for half in range(2):
                i = c * 2 + half
                bn = f"pb{4 + i % 2}"
                bank = t[bn]

                @blk.tensor
                def _(te, c=c, half=half, bank=bank, bn=bn, wname=wname):
                    p.wait(te, tok_xnb, only=("dve",))
                    p.wait(te, p.bank_tok.get(bn))
                    p.mmdt(te, "b")
                    for kc in range(NCH):
                        ii = te.matmul(bank[:],
                                       t["xnb"][:, kc, c * 128:(c + 1) * 128],
                                       t[wname][:, kc, half * 512:(half + 1) * 512],
                                       start=(kc == 0), stop=(kc == NCH - 1))
                    p.inc(ii, "pe")
                tpe = p.snap()

                if sig:
                    @blk.scalar
                    def _(a, c=c, half=half, bank=bank, dst=dst):
                        p.wait(a, tpe, only=("pe",))
                        p.inc(a.activation(t[dst][:, c, half * 512:(half + 1) * 512],
                                           bank[:], AF.Sigmoid), "act")
                else:
                    @blk.vector
                    def _(v, c=c, half=half, bank=bank, dst=dst):
                        p.wait(v, tpe, only=("pe",))
                        p.inc(v.tensor_copy(t[dst][:, c, half * 512:(half + 1) * 512],
                                            bank[:]), "dve")
                p.bank_tok[bn] = p.snap()
    tok_proj = p.snap()

    if STAGE < 2:
        prev["xgrp"] = {"dve": p.val("dve"), "act": p.val("act"), "qg": p.val("qg")}
        prev["slot_free"] = {"dve": p.val("dve")}
        return

    @blk.tensor
    def _(te):
        p.wait(te, tok_xnb)
        p.wait(te, p.bank_tok.get("pb6"))
        p.mmdt(te, "f")
        for kc in range(NCH):
            ii = te.matmul(t["pb6"][0:16, 0:GRP], t["WdT_f"][:, kc, :],
                           t["xnf"][:, kc, :], start=(kc == 0), stop=(kc == NCH - 1))
        p.inc(ii, "pe")
    tpe = p.snap()

    @blk.scalar
    def _(a):
        p.wait(a, tpe, only=("pe",))
        p.inc(a.activation(t["dd"][:], t["pb6"][0:16, 0:GRP], AF.Identity,
                           bias=t["bd_c"][:]), "act")
        a.wait_ge(p.S["act"], p.C["act"])
        p.inc(a.activation(t["lnp"][:], t["dd"][:], AF.Abs), "act")
        a.wait_ge(p.S["act"], p.C["act"])
        p.inc(a.activation(t["lnp"][:], t["lnp"][:], AF.Exp, scale=-1.0), "act")
        a.wait_ge(p.S["act"], p.C["act"])
        p.inc(a.activation(t["lnp"][:], t["lnp"][:], AF.Ln,
                           bias=t["fconst"][0:16, 1:2]), "act")
    ta_d = p.snap()
    p.bank_tok["pb6"] = ta_d

    @blk.vector
    def _(v):
        p.wait(v, ta_d, only=("act",))
        p.inc(v.tensor_scalar_max(t["t1d"][:], t["dd"][:], 0.0), "dve")
        v.wait_ge(p.S["dve"], p.C["dve"])
        p.inc(v.scalar_tensor_tensor(t["dd"][:], t["t1d"][:], -1.0, t["lnp"][:],
                                     ALU.mult, ALU.subtract), "dve")
        v.wait_ge(p.S["dve"], p.C["dve"])
        for c in range(NCHK):
            ii = v.tensor_tensor_scan(t["p_fm"][:, c * 128:(c + 1) * 128],
                                      t["ones16"][:], t["dd"][:, c * 128:(c + 1) * 128],
                                      0.0, ALU.mult, ALU.add)
        p.inc(ii, "dve")
        v.wait_ge(p.S["dve"], p.C["dve"])
        for c in range(NCHK):
            v.tensor_copy(t["etf"][:, c:c + 1], t["p_fm"][:, (c + 1) * 128 - 1:(c + 1) * 128])
        p.inc(v.tensor_scalar_mul(t["lnp"][:], t["p_fm"][:], -1.0), "dve")
        v.wait_ge(p.S["dve"], p.C["dve"])
        for c in range(NCHK):
            ii = v.tensor_scalar(t["t1d"][:, c * 128:(c + 1) * 128],
                                 t["p_fm"][:, c * 128:(c + 1) * 128],
                                 t["etf"][:, c:c + 1], None, ALU.subtract)
        p.inc(ii, "dve")
    tv_d = p.snap()

    @blk.scalar
    def _(a):
        p.wait(a, tv_d, only=("dve",))
        a.activation(t["ek_fm"][:], t["t1d"][:], AF.Exp, scale=-1.0)
        a.activation(t["eq_fm"][:], t["p_fm"][:], AF.Exp)
        p.inc(a.activation(t["etf"][:, 2:2 + NCHK], t["etf"][:, 0:NCHK], AF.Exp), "act")
    ta_e = p.snap()

    @blk.gpsimd
    def _(gp):
        p.wait(gp, tv_d, only=("dve",))
        p.wait(gp, prev.get("bounce_free"))
        dma(p, gp, "qg", p.d["bounce"].ap()[:, 0:GRP], t["p_fm"][:])
        dma(p, gp, "qg", p.d["bounce"].ap()[:, GRP:2 * GRP], t["lnp"][:])
    tk = p.snap()

    prev["bounce_free"] = tk

    if is_seq_start:
        @blk.vector
        def _(v):
            p.wait(v, prev.get("state_users"))
            v.memset(t["sstate"][:], 0.0)
            p.inc(v.memset(t["ssb"][:], 0.0), "dve")
    tok_sinit = p.snap()

    if STAGE < 3:
        prev["xgrp"] = {"dve": p.val("dve"), "act": p.val("act"), "qg": p.val("qg")}
        prev["slot_free"] = {"dve": p.val("dve")}
        return

    for c in range(NCHK):
        @blk.gpsimd
        def _(gp, c=c):
            p.wait(gp, tk, only=("qg",))
            p.wait(gp, prev.get("prow_free"))
            bap = p.d["bounce"].ap()
            dma(p, gp, "qg", t["prow"][:],
                bap[:, c * 128:(c + 1) * 128].unsqueeze(0))
            dma(p, gp, "qg", t["nprow"][:],
                bap[:, GRP + c * 128:GRP + (c + 1) * 128].unsqueeze(0))
        tok_prow = p.snap()

        @blk.tensor
        def _(te, c=c):
            p.wait(te, ta_e, only=("act",))
            p.wait(te, p.bank_tok.get("pb6"))
            p.wait(te, p.bank_tok.get("pb4"))
            p.wait(te, p.bank_tok.get("pb5"))
            p.mmdt(te, "f")
            te.transpose(t["pb6"][:, 0:16], t["eq_fm"][:, c * 128:(c + 1) * 128],
                         t["ident"][0:16, 0:16])
            te.transpose(t["pb4"][:, 0:16], t["ek_fm"][:, c * 128:(c + 1) * 128],
                         t["ident"][0:16, 0:16])
            p.inc(te.transpose(t["pb5"][0:1, 0:16], t["etf"][:, 2 + c:3 + c],
                               t["ident"][0:16, 0:16]), "pe")
        tpe = p.snap()

        @blk.vector
        def _(v, c=c):
            p.wait(v, tpe, only=("pe",))
            v.tensor_copy(t["eqtm"][:], t["pb6"][:, 0:16])
            v.tensor_copy(t["ektm"][:], t["pb4"][:, 0:16])
            p.inc(v.tensor_copy(t["etrow"][:], t["pb5"][0:1, 0:16]), "dve")
        tv_t = p.snap()
        p.bank_tok["pb4"] = tv_t
        p.bank_tok["pb5"] = tv_t

        @blk.tensor
        def _(te, c=c):
            p.wait(te, tv_t, only=("dve",))
            p.mmdt(te, "f")
            erow2 = t["etrow"][:].rearrange("o (h two) -> o two h", two=2)
            te.matmul(t["pb6"][:, 48:56], t["mtop"][:], erow2[0:1, 0, :],
                      start=True, stop=False)
            p.inc(te.matmul(t["pb6"][:, 48:56], t["mbot"][:], erow2[0:1, 1, :],
                            start=False, stop=True, skip_group_check=True), "pe")
        tpe = p.snap()

        @blk.vector
        def _(v, c=c):
            p.wait(v, tpe, only=("pe",))
            p.inc(v.tensor_copy(t["etr"][:], t["pb6"][:, 48:56]), "dve")
        p.bank_tok["pb6"] = p.snap()
        tok_etr = p.snap()

        @blk.tensor
        def _(te, c=c):
            p.wait(te, tok_prow, only=("qg",))
            p.mmdt(te, "f")
            for q in range(4):
                p.wait(te, p.bank_tok.get(f"pb{q}"))
            for q in range(4):
                te.matmul(t[f"pb{q}"][:], t["ones_row"][:],
                          t["prow"][0:1, 4 * q:4 * q + 4, :]
                          .rearrange("o a b -> o (a b)"),
                          start=True, stop=False)
            for h in range(NH):
                ii = te.matmul(t[f"pb{h // 4}"][:, (h % 4) * 128:(h % 4 + 1) * 128],
                               t["nprow"][0:1, h, :], t["ones_row"][:],
                               start=False, stop=True, skip_group_check=True)
            p.inc(ii, "pe")
        tok_B = p.snap()
        prev["prow_free"] = {"pe": tok_B["pe"]}

        @blk.vector
        def _(v, c=c):
            p.wait(v, tok_B, only=("pe",))
            lt = t["ltneg"][:].unsqueeze(1).broadcast_to([128, 4, 128])
            for q in range(4):
                ii = v.scalar_tensor_tensor(
                    t["dt"][:, q * 512:(q + 1) * 512]
                    .rearrange("j (a i) -> j a i", i=128),
                    t[f"pb{q}"][:].rearrange("j (a i) -> j a i", i=128),
                    0.0, lt, ALU.min, ALU.add)
            p.inc(ii, "dve")
        tok_dt1 = p.snap()

        @blk.scalar
        def _(a, c=c):
            p.wait(a, tok_dt1, only=("dve",))
            p.inc(a.activation(t["dt"][:], t["dt"][:], AF.Exp), "act")
        tok_dt = p.snap()

        if STAGE < 4:
            continue
        @blk.tensor
        def _(te, c=c):
            p.wait(te, tok_dt1, only=("dve",))
            p.wait(te, tok_proj)
            p.mmdt(te, "b")
            for r in range(4):
                if r > 0:
                    te.wait_ge(p.S["pe"], p.C["pe"])
                for q in range(4):
                    h = q * 4 + r
                    base = (h % 2) * 64
                    ii = te.matmul(
                        t[f"pb{q}"][:, r * 128:(r + 1) * 128],
                        t["fmb2"][base:base + 64, h // 2, c * 128:(c + 1) * 128],
                        t["fmb1"][base:base + 64, h // 2, c * 128:(c + 1) * 128],
                        start=True, stop=True)
                p.inc(ii, "pe")
        tok_sc = p.snap()

        if STAGE < 4.6:
            for q in range(4):
                p.bank_tok[f"pb{q}"] = tok_sc
            continue

        @blk.vector
        def _(v, c=c):
            p.wait(v, tok_sc, only=("pe",))
            p.wait(v, tok_dt, only=("act",))
            for q in range(4):
                st3 = t["stb"][:].rearrange("j (q2 r i) -> j q2 r i", q2=4, i=128)
                dt3 = t["dt"][:].rearrange("j (q2 r i) -> j q2 r i", q2=4, i=128)
                ii = v.tensor_mul(st3[:, q, :, :], t[f"pb{q}"][:]
                                  .rearrange("j (r i) -> j r i", i=128),
                                  dt3[:, q, :, :])
            p.inc(ii, "dve")
        tok_st = p.snap()
        for q in range(4):
            p.bank_tok[f"pb{q}"] = tok_st

        if STAGE < 5:
            continue
        @blk.tensor
        def _(te, c=c):
            p.wait(te, tok_st, only=("dve",))
            p.wait(te, tok_sinit, only=("dve",))
            p.wait(te, prev.get("state_ready"))
            p.mmdt(te, "b")
            for r in range(8):
                if r > 0:
                    te.wait_ge(p.S["pe"], p.C["pe"])
                for hb in range(2):
                    h = hb * 8 + r
                    base = (h % 2) * 64
                    te.matmul(t[f"pb{hb}"][:, r * 64:(r + 1) * 64],
                              t["stb"][:, h * 128:(h + 1) * 128],
                              t["tmb1"][:, c, h * 64:(h + 1) * 64],
                              start=True, stop=True)
                    ii = te.matmul(t[f"pb{2 + hb}"][:, r * 64:(r + 1) * 64],
                                   t["fmb1"][base:base + 64, h // 2,
                                             c * 128:(c + 1) * 128],
                                   t["ssb"][base:base + 64,
                                            (h // 2) * 64:(h // 2 + 1) * 64],
                                   start=True, stop=True)
                p.inc(ii, "pe")
        tok_omm = p.snap()

        @blk.vector
        def _(v, c=c):
            p.wait(v, tok_omm, only=("pe",))
            for q in range(2):
                eqv = t["eqtm"][:, q * 8:(q + 1) * 8].unsqueeze(2)                     .broadcast_to([128, 8, 64])
                v.tensor_tensor(o_s[:, q * 512:(q + 1) * 512].rearrange("i (a d) -> i a d", d=64),
                                t[f"pb{2 + q}"][:].rearrange("i (a d) -> i a d", d=64),
                                eqv, ALU.mult)
            p.inc(v.engine_nop(), "dve")
            v.wait_ge(p.S["dve"], p.C["dve"])
            for q in range(2):
                ii = v.tensor_add(o_s[:, q * 512:(q + 1) * 512],
                                  o_s[:, q * 512:(q + 1) * 512], t[f"pb{q}"][:])
            p.inc(ii, "dve")
        tok_o = p.snap()
        for q in range(4):
            p.bank_tok[f"pb{q}"] = tok_o

        if STAGE < 6:
            continue
        @blk.scalar
        def _(a, c=c):
            p.wait(a, tok_o, only=("dve",))
            p.inc(a.activation(t["sq_scr"][:, 0, 0:NH * DH], o_s, AF.Square,
                               accum_out=t["tiny"][:, 12:13]), "act")
        ta_r = p.snap()

        @blk.vector
        def _(v, c=c):
            p.wait(v, ta_r, only=("act",))
            p.inc(v.tensor_scalar_mul(t["tiny"][:, 13:14], t["tiny"][:, 12:13],
                                      1.0 / float(NH * DH)), "dve")
        tv_r = p.snap()

        @blk.scalar
        def _(a, c=c):
            p.wait(a, tv_r, only=("dve",))
            p.inc(a.activation(t["tiny"][:, 14:15], t["tiny"][:, 13:14], AF.Sqrt,
                               bias=t["fconst"][:, 5:6]), "act")
        ta_r2 = p.snap()

        @blk.vector
        def _(v, c=c):
            p.wait(v, ta_r2, only=("act",))
            p.inc(v.reciprocal(t["tiny"][:, 15:16], t["tiny"][:, 14:15]), "dve")
            v.wait_ge(p.S["dve"], p.C["dve"])
            p.inc(v.tensor_scalar(o_s, o_s, t["tiny"][:, 15:16],
                                  None, ALU.mult), "dve")
            v.wait_ge(p.S["dve"], p.C["dve"])
            p.inc(v.tensor_mul(o3b, o_s, t["tmb2"][:, c, :]), "dve")
        tok_o3 = p.snap()

        @blk.tensor
        def _(te, c=c):
            p.wait(te, tok_o3, only=("dve",))
            p.wait(te, p.bank_tok.get("pbt"))
            p.mmdt(te, "b")
            for j in range(8):
                if j > 0:
                    te.wait_ge(p.S["pe"], p.C["pe"])
                p.inc(te.transpose(t["pbt"][:, j * 128:(j + 1) * 128],
                                   o3b[:, j * 128:(j + 1) * 128], t["identb"][:]),
                      "pe")
        tpe = p.snap()

        @blk.vector
        def _(v, c=c):
            p.wait(v, tpe, only=("pe",))
            p.inc(v.tensor_copy(t["o3f"][:], t["pbt"][:]
                                .rearrange("q (j c2) -> q j c2", c2=128)), "dve")
        p.bank_tok["pbt"] = p.snap()
        tok_o3f = p.snap()

        for dh in range(2):
            bn = f"pb{dh % 2}"
            bank = t[bn]

            @blk.tensor
            def _(te, c=c, dh=dh, bank=bank, bn=bn):
                p.wait(te, tok_o3f, only=("dve",))
                p.wait(te, p.bank_tok.get(bn))
                p.mmdt(te, "b")
                for kc in range(NCH):
                    ii = te.matmul(bank[:], t["o3f"][:, kc, :],
                                   t["WoT_b"][:, kc, dh * 512:(dh + 1) * 512],
                                   start=(kc == 0), stop=(kc == NCH - 1))
                p.inc(ii, "pe")
            tpe = p.snap()

            @blk.scalar
            def _(a, c=c, dh=dh, bank=bank, bn=bn):
                p.wait(a, tpe, only=("pe",))
                p.inc(a.copy(t["tmw"][:, c, dh * 512:(dh + 1) * 512], bank[:]), "act")
            p.bank_tok[bn] = p.snap()

        if STAGE < 7:
            continue
        @blk.vector
        def _(v, c=c):
            p.wait(v, tok_etr, only=("dve",))
            p.wait(v, tok_omm, only=("pe",))
            ekv = t["ektm"][:].unsqueeze(2).broadcast_to([128, 16, 64])
            p.inc(v.tensor_tensor(vekb.rearrange("j (h d) -> j h d", d=64),
                                  t["tmb1"][:, c, :].rearrange("j (h d) -> j h d", d=64),
                                  ekv, ALU.mult), "dve")
        tok_vek = p.snap()

        @blk.tensor
        def _(te, c=c):
            p.wait(te, tok_o3f, only=("dve",))
            p.wait(te, p.bank_tok.get("pbt"))
            p.mmdt(te, "b")
            for hc in range(8):
                if hc > 0:
                    te.wait_ge(p.S["pe"], p.C["pe"])
                p.inc(te.transpose(t["pbt"][:, hc * 128:(hc + 1) * 128],
                                   t["fmb2"][:, hc, c * 128:(c + 1) * 128],
                                   t["identb"][:]), "pe")
        tpe = p.snap()

        @blk.vector
        def _(v, c=c):
            p.wait(v, tpe, only=("pe",))
            p.inc(v.tensor_copy(t["kgt"][:], t["pbt"][:]
                                .rearrange("q (j c2) -> q j c2", c2=128)), "dve")
        p.bank_tok["pbt"] = p.snap()
        tok_kgt = p.snap()

        @blk.tensor
        def _(te, c=c):
            p.wait(te, tok_kgt, only=("dve",))
            p.wait(te, tok_vek, only=("dve",))
            p.wait(te, tok_o, only=("dve",))
            p.wait(te, p.bank_tok.get("pb4"))
            p.wait(te, p.bank_tok.get("pb5"))
            p.mmdt(te, "b")
            for r in range(4):
                if r == 2:
                    te.wait_ge(p.S["pe"], p.C["pe"])
                for two in range(2):
                    hc = r * 2 + two
                    bank = t[f"pb{2 + hc % 4}"]
                    col = (hc // 4) * 64
                    for par in range(2):
                        h = hc * 2 + par
                        base = par * 64
                        ii = te.matmul(bank[base:base + 64, col:col + 64],
                                       t["kgt"][:, hc, base:base + 64],
                                       vekb[:, h * 64:(h + 1) * 64],
                                       start=True, stop=True)
                p.inc(ii, "pe")
        tok_kv = p.snap()

        @blk.vector
        def _(v, c=c):
            p.wait(v, tok_kv, only=("pe",))
            etrv = t["etr"][:].unsqueeze(2).broadcast_to([128, 8, 64])
            p.inc(v.tensor_tensor(stmp.rearrange("q (h d) -> q h d", d=64),
                                  t["sstate"][:].rearrange("q (h d) -> q h d", d=64),
                                  etrv, ALU.mult), "dve")
            v.wait_ge(p.S["dve"], p.C["dve"])
            sv = t["sstate"][:].rearrange("q (h4 four d) -> q h4 four d", four=4, d=64)
            sm = stmp.rearrange("q (h4 four d) -> q h4 four d", four=4, d=64)
            for b in range(4):
                ii = v.tensor_add(sv[:, :, b, :], sm[:, :, b, :],
                                  t[f"pb{2 + b}"][:, 0:128]
                                  .rearrange("q (h4 d) -> q h4 d", d=64))
            p.inc(ii, "dve")
            v.wait_ge(p.S["dve"], p.C["dve"])
            p.inc(v.tensor_copy(t["ssb"][:], t["sstate"][:]), "dve")
        tok_state = p.snap()
        p.bank_tok["pb4"] = tok_state
        p.bank_tok["pb5"] = tok_state
        p.bank_tok["pb2"] = tok_state
        p.bank_tok["pb3"] = tok_state
        prev["state_ready"] = {"dve": tok_state["dve"]}
        tok_sinit = {"dve": 0}

    if STAGE < 8:
        prev["xgrp"] = {"dve": p.val("dve"), "act": p.val("act"), "qg": p.val("qg")}
        prev["slot_free"] = {"dve": p.val("dve")}
        return
    ta = emit_ln(p, t["tmw"], t["ob"], wait0={"act": p.val("act"), "dve": p.val("dve")},
                 extra_apply_wait=prev.get("ob_store"))

    @blk.vector
    def _(v):
        p.wait(v, ta, only=("act",))
        p.inc(v.tensor_add(t["ob"][:], t["ob"][:], sx[:]), "dve")
    tok_x3 = p.snap()
    prev["slot_free"] = {"dve": tok_x3["dve"]}
    prev["state_users"] = {"pe": p.val("pe")}

    @blk.gpsimd
    def _(gp):
        p.wait(gp, tok_x3, only=("dve",))
        dma(p, gp, "qg",
            p.d["x3"].ap()[r0:r1, :].rearrange("(s q) d -> q s d", q=128), t["ob"][:])
    prev["ob_store"] = {"qg": p.val("qg")}

    ta2 = emit_ln(p, t["ob"], t["tmw"], wait0={"dve": tok_x3["dve"]})

    @blk.gpsimd
    def _(gp):
        p.wait(gp, ta2, only=("act",))
        dma(p, gp, "qg",
            p.d["xn3"].ap()[r0:r1, :].rearrange("(s q) d -> q s d", q=128), t["tmw"][:])
    prev["ob_store"]["qg"] = p.val("qg")

    @blk.vector
    def _(v):
        p.wait(v, ta2, only=("act",))
        p.inc(v.reduce_max(t["tiny"][:, 11:12],
                           t["tmw"][:].rearrange("q s d -> q (s d)"), axis=AX.X,
                           apply_absolute_value=True), "dve")
        v.wait_ge(p.S["dve"], p.C["dve"])
        p.inc(v.tensor_max(t["gmax"][:, 0:1], t["gmax"][:, 0:1], t["tiny"][:, 11:12]),
              "dve")
    prev["xgrp"] = {"dve": p.val("dve"), "act": p.val("act"), "qg": p.val("qg")}


def emit_c(p, es, wtok):
    alloc_c(p, es)
    blk = p.blk
    prev = {"load_tok": {}, "wtok": wtok}

    @blk.sync
    def _(sync):
        r0, r1 = grows(0)
        dma(p, sync, "qs", p.t["sx0"][:],
            p.d["x2"].ap()[r0:r1, :].rearrange("(s q) d -> q s d", q=128))
        prev["load_tok"][0] = {"qs": p.val("qs")}

    maxg = MAXG_C
    for g in range(min(NGRP, maxg)):
        emit_c_group(p, g, prev)
    return p.snap()




GRPD = 512
NGRPD = NTOK // GRPD
NSUBD = GRPD // 128


def emit_dequant(p, es):
    blk, t = p.blk, p.t
    p.sb("dq_w", [128, 3072], F32, es=es)
    p.sb("dq_s", [128, 96], F32, es=es)
    p.sb("dq_is", [128, 96], F32, es=es)
    p.sb("dq_t", [128, 3072], F32, es=es)
    p.sb("dq_o", [128, 3072], BF16, es=es)
    jobs = []
    for nm, rows, cols in [("w_up", 3072, 1024), ("w_gate", 3072, 1024),
                           ("w_down", 1024, 3072)]:
        for r0 in range(0, rows, 128):
            jobs.append((nm, r0, cols))
    tprev = None
    for nm, r0, cols in jobs:
        nb = cols // 32

        @blk.sync
        def _(sync, nm=nm, r0=r0, cols=cols):
            p.wait(sync, tprev, only=("qg",))
            dma(p, sync, "qs", t["dq_w"][:, 0:cols], p.d[nm].ap()[r0:r0 + 128, :])
        tok_l = {"qs": p.val("qs")}

        @blk.vector
        def _(v, nm=nm, r0=r0, cols=cols, nb=nb):
            p.wait(v, tok_l)
            w3 = t["dq_w"][:, 0:cols].rearrange("q (n k) -> q n k", k=32)
            p.inc(v.reduce_max(t["dq_s"][:, 0:nb], w3, axis=AX.X,
                               apply_absolute_value=True), "dve")
            v.wait_ge(p.S["dve"], p.C["dve"])
            p.inc(v.tensor_scalar(t["dq_s"][:, 0:nb], t["dq_s"][:, 0:nb],
                                  1.0 / 7.0, 1e-12, ALU.mult, ALU.max), "dve")
            v.wait_ge(p.S["dve"], p.C["dve"])
            p.inc(v.reciprocal(t["dq_is"][:, 0:nb], t["dq_s"][:, 0:nb]), "dve")
            v.wait_ge(p.S["dve"], p.C["dve"])
            isv = t["dq_is"][:, 0:nb].unsqueeze(2).broadcast_to([128, nb, 32])
            p.inc(v.tensor_tensor(t["dq_t"][:, 0:cols].rearrange(
                "q (n k) -> q n k", k=32), w3, isv, ALU.mult), "dve")
        tok_v = p.snap()

        @blk.scalar
        def _(a, cols=cols):
            p.wait(a, tok_v, only=("dve",))
            p.inc(a.activation(t["dq_t"][:, 0:cols], t["dq_t"][:, 0:cols],
                               AF.Identity, bias=t["fconst"][:, 2:3]), "act")
            a.wait_ge(p.S["act"], p.C["act"])
            p.inc(a.activation(t["dq_t"][:, 0:cols], t["dq_t"][:, 0:cols],
                               AF.Identity, bias=t["fconst"][:, 3:4]), "act")
        tok_a = p.snap()

        @blk.vector
        def _(v, cols=cols, nb=nb):
            p.wait(v, tok_a, only=("act",))
            sv = t["dq_s"][:, 0:nb].unsqueeze(2).broadcast_to([128, nb, 32])
            p.inc(v.tensor_tensor(t["dq_o"][:, 0:cols].rearrange(
                "q (n k) -> q n k", k=32),
                t["dq_t"][:, 0:cols].rearrange("q (n k) -> q n k", k=32),
                sv, ALU.mult), "dve")
        tok_v2 = p.snap()

        @blk.gpsimd
        def _(gp, nm=nm, r0=r0, cols=cols):
            p.wait(gp, tok_v2, only=("dve",))
            dma(p, gp, "qg", p.d["wq_" + nm].ap()[r0:r0 + 128, :],
                t["dq_o"][:, 0:cols])
        tprev = {"qg": p.val("qg")}
    return p.snap()


def emit_cc(p, j, n_vals, gcol0, prev):
    blk, t = p.blk, p.t

    @blk.tensor
    def _(te):
        p.wait(te, prev["gmax_tok"], only=("dve",))
        p.wait(te, p.bank_tok.get("pb6"))
        p.mmdt(te, "f")
        p.inc(te.transpose(t["pb6"][0:n_vals, 0:128],
                           t["gmax"][:, gcol0:gcol0 + n_vals],
                           t["ident"][:]), "pe")
    tpe = p.snap()

    @blk.vector
    def _(v):
        p.wait(v, tpe, only=("pe",))
        p.inc(v.tensor_copy(t["glob2"][0:n_vals, 0:128], t["pb6"][0:n_vals, 0:128]),
              "dve")
        v.wait_ge(p.S["dve"], p.C["dve"])
        p.inc(v.reduce_max(t["glob"][0:n_vals, 0:1], t["glob2"][0:n_vals, 0:128],
                           axis=AX.X), "dve")
    p.bank_tok["pb6"] = p.snap()
    tv = p.snap()

    @blk.gpsimd
    def _(gp):
        p.wait(gp, tv, only=("dve",))
        dma(p, gp, "qg", p.d[f"ccin{j}"].ap(), t["glob"][:])
        gp.wait_ge(p.S["qg"], p.C["qg"])
        if DEV_NO_CC:
            dma(p, gp, "qg", p.d[f"ccout{j}"].ap(), p.d[f"ccin{j}"].ap())
            gp.wait_ge(p.S["qg"], p.C["qg"])
        else:
            p.inc(gp.collective_compute(
                "AllReduce", ALU.max, replica_groups=[list(range(N_CORES))],
                ins=[p.d[f"ccin{j}"].ap().opt()], outs=[p.d[f"ccout{j}"].ap().opt()]),
                "cc" + str(j), 1)
            gp.wait_ge(p.sem("cc" + str(j)), 1)
        dma(p, gp, "qg", t["gres"][:], p.d[f"ccout{j}"].ap())
    tok_g = {"qg": p.val("qg")}

    @blk.tensor
    def _(te):
        p.wait(te, tok_g)
        p.wait(te, p.bank_tok.get("pb6"))
        p.mmdt(te, "f")
        p.inc(te.transpose(t["pb6"][0:1, 56:56 + 16], t["gres"][0:16, 0:1],
                           t["ident"][0:16, 0:16]), "pe")
    tpe2 = p.snap()

    @blk.vector
    def _(v):
        p.wait(v, tpe2, only=("pe",))
        p.inc(v.tensor_copy(t["gr_row"][:], t["pb6"][0:1, 56:56 + 16]), "dve")
    p.bank_tok["pb6"] = p.snap()
    return p.snap()


def bcast_col(p, te_tok, src_cell, dst_col):
    blk, t = p.blk, p.t

    @blk.tensor
    def _(te):
        p.wait(te, te_tok, only=("dve",))
        p.wait(te, p.bank_tok.get("pb6"))
        p.mmdt(te, "f")
        p.inc(te.matmul(t["pb6"][:, 60:61], t["ones_row"][:], src_cell,
                        start=True, stop=True), "pe")
    tpe = p.snap()

    @blk.vector
    def _(v):
        p.wait(v, tpe, only=("pe",))
        p.inc(v.tensor_copy(dst_col, t["pb6"][:, 60:61]), "dve")
    p.bank_tok["pb6"] = p.snap()
    return p.snap()




def emit_d(p, es, prev_c_tok, dqtok=None):
    blk, t = p.blk, p.t
    p.sb("sx0", [128, NSUBD, 1024], F32, es=es)
    p.sb("tmw", [128, NSUBD, 1024], F32, es=es)
    p.sb("ob", [128, NSUBD, 1024], F32, es=es)
    p.sb("scl", [128, 8], F32, es=es)
    p.sb("glob2", [16, 128], F32, es=es)
    p.sb("gres", [16, 128], F32, es=es)
    p.sb("gr_row", [1, 16], F32, es=es)
    sx = t["sx0"]
    prev = {"gmax_tok": prev_c_tok, "dq_done": dqtok}

    if "C" not in p.phases:
        for g in range(NGRPD):
            @blk.sync
            def _(sync, g=g):
                p.wait(sync, prev.get("pre_done"))
                dma(p, sync, "qs", sx[:],
                    p.d["xn3"].ap()[g * GRPD:(g + 1) * GRPD, :]
                    .rearrange("(s q) d -> q s d", q=128))
            tok_l = {"qs": p.val("qs")}

            @blk.vector
            def _(v, g=g):
                p.wait(v, tok_l)
                p.inc(v.reduce_max(t["stats"][:, g:g + 1],
                                   sx[:].rearrange("q s d -> q (s d)"), axis=AX.X,
                                   apply_absolute_value=True), "dve")
            prev["pre_done"] = {"dve": p.val("dve")}
        @blk.vector
        def _(v):
            v.wait_ge(p.S["dve"], p.C["dve"])
            p.inc(v.reduce_max(t["gmax"][:, 0:1], t["stats"][:, 0:NGRPD], axis=AX.X),
                  "dve")
        if "dev_dbg" in p.dev_taps:
            @blk.gpsimd
            def _(gp):
                p.wait(gp, {"dve": p.val("dve")})
                dma(p, gp, "qg", p.d["dbg_pre"].ap(), t["stats"][:])
        prev["gmax_tok"] = {"dve": p.val("dve")}

    emit_cc(p, 1, 1, 0, prev)

    @blk.vector
    def _(v):
        p.wait(v, {"dve": p.val("dve")})
        p.inc(v.tensor_scalar(t["tiny"][0:1, 0:1], t["gr_row"][0:1, 0:1],
                              1.0 / QMAX, 1e-12, ALU.mult, ALU.max), "dve")
        v.wait_ge(p.S["dve"], p.C["dve"])
        p.inc(v.reciprocal(t["tiny"][0:1, 1:2], t["tiny"][0:1, 0:1]), "dve")
    tvc = p.snap()
    bcast_col(p, tvc, t["tiny"][0:1, 1:2], t["scl"][:, 0:1])
    tok_sc1 = bcast_col(p, tvc, t["tiny"][0:1, 0:1], t["scl"][:, 1:2])

    emit_barrier(p)
    with ExitStack() as es_w:
        p.sb("wupT_b", [128, NCH, DFF], BF16, es=es_w)
        p.sb("wgT_b", [128, NCH, DFF], BF16, es=es_w)
        p.sb("stage0", [128, 12, GRPD], BF16, es=es_w)
        p.sb("stage1", [128, 12, GRPD], BF16, es=es_w)
        p.sb("xqf", [128, NCH, GRPD], F32, es=es_w)
        p.sb("xqb", [128, NCH, GRPD], BF16, es=es_w)

        @blk.sync
        def _(sync):
            p.wait(sync, prev.get("dq_done"), only=("qg",))
            for kc in range(NCH):
                p.inc(sync.dma_start(out=t["wupT_b"][:, kc, :],
                                     in_=p.d["wq_w_up"].ap()[:, kc * 128:(kc + 1) * 128],
                                     transpose=True), "qs", 16)
                p.inc(sync.dma_start(out=t["wgT_b"][:, kc, :],
                                     in_=p.d["wq_w_gate"].ap()[:, kc * 128:(kc + 1) * 128],
                                     transpose=True), "qs", 16)
        tok_w = {"qs": p.val("qs")}

        if "dev_dbg" in p.dev_taps:
            @blk.gpsimd
            def _(gp):
                p.wait(gp, tok_w)
                dma(p, gp, "qg", p.d["dbg_wupT"].ap(), t["wupT_b"][:])

        for g in range(NGRPD):
            @blk.sync
            def _(sync, g=g):
                p.wait(sync, prev.get("d1_sx_free"))
                dma(p, sync, "qs", sx[:],
                    p.d["xn3"].ap()[g * GRPD:(g + 1) * GRPD, :]
                    .rearrange("(s q) d -> q s d", q=128))
            tok_l = {"qs": p.val("qs")}

            tok_tr = emit_tr_fm(p, sx, t["xqf"], tok_l, nsub=NSUBD)
            prev["d1_sx_free"] = {"pe": p.val("pe")}

            @blk.vector
            def _(v, g=g):
                p.wait(v, tok_tr, only=("dve",))
                p.wait(v, tok_sc1, only=("dve",))
                p.inc(v.tensor_scalar(t["xqf"][:], t["xqf"][:], t["scl"][:, 0:1],
                                      t["fconst"][:, 2:3], ALU.mult, ALU.add), "dve")
                v.wait_ge(p.S["dve"], p.C["dve"])
                p.inc(v.tensor_scalar(t["xqb"][:], t["xqf"][:], t["fconst"][:, 2:3],
                                      t["scl"][:, 1:2], ALU.subtract, ALU.mult), "dve")
            tok_xq = p.snap()
            if g == 0 and "dev_dbg" in p.dev_taps:
                @blk.gpsimd
                def _(gp):
                    p.wait(gp, tok_xq, only=("dve",))
                    dma(p, gp, "qg", p.d["dbg_xq"].ap(), t["xqb"][:])
                    dma(p, gp, "qg", p.d["dbg_scl"].ap(), t["scl"][:])
                    dma(p, gp, "qg", p.d["dbg_tiny"].ap(), t["tiny"][:])
                    dma(p, gp, "qg", p.d["dbg_grow"].ap(), t["gr_row"][:])
                    dma(p, gp, "qg", p.d["dbg_gmax"].ap(), t["gmax"][:])

            for wi, (wn, spn, gcol) in enumerate(
                    [("wupT_b", "up_sp", 1), ("wgT_b", "gate_sp", 2)]):
                for half in range(2):
                    hb = (2 * wi + half) % 2
                    stg = t[f"stage{hb}"]
                    for oc12 in range(12):
                        oc = half * 12 + oc12
                        bn = f"pb{2 + oc % 2}"
                        bank = t[bn]

                        @blk.tensor
                        def _(te, oc=oc, bank=bank, bn=bn, wn=wn):
                            p.wait(te, tok_xq, only=("dve",))
                            p.wait(te, tok_w)
                            p.wait(te, p.bank_tok.get(bn))
                            p.mmdt(te, "b")
                            for kc in range(NCH):
                                ii = te.matmul(bank[:],
                                               t[wn][:, kc, oc * 128:(oc + 1) * 128],
                                               t["xqb"][:, kc, :],
                                               start=(kc == 0), stop=(kc == NCH - 1))
                            p.inc(ii, "pe")
                        tpe = p.snap()

                        @blk.scalar
                        def _(a, oc12=oc12, bank=bank, bn=bn, hb=hb, stg=stg):
                            p.wait(a, tpe, only=("pe",))
                            p.wait(a, prev.get(f"stage_free{hb}"))
                            p.inc(a.copy(stg[:, oc12, :], bank[:]), "act")
                        p.bank_tok[bn] = p.snap()
                    tok_half = p.snap()

                    @blk.vector
                    def _(v, stg=stg):
                        p.wait(v, tok_half, only=("act",))
                        p.inc(v.reduce_max(t["tiny"][:, 2:3],
                                           stg[:].rearrange("q a b -> q (a b)"),
                                           axis=AX.X, apply_absolute_value=True),
                              "dve")
                        v.wait_ge(p.S["dve"], p.C["dve"])
                        p.inc(v.tensor_max(t["gmax"][:, gcol:gcol + 1],
                                           t["gmax"][:, gcol:gcol + 1],
                                           t["tiny"][:, 2:3]), "dve")
                    tok_red = p.snap()

                    @blk.gpsimd
                    def _(gp, g=g, spn=spn, half=half, stg=stg):
                        p.wait(gp, tok_red, only=("act", "dve"))
                        dma(p, gp, "qg",
                            p.d[spn].ap()[g, half * 1536:(half + 1) * 1536, :]
                            .rearrange("(a q) b -> q a b", q=128), stg[:])
                    prev[f"stage_free{hb}"] = {"qg": p.val("qg")}
        prev["gmax_tok"] = {"dve": p.val("dve")}

    emit_cc(p, 2, 2, 1, prev)

    @blk.vector
    def _(v):
        p.wait(v, {"dve": p.val("dve")})
        p.inc(v.tensor_scalar(t["tiny"][0:1, 0:1], t["gr_row"][0:1, 0:1],
                              1.0 / QMAX, 1e-12, ALU.mult, ALU.max), "dve")
        p.inc(v.tensor_scalar(t["tiny"][0:1, 2:3], t["gr_row"][0:1, 1:2],
                              1.0 / QMAX, 1e-12, ALU.mult, ALU.max), "dve")
        v.wait_ge(p.S["dve"], p.C["dve"])
        v.reciprocal(t["tiny"][0:1, 1:2], t["tiny"][0:1, 0:1])
        p.inc(v.reciprocal(t["tiny"][0:1, 3:4], t["tiny"][0:1, 2:3]), "dve")
    tvc2 = p.snap()
    bcast_col(p, tvc2, t["tiny"][0:1, 1:2], t["scl"][:, 2:3])
    bcast_col(p, tvc2, t["tiny"][0:1, 2:3], t["scl"][:, 3:4])
    tok_sc2 = bcast_col(p, tvc2, t["tiny"][0:1, 3:4], t["scl"][:, 4:5])

    emit_barrier(p)
    with ExitStack() as es_d2:
        for b in range(2):
            p.sb(f"gsp{b}", [128, 12, GRPD], BF16, es=es_d2)
            p.sb(f"g2b{b}", [128, 12, GRPD], BF16, es=es_d2)
        p.sb("rgf", [128, 12, GRPD], F32, es=es_d2)
        p.sb("sgv", [128, 12, GRPD], F32, es=es_d2)
        for g in range(NGRPD):
            for half in range(2):
                hb = half
                gsp = t[f"gsp{hb}"]
                g2b = t[f"g2b{hb}"]

                @blk.sync
                def _(sync, g=g, half=half, hb=hb, gsp=gsp):
                    p.wait(sync, prev.get(f"d2_buf_free{hb}"))
                    dma(p, sync, "qs", gsp[:],
                        p.d["gate_sp"].ap()[g, half * 1536:(half + 1) * 1536, :]
                        .rearrange("(a q) b -> q a b", q=128))
                tok_l = {"qs": p.val("qs")}

                @blk.vector
                def _(v, half=half, hb=hb, gsp=gsp):
                    p.wait(v, tok_l)
                    p.wait(v, tok_sc2, only=("dve",))
                    n = p.inc(v.tensor_scalar(t["rgf"][:], gsp[:],
                                              t["scl"][:, 4:5],
                                              t["fconst"][:, 2:3],
                                              ALU.mult, ALU.add), "dve")
                    prev[f"d2_buf_free{hb}"] = {"dve": n}
                    v.wait_ge(p.S["dve"], p.C["dve"])
                    p.inc(v.tensor_scalar(t["rgf"][:], t["rgf"][:],
                                          t["fconst"][:, 2:3], None, ALU.subtract),
                          "dve")
                tv1 = p.snap()

                @blk.scalar
                def _(a, half=half):
                    p.wait(a, tv1, only=("dve",))
                    p.inc(a.activation(t["sgv"][:], t["rgf"][:], AF.Sigmoid,
                                       scale=t["scl"][:, 3:4]), "act")
                ta1 = p.snap()

                @blk.vector
                def _(v, half=half, g=g, hb=hb, g2b=g2b):
                    p.wait(v, ta1, only=("act",))
                    p.wait(v, prev.get(f"d2_st_free{hb}"))
                    p.inc(v.tensor_scalar(t["sgv"][:], t["sgv"][:], 65536.0,
                                          t["fconst"][:, 2:3], ALU.mult, ALU.add),
                          "dve")
                    v.wait_ge(p.S["dve"], p.C["dve"])
                    p.inc(v.tensor_scalar(t["sgv"][:], t["sgv"][:],
                                          t["fconst"][:, 2:3], 65535.0,
                                          ALU.subtract, ALU.min), "dve")
                    v.wait_ge(p.S["dve"], p.C["dve"])
                    p.inc(v.scalar_tensor_tensor(g2b[:], t["sgv"][:],
                                                 1.0 / 65536.0, t["rgf"][:],
                                                 ALU.mult, ALU.mult), "dve")
                    v.wait_ge(p.S["dve"], p.C["dve"])
                    p.inc(v.reduce_max(t["tiny"][:, 2:3],
                                       g2b[:].rearrange("q a b -> q (a b)"),
                                       axis=AX.X, apply_absolute_value=True), "dve")
                    v.wait_ge(p.S["dve"], p.C["dve"])
                    p.inc(v.tensor_max(t["gmax"][:, 3:4], t["gmax"][:, 3:4],
                                       t["tiny"][:, 2:3]), "dve")
                tok_g2 = p.snap()

                @blk.gpsimd
                def _(gp, g=g, half=half, g2b=g2b):
                    p.wait(gp, tok_g2, only=("dve",))
                    dma(p, gp, "qg",
                        p.d["gate_sp"].ap()[g, half * 1536:(half + 1) * 1536, :]
                        .rearrange("(a q) b -> q a b", q=128), g2b[:])
                prev[f"d2_st_free{hb}"] = {"qg": p.val("qg")}
        prev["gmax_tok"] = {"dve": p.val("dve")}

    emit_cc(p, 3, 1, 3, prev)

    @blk.vector
    def _(v):
        p.wait(v, {"dve": p.val("dve")})
        p.inc(v.tensor_mul(t["tiny"][0:1, 4:5], t["gr_row"][0:1, 0:1],
                           t["tiny"][0:1, 2:3]), "dve")
        v.wait_ge(p.S["dve"], p.C["dve"])
        p.inc(v.tensor_scalar(t["tiny"][0:1, 4:5], t["tiny"][0:1, 4:5],
                              1.0 / QMAX, 1e-12, ALU.mult, ALU.max), "dve")
        v.wait_ge(p.S["dve"], p.C["dve"])
        p.inc(v.reciprocal(t["tiny"][0:1, 5:6], t["tiny"][0:1, 4:5]), "dve")
        v.wait_ge(p.S["dve"], p.C["dve"])
        p.inc(v.tensor_mul(t["tiny"][0:1, 5:6], t["tiny"][0:1, 5:6],
                           t["tiny"][0:1, 2:3]), "dve")
    tvc3 = p.snap()
    tok_sc3 = bcast_col(p, tvc3, t["tiny"][0:1, 5:6], t["scl"][:, 5:6])

    emit_barrier(p)
    with ExitStack() as es_d3:
        for b in range(2):
            p.sb(f"usp{b}", [128, 12, GRPD], BF16, es=es_d3)
            p.sb(f"gsp{b}", [128, 12, GRPD], BF16, es=es_d3)
            p.sb(f"ob16{b}", [128, 12, GRPD], BF16, es=es_d3)
        p.sb("ruf", [128, 12, GRPD], F32, es=es_d3)
        p.sb("rg2f", [128, 12, GRPD], F32, es=es_d3)
        for g in range(NGRPD):
            for half in range(2):
                hb = half
                usp, gsp, ob16 = t[f"usp{hb}"], t[f"gsp{hb}"], t[f"ob16{hb}"]

                @blk.sync
                def _(sync, g=g, half=half, hb=hb, usp=usp, gsp=gsp):
                    p.wait(sync, prev.get(f"d3_buf_free{hb}"))
                    sl = p.d["up_sp"].ap()[g, half * 1536:(half + 1) * 1536, :]
                    dma(p, sync, "qs", usp[:],
                        sl.rearrange("(a q) b -> q a b", q=128))
                    sl = p.d["gate_sp"].ap()[g, half * 1536:(half + 1) * 1536, :]
                    dma(p, sync, "qs", gsp[:],
                        sl.rearrange("(a q) b -> q a b", q=128))
                tok_l = {"qs": p.val("qs")}

                @blk.vector
                def _(v, half=half, g=g, hb=hb, usp=usp, gsp=gsp, ob16=ob16):
                    p.wait(v, tok_l)
                    p.wait(v, tok_sc3, only=("dve",))
                    p.wait(v, prev.get(f"d3_st_free{hb}"))
                    p.inc(v.tensor_scalar(t["ruf"][:], usp[:], t["scl"][:, 2:3],
                                          t["fconst"][:, 2:3], ALU.mult, ALU.add),
                          "dve")
                    n = p.inc(v.tensor_scalar(t["rg2f"][:], gsp[:],
                                              t["scl"][:, 5:6],
                                              t["fconst"][:, 2:3],
                                              ALU.mult, ALU.add), "dve")
                    prev[f"d3_buf_free{hb}"] = {"dve": n}
                    v.wait_ge(p.S["dve"], p.C["dve"])
                    p.inc(v.tensor_scalar(t["ruf"][:], t["ruf"][:],
                                          t["fconst"][:, 2:3], None, ALU.subtract),
                          "dve")
                    p.inc(v.tensor_scalar(t["rg2f"][:], t["rg2f"][:],
                                          t["fconst"][:, 2:3], None, ALU.subtract),
                          "dve")
                    v.wait_ge(p.S["dve"], p.C["dve"])
                    p.inc(v.tensor_mul(t["ruf"][:], t["ruf"][:], t["rg2f"][:]), "dve")
                    v.wait_ge(p.S["dve"], p.C["dve"])
                    p.inc(v.reduce_max(t["tiny"][:, 2:3],
                                       t["ruf"][:].rearrange("q a b -> q (a b)"),
                                       axis=AX.X, apply_absolute_value=True), "dve")
                    v.wait_ge(p.S["dve"], p.C["dve"])
                    p.inc(v.tensor_max(t["gmax"][:, 4:5], t["gmax"][:, 4:5],
                                       t["tiny"][:, 2:3]), "dve")
                    p.inc(v.tensor_copy(ob16[:], t["ruf"][:]), "dve")
                tok_o2 = p.snap()

                @blk.gpsimd
                def _(gp, g=g, half=half, ob16=ob16):
                    p.wait(gp, tok_o2, only=("dve",))
                    dma(p, gp, "qg",
                        p.d["up_sp"].ap()[g, half * 1536:(half + 1) * 1536, :]
                        .rearrange("(a q) b -> q a b", q=128), ob16[:])
                prev[f"d3_st_free{hb}"] = {"qg": p.val("qg")}
        prev["gmax_tok"] = {"dve": p.val("dve")}

    emit_cc(p, 4, 1, 4, prev)

    @blk.vector
    def _(v):
        p.wait(v, {"dve": p.val("dve")})
        p.inc(v.tensor_mul(t["tiny"][0:1, 6:7], t["tiny"][0:1, 0:1],
                           t["tiny"][0:1, 4:5]), "dve")
        v.wait_ge(p.S["dve"], p.C["dve"])
        p.inc(v.tensor_mul(t["tiny"][0:1, 7:8], t["tiny"][0:1, 6:7],
                           t["gr_row"][0:1, 0:1]), "dve")
        v.wait_ge(p.S["dve"], p.C["dve"])
        p.inc(v.tensor_scalar(t["tiny"][0:1, 7:8], t["tiny"][0:1, 7:8],
                              1.0 / QMAX, 1e-12, ALU.mult, ALU.max), "dve")
        v.wait_ge(p.S["dve"], p.C["dve"])
        p.inc(v.reciprocal(t["tiny"][0:1, 8:9], t["tiny"][0:1, 7:8]), "dve")
        v.wait_ge(p.S["dve"], p.C["dve"])
        p.inc(v.tensor_mul(t["tiny"][0:1, 6:7], t["tiny"][0:1, 6:7],
                           t["tiny"][0:1, 8:9]), "dve")
    tvc4 = p.snap()
    bcast_col(p, tvc4, t["tiny"][0:1, 6:7], t["scl"][:, 6:7])
    tok_sc4 = bcast_col(p, tvc4, t["tiny"][0:1, 7:8], t["scl"][:, 7:8])

    emit_barrier(p)
    with ExitStack() as es_d4:
        p.sb("wdT_b", [128, NFF, 1024], BF16, es=es_d4)
        p.sb("osp0", [128, NFF, GRPD], BF16, es=es_d4)
        p.sb("osp1", [128, NFF, GRPD], BF16, es=es_d4)
        p.sb("rof", [128, 12, GRPD], F32, es=es_d4)
        p.sb("rob", [128, NFF, GRPD], BF16, es=es_d4)
        sxv = (t["rof"][:].rearrange("q a b -> q (a b)")[:, 0:4 * 1024]
               .rearrange("q (s d) -> q s d", d=1024))

        @blk.sync
        def _(sync):
            for ic in range(NFF):
                p.inc(sync.dma_start(out=t["wdT_b"][:, ic, :],
                                     in_=p.d["wq_w_down"].ap()
                                     [:, ic * 128:(ic + 1) * 128],
                                     transpose=True), "qs", 16)
        tok_wd = {"qs": p.val("qs")}

        for g in range(NGRPD):
            gb = g % 2
            osp = t[f"osp{gb}"]

            @blk.sync
            def _(sync, g=g, gb=gb, osp=osp):
                p.wait(sync, prev.get(f"d4_free{gb}"))
                dma(p, sync, "qs", osp[:],
                    p.d["up_sp"].ap()[g].rearrange("(a q) b -> q a b", q=128))
            tok_l = {"qs": p.val("qs")}

            for half in range(2):
                @blk.vector
                def _(v, half=half, osp=osp):
                    p.wait(v, tok_l)
                    p.wait(v, tok_sc4, only=("dve",))
                    p.wait(v, prev.get("d4_mm_done"))
                    osl = osp[:, half * 12:(half + 1) * 12, :]
                    p.inc(v.tensor_scalar(t["rof"][:], osl, t["scl"][:, 6:7],
                                          t["fconst"][:, 2:3], ALU.mult, ALU.add),
                          "dve")
                    v.wait_ge(p.S["dve"], p.C["dve"])
                    p.inc(v.tensor_scalar(t["rob"][:, half * 12:(half + 1) * 12, :],
                                          t["rof"][:], t["fconst"][:, 2:3], None,
                                          ALU.subtract), "dve")
            tok_ro = p.snap()
            prev[f"d4_free{gb}"] = {"dve": tok_ro["dve"]}

            @blk.sync
            def _(sync, g=g):
                p.wait(sync, tok_ro, only=("dve",))
                dma(p, sync, "qs", sxv,
                    p.d["x3"].ap()[g * GRPD:(g + 1) * GRPD, :]
                    .rearrange("(s q) d -> q s d", q=128))
            tok_lsx = {"qs": p.val("qs")}

            for s in range(NSUBD):
                for dh in range(2):
                    i = s * 2 + dh
                    bn = f"pb{2 + i % 2}"
                    bank = t[bn]

                    @blk.tensor
                    def _(te, s=s, dh=dh, bank=bank, bn=bn):
                        p.wait(te, tok_ro, only=("dve",))
                        p.wait(te, tok_wd)
                        p.wait(te, p.bank_tok.get(bn))
                        p.mmdt(te, "b")
                        for ic in range(NFF):
                            ii = te.matmul(bank[:],
                                           t["rob"][:, ic, s * 128:(s + 1) * 128],
                                           t["wdT_b"][:, ic, dh * 512:(dh + 1) * 512],
                                           start=(ic == 0), stop=(ic == NFF - 1))
                        p.inc(ii, "pe")
                    tpe = p.snap()

                    @blk.scalar
                    def _(a, s=s, dh=dh, bank=bank, bn=bn):
                        p.wait(a, tpe, only=("pe",))
                        p.wait(a, prev.get("tmw_free"), only=("dve",))
                        p.inc(a.activation(t["tmw"][:, s, dh * 512:(dh + 1) * 512],
                                           bank[:], AF.Identity,
                                           scale=t["scl"][:, 7:8]), "act")
                    p.bank_tok[bn] = p.snap()
            tok_y = p.snap()
            prev["d4_mm_done"] = {"pe": p.val("pe")}

            ta = emit_ln(p, t["tmw"], t["ob"],
                         wait0={"act": tok_y["act"]},
                         extra_apply_wait=prev.get("out_store"), nsub=NSUBD)
            prev["tmw_free"] = {"dve": p.val("dve")}

            @blk.vector
            def _(v, g=g):
                p.wait(v, ta, only=("act",))
                p.wait(v, tok_lsx)
                p.inc(v.tensor_add(t["ob"][:], t["ob"][:], sxv), "dve")
            tok_out = p.snap()

            @blk.gpsimd
            def _(gp, g=g):
                p.wait(gp, tok_out, only=("dve",))
                dma(p, gp, "qg",
                    p.d["out"].ap()[g * GRPD:(g + 1) * GRPD, :]
                    .rearrange("(s q) d -> q s d", q=128), t["ob"][:])
            prev["out_store"] = {"qg": p.val("qg")}
    return p.snap()



def build_program(phases, dev_taps=()):
    nc = bass.Bass("TRN2", target_bir_lowering=False)
    es = ExitStack()
    p = Prog(nc, es, phases, dev_taps)
    with es:
        p.dram_in("x", [NTOK, D])
        for nm, shape in SHARED_SPECS.items():
            p.dram_in(nm, shape)
        if "C" in phases and "AB" not in phases:
            p.dram_in("x2", [NTOK, D])
        else:
            p.dram_tmp("x2", [NTOK, D])
        if "D" in phases and "C" not in phases:
            p.dram_in("x3", [NTOK, D])
            p.dram_in("xn3", [NTOK, D])
        else:
            p.dram_tmp("x3", [NTOK, D])
            p.dram_tmp("xn3", [NTOK, D])
        p.dram_tmp("bounce", [16, 2 * GRP])
        if "D" in phases:
            for nm in ["w_up", "w_gate"]:
                p.dram_tmp("wq_" + nm, [3072, 1024], BF16)
            p.dram_tmp("wq_w_down", [1024, 3072], BF16)
            p.dram_tmp("up_sp", [NTOK // 512, 3072, 512], BF16)
            p.dram_tmp("gate_sp", [NTOK // 512, 3072, 512], BF16)
            for j in range(1, 5):
                p.dram_tmp(f"ccin{j}", [16, 128])
                p.dram_tmp(f"ccout{j}", [16, 128])
        if "dev_x1" in p.dev_taps:
            p.dram_out("dev_x1", [NTOK, D])
        if "dev_dbg" in p.dev_taps:
            if "D" in p.phases:
                p.dram_out("dbg_wupT", [128, NCH, DFF], BF16)
                p.dram_out("dbg_xq", [128, NCH, GRPD], BF16)
                p.dram_out("dbg_scl", [128, 8])
                p.dram_out("dbg_tiny", [128, 16])
                p.dram_out("dbg_grow", [1, 16])
                p.dram_out("dbg_gmax", [128, 8])
                p.dram_out("dbg_pre", [128, 40])
            p.dram_out("dbg_xn", [128, NSUB, 1024])
            p.dram_out("dbg_sx", [128, NSUB, 1024])
            p.dram_out("dbg_st", [128, 40])
            p.dram_out("dbg_gx", [128, NCH, GRP])
            p.dram_out("dbg_ga", [128, NCH, GRP])
            p.dram_out("dbg_at", [128, NCH, GRP])
            p.dram_out("dbg_ut", [128, NCH, GRP])
            p.dram_out("dbg_ht", [128, NCH, GRP])
            p.dram_out("dbg_y", [128, NSUB, 1024])
            p.dram_out("dbg_projf", [128, 2, GRP])
            p.dram_out("dbg_lg", [4, GRP])
            p.dram_out("dbg_ret", [128, NSUB * 4])
            p.dram_out("dbg_tmpf", [15, GRP])
            p.dram_out("dbg_y2", [128, NSUB, 1024])
        p.dram_out("out", [NTOK, D])

        es.enter_context(nc.Block())
        p.blk = FlatBlk(nc)
        alloc_common(p)
        emit_setup(p)
        if "AB" in p.phases:
            with ExitStack() as es_ab:
                wtok = emit_l0_weights(p, es_ab)
                emit_ab(p, es_ab, {"qg": wtok["qg"]})
        ctok = {"dve": 0}
        if "C" in p.phases:
            emit_barrier(p)
            with ExitStack() as es_c:
                wtok_c = emit_l1_weights(p, es_c)
                emit_c(p, es_c, {"dve": wtok_c["dve"]})
                ctok = {"dve": p.val("dve")}
        if "D" in p.phases:
            emit_barrier(p)
            with ExitStack() as es_d:
                dqtok = emit_dequant(p, es_d)
            emit_barrier(p)
            with ExitStack() as es_d:
                emit_d(p, es_d, ctok, dqtok)

        @p.blk.gpsimd
        def _(gp):
            gp.wait_ge(p.sem("qg"), p.val("qg"))
            gp.wait_ge(p.sem("qs"), p.val("qs"))
    return p


def kernel(**inputs):
    shared = prepare_shared_inputs(inputs)
    x = np.ascontiguousarray(np.asarray(inputs["L0_attn_x"], np.float32)
                             .reshape(16, SEQ, D))
    p = build_program(("AB", "C", "D"))
    in_maps = []
    for c in range(N_CORES):
        m = dict(shared)
        m["x"] = np.ascontiguousarray(x[2 * c:2 * c + 2].reshape(NTOK, D))
        in_maps.append(m)
    res = run_bass_kernel_spmd(p.nc, in_maps, core_ids=list(range(N_CORES)))
    out = np.concatenate([r["out"].reshape(2, SEQ, D) for r in res.results], 0)
    return out


# version 08
# speedup vs baseline: 1.0352x; 1.0352x over previous
import sys
sys.path.insert(0, "/opt/trn_rl_repo")
from contextlib import ExitStack
import numpy as np
import concourse.bass as bass
import concourse.mybir as mybir
from concourse.bass_utils import run_bass_kernel_spmd

F32 = mybir.dt.float32
BF16 = mybir.dt.bfloat16
AF = mybir.ActivationFunctionType
ALU = mybir.AluOpType
AX = mybir.AxisListType

N_CORES = 8
SEQ = 4096
D = 1024
NTOK = 2 * SEQ
GRP = 256
NGRP = NTOK // GRP
GPB = SEQ // GRP
NSUB = GRP // 128
NCH = D // 128
C_RND = 12582912.0
LN_EPS = 1e-5
NH, DH, CHUNK = 16, 64, 128
NCHK = GRP // CHUNK
DFF = 3072
NFF = DFF // 128
QMAX = 32767.0
CLAMP_MAX = float(np.log(1.0 / 0.01))
ENGS = ["pe", "dve", "act", "pool", "qs", "qg", "qc"]
MAXG_C = 999
MAXG_AB = 999
AB_F32GA = True
AB_GA_N = 8
AB_GA_MM = True
AB_GA_EVAC = True
TR_DUAL = True
TR_EVAC = "dve"
AB_STAGE = 99
C_STAGE = 99
DEV_NO_CC = False


def grows(g):
    b, t = g // GPB, g % GPB
    r0 = b * SEQ + t * GRP
    return r0, r0 + GRP


def prepare_shared_inputs(w):
    f = np.float32
    C = np.ascontiguousarray
    out = {}
    out["w_in_h"] = C(np.asarray(w["L0_attn_w_in"], f))
    out["w_a_h"] = C(np.asarray(w["L0_attn_w_a"], f))
    out["bin_t"] = C(np.asarray(w["L0_attn_b_in"], f).T)
    out["ba_t"] = C(np.asarray(w["L0_attn_b_a"], f).T)
    out["ap_t"] = C(np.asarray(w["L0_attn_a_param"], f).reshape(8, 128).T)
    out["WtT"] = C(np.asarray(w["L0_ffn_W"], f).T)
    out["WpT"] = C(np.asarray(w["L0_ffn_Wp"], f).T)
    out["bp_t"] = C(np.asarray(w["L0_ffn_bp"], f).reshape(2, 128).T)
    out["simc"] = C(np.asarray(w["L0_ffn_sim"], f))
    out["simT"] = C(np.asarray(w["L0_ffn_sim"], f).T)
    out["temp"] = C(np.asarray(w["L0_ffn_temp"], f).reshape(1, 1))
    out["WdallT"] = C(np.concatenate(
        [np.asarray(w[f"L0_ffn_Wd{i}"], f) for i in range(4)], 0).T)
    out["WuallT"] = C(np.concatenate(
        [np.asarray(w[f"L0_ffn_Wu{i}"], f) for i in range(4)], 1).T)
    M = np.zeros((4, 15), dtype=f)
    r0 = 0
    for e, r in enumerate([1, 2, 4, 8]):
        M[e, r0:r0 + r] = 1.0
        r0 += r
    out["Mmap"] = M
    for nm in ["Wq", "Wk", "Wv", "Wg", "Wog", "Wo"]:
        out[nm + "T"] = C(np.asarray(w[f"L1_attn_{nm}"], f).T)
    out["WdT"] = C(np.asarray(w["L1_attn_Wd"], f).T)
    out["bd_c"] = C(np.asarray(w["L1_attn_bd"], f).reshape(16, 1))
    out["normw_t"] = C(np.asarray(w["L1_attn_norm_w"], f).reshape(8, 128).T)
    out["w_gate"] = C(np.asarray(w["L1_ffn_w_gate"], f))
    out["w_up"] = C(np.asarray(w["L1_ffn_w_up"], f))
    out["w_down"] = C(np.asarray(w["L1_ffn_w_down"], f))
    out["ident"] = np.eye(128, dtype=f)
    fc = np.zeros((128, 8), dtype=f)
    fc[:, 0] = 1e-5
    fc[:, 1] = 1.0
    fc[:, 2] = 12582912.0
    fc[:, 3] = -12582912.0
    fc[:, 4] = 65535.0 / 65536.0
    fc[:, 5] = np.float32(np.finfo(np.float32).eps)
    out["fconst"] = fc
    out["ones_row"] = np.ones((1, 128), dtype=f)
    out["ones_col"] = np.ones((128, 1), dtype=f)
    out["triu"] = np.triu(np.ones((128, 128), dtype=f))
    lt = np.zeros((128, 128), dtype=f)
    lt[np.tril_indices(128, -1)] = -1e30
    out["ltneg"] = lt
    return out


SHARED_SPECS = {
    "w_in_h": [8, 128, 128], "w_a_h": [8, 128, 128],
    "bin_t": [128, 8], "ba_t": [128, 8], "ap_t": [128, 8],
    "WtT": [1024, 1024], "WpT": [1024, 256], "bp_t": [128, 2],
    "simc": [256, 4], "simT": [4, 256], "temp": [1, 1],
    "WdallT": [1024, 15], "WuallT": [15, 1024], "Mmap": [4, 15],
    "WqT": [1024, 1024], "WkT": [1024, 1024], "WvT": [1024, 1024],
    "WgT": [1024, 1024], "WogT": [1024, 1024], "WoT": [1024, 1024],
    "WdT": [1024, 16], "bd_c": [16, 1], "normw_t": [128, 8],
    "w_gate": [3072, 1024], "w_up": [3072, 1024], "w_down": [1024, 3072],
    "ident": [128, 128], "ones_row": [1, 128], "ones_col": [128, 1],
    "triu": [128, 128], "fconst": [128, 8], "ltneg": [128, 128],
}


class Prog:
    def __init__(self, nc, es, phases, dev_taps=()):
        self.nc = nc
        self.es = es
        self.phases = list(phases)
        self.dev_taps = set(dev_taps)
        self.t = {}
        self.d = {}
        self.S = {}
        self.C = {}
        self.blk = None
        self.bank_tok = {}

    def dram_in(self, name, shape, dtype=F32):
        self.d[name] = self.nc.dram_tensor(name, list(shape), dtype, kind="ExternalInput")
        return self.d[name]

    def dram_out(self, name, shape, dtype=F32):
        self.d[name] = self.nc.dram_tensor(name, list(shape), dtype, kind="ExternalOutput")
        return self.d[name]

    def dram_tmp(self, name, shape, dtype=F32):
        kind = "ExternalOutput" if name in self.dev_taps else "Internal"
        self.d[name] = self.nc.dram_tensor(name, list(shape), dtype, kind=kind)
        return self.d[name]

    _uid = 0

    def sb(self, name, shape, dtype=F32, es=None):
        Prog._uid += 1
        self.t[name] = (es or self.es).enter_context(
            self.nc.sbuf_tensor(f"sb{Prog._uid}_{name}", list(shape), dtype))
        return self.t[name]

    def ps(self, name, shape, dtype=F32, es=None):
        self.t[name] = (es or self.es).enter_context(
            self.nc.psum_tensor("ps_" + name, list(shape), dtype))
        return self.t[name]

    def sem(self, name):
        if name not in self.S:
            self.S[name] = self.es.enter_context(self.nc.semaphore(name))
            self.C[name] = 0
        return self.S[name]

    def inc(self, instr, name, amt=1):
        instr.then_inc(self.sem(name), amt)
        self.C[name] += amt
        return self.C[name]

    def val(self, name):
        self.sem(name)
        return self.C[name]

    def snap(self):
        return {k: self.val(k) for k in ENGS}

    def mmdt(self, te, d):
        if getattr(self, "_mmdt", None) != d:
            if getattr(self, "_mmdt", None) is not None:
                te.wait_ge(self.S["pe"], self.C["pe"])
            self._mmdt = d

    def wait(self, eng, tok, only=None):
        if tok is None:
            return
        items = tok.items() if isinstance(tok, dict) else tok
        for k, v in items:
            if only is not None and k not in only:
                continue
            if v > 0:
                eng.wait_ge(self.S[k], v)

    def tt(self, name):
        k = name + getattr(self, "sfx", "")
        return self.t[k] if k in self.t else self.t[name]


class FlatBlk:

    def __init__(self, nc):
        self._nc = nc

    def sync(self, f):
        f(self._nc.sync)

    def tensor(self, f):
        f(self._nc.tensor)

    def vector(self, f):
        f(self._nc.vector)

    def scalar(self, f):
        f(self._nc.scalar)

    def gpsimd(self, f):
        f(self._nc.gpsimd)


def dma(p, eng, q, out, in_):
    return p.inc(eng.dma_start(out=out, in_=in_), q, 16)


def emit_barrier(p):
    bar = p.snap()
    blk = p.blk

    @blk.sync
    def _(e):
        p.wait(e, bar)

    @blk.tensor
    def _(e):
        p.wait(e, bar)

    @blk.vector
    def _(e):
        p.wait(e, bar)

    @blk.scalar
    def _(e):
        p.wait(e, bar)

    @blk.gpsimd
    def _(e):
        p.wait(e, bar)
    return bar



def emit_ln(p, src, dst, wait0=None, extra_apply_wait=None, nsub=NSUB):
    blk, t = p.blk, p.t
    st = p.tt("stats")

    @blk.vector
    def _(v):
        p.wait(v, wait0)
        for s in range(nsub):
            ii = v.reduce_sum(st[:, s:s + 1], src[:, s, :], axis=AX.X)
        p.inc(ii, "dve")
    tv1 = p.snap()

    @blk.scalar
    def _(a):
        p.wait(a, wait0)
        p.wait(a, tv1, only=("dve",))
        p.wait(a, extra_apply_wait)
        for s in range(nsub):
            ii = a.activation(dst[:, s, :], src[:, s, :],
                              AF.Square, accum_out=st[:, 4 + s:5 + s])
        p.inc(ii, "act")
    ta1 = p.snap()

    @blk.vector
    def _(v):
        p.wait(v, ta1, only=("act",))
        v.tensor_scalar_mul(st[:, 8:8 + nsub], st[:, 0:nsub], 1.0 / D)
        p.inc(v.tensor_scalar_mul(st[:, 12:12 + nsub], st[:, 4:4 + nsub], 1.0 / D),
              "dve")
        v.wait_ge(p.S["dve"], p.C["dve"])
        p.inc(v.tensor_mul(st[:, 16:16 + nsub], st[:, 8:8 + nsub], st[:, 8:8 + nsub]),
              "dve")
        v.wait_ge(p.S["dve"], p.C["dve"])
        p.inc(v.tensor_sub(st[:, 20:20 + nsub], st[:, 16:16 + nsub],
                           st[:, 12:12 + nsub]), "dve")
    tv2 = p.snap()

    @blk.scalar
    def _(a):
        p.wait(a, tv2, only=("dve",))
        p.inc(a.activation(st[:, 24:24 + nsub], st[:, 20:20 + nsub], AF.Sqrt,
                           scale=-1.0, bias=t["fconst"][:, 0:1]), "act")
    ta2 = p.snap()

    @blk.vector
    def _(v):
        p.wait(v, ta2, only=("act",))
        p.inc(v.reciprocal(st[:, 28:28 + nsub], st[:, 24:24 + nsub]), "dve")
        v.wait_ge(p.S["dve"], p.C["dve"])
        p.inc(v.scalar_tensor_tensor(st[:, 32:32 + nsub], st[:, 8:8 + nsub], -1.0,
                                     st[:, 28:28 + nsub], ALU.mult, ALU.mult), "dve")
    tv3 = p.snap()

    @blk.scalar
    def _(a):
        p.wait(a, tv3, only=("dve",))
        p.wait(a, extra_apply_wait)
        for s in range(nsub):
            ii = a.activation(dst[:, s, :], src[:, s, :], AF.Identity,
                              bias=st[:, 32 + s:33 + s], scale=st[:, 28 + s:29 + s])
        p.inc(ii, "act")
    return p.snap()


def emit_tr_fm(p, src_tm, dst_fm, wait_tok, dtype_note="bf16", banks=(0, 1, 2, 3),
               evac_eng="dve", dual_f32=None, nsub=NSUB):
    blk, t = p.blk, p.t
    fills = [(s, cq) for s in range(nsub) for cq in range(2)]
    for fi, (s, cq) in enumerate(fills):
        @blk.tensor
        def _(te, s=s, cq=cq):
            p.wait(te, wait_tok)
            p.mmdt(te, "f")
            for j in range(4):
                bn = f"pb{banks[j]}"
                p.wait(te, p.bank_tok.get(bn))
                ch = cq * 4 + j
                p.inc(te.transpose(t[bn][:, 0:128],
                                   src_tm[:, s, ch * 128:(ch + 1) * 128],
                                   t["ident"][:]), "pe")
        tpe = p.snap()

        if evac_eng == "dve":
            @blk.vector
            def _(v, s=s, cq=cq):
                p.wait(v, tpe, only=("pe",))
                for j in range(4):
                    ii = v.tensor_copy(
                        dst_fm[:, cq * 4 + j, s * 128:(s + 1) * 128],
                        t[f"pb{banks[j]}"][:, 0:128])
                    if dual_f32 is not None:
                        ii = v.tensor_copy(
                            dual_f32[:, cq * 4 + j, s * 128:(s + 1) * 128],
                            t[f"pb{banks[j]}"][:, 0:128])
                p.inc(ii, "dve")
        else:
            @blk.scalar
            def _(a, s=s, cq=cq):
                p.wait(a, tpe, only=("pe",))
                for j in range(4):
                    ii = a.copy(dst_fm[:, cq * 4 + j, s * 128:(s + 1) * 128],
                                t[f"pb{banks[j]}"][:, 0:128])
                    if dual_f32 is not None:
                        ii = a.copy(dual_f32[:, cq * 4 + j, s * 128:(s + 1) * 128],
                                    t[f"pb{banks[j]}"][:, 0:128])
                p.inc(ii, "act")
        tokc = p.snap()
        for j in range(4):
            p.bank_tok[f"pb{banks[j]}"] = tokc
    return p.snap()



def alloc_common(p):
    p.sb("stats", [128, 40], F32)
    p.sb("tiny", [128, 16], F32)
    p.sb("sq_scr", [128, NSUB, 1024], F32)
    p.sb("state0", [128, NCH], F32)
    p.sb("msp8", [128, 8], F32)
    p.sb("gmax", [128, 8], F32)
    p.sb("glob", [16, 128], F32)
    for nm in ["ident", "ones_row", "ones_col", "triu", "bin_t", "ba_t",
               "bp_t", "bd_c", "normw_t", "fconst", "ltneg"]:
        p.sb(nm, SHARED_SPECS[nm], F32)
    p.sb("ones16", [16, 128], F32)
    p.sb("mtop", [1, 128], F32)
    p.sb("mbot", [1, 128], F32)
    p.sb("identb", [128, 128], BF16)
    p.sb("Mmap", [4, 15], F32)
    p.sb("rstd_es", [4, 1], F32)
    p.sb("ap_s", [128, 8], F32)
    p.sb("ab_s", [128, 8], F32)
    p.sb("sim_s", [4, 256], F32)
    p.sb("sc_s", [4, 8], F32)
    for i in range(7):
        p.ps(f"pb{i}", [128, 512], F32)
    p.ps("pbt", [128, 1024], BF16)


def emit_setup(p):
    blk, t = p.blk, p.t

    @blk.sync
    def _(sync):
        for nm in ["ident", "ones_row", "ones_col", "triu", "bin_t", "ba_t",
                   "bp_t", "Mmap", "bd_c", "normw_t", "fconst", "ltneg"]:
            dma(p, sync, "qs", t[nm][:], p.d[nm].ap())
        dma(p, sync, "qs", t["ap_s"][:], p.d["ap_t"].ap())
        dma(p, sync, "qs", t["sim_s"][:], p.d["simT"].ap())
        dma(p, sync, "qs", t["sc_s"][0:1, 0:1], p.d["temp"].ap())
    tok_l = p.snap()

    @blk.scalar
    def _(a):
        p.wait(a, tok_l, only=("qs",))
        a.copy(t["identb"][:], t["ident"][:])
        p.inc(a.activation(t["ab_s"][:], t["ap_s"][:], AF.Abs), "act")
        a.wait_ge(p.S["act"], p.C["act"])
        p.inc(a.activation(t["msp8"][:], t["ab_s"][:], AF.Exp, scale=-1.0), "act")
        a.wait_ge(p.S["act"], p.C["act"])
        p.inc(a.activation(t["ab_s"][:], t["msp8"][:], AF.Ln,
                           bias=t["fconst"][:, 1:2]), "act")
        p.inc(a.activation(t["sim_s"][:], t["sim_s"][:], AF.Square,
                           accum_out=t["sc_s"][0:4, 1:2]), "act")
    ta = p.snap()

    @blk.vector
    def _(v):
        p.wait(v, ta, only=("act",))
        p.inc(v.tensor_scalar_max(t["msp8"][:], t["ap_s"][:], 0.0), "dve")
        v.wait_ge(p.S["dve"], p.C["dve"])
        p.inc(v.tensor_add(t["msp8"][:], t["msp8"][:], t["ab_s"][:]), "dve")
        v.wait_ge(p.S["dve"], p.C["dve"])
        p.inc(v.tensor_scalar_mul(t["msp8"][:], t["msp8"][:], -8.0), "dve")
        v.memset(t["state0"][:], 0.0)
        v.memset(t["gmax"][:], 0.0)
        v.memset(t["glob"][:], 0.0)
        v.memset(t["ones16"][:], 1.0)
        v.memset(t["mtop"][0:1, 0:64], 1.0)
        v.memset(t["mtop"][0:1, 64:128], 0.0)
        v.memset(t["mbot"][0:1, 0:64], 0.0)
        v.memset(t["mbot"][0:1, 64:128], 1.0)
        p.inc(v.tensor_scalar_min(t["sc_s"][0:1, 0:1], t["sc_s"][0:1, 0:1], CLAMP_MAX),
              "dve")
    tv = p.snap()

    @blk.scalar
    def _(a):
        p.wait(a, tv, only=("dve",))
        p.inc(a.activation(t["sc_s"][0:1, 2:3], t["sc_s"][0:1, 0:1], AF.Exp), "act")
        p.inc(a.activation(t["sc_s"][0:4, 3:4], t["sc_s"][0:4, 1:2], AF.Sqrt), "act")
    ta = p.snap()

    @blk.tensor
    def _(te):
        p.wait(te, ta, only=("act",))
        p.mmdt(te, "f")
        p.inc(te.matmul(t["pb6"][0:4, 0:1], t["ones_row"][0:1, 0:4],
                        t["sc_s"][0:1, 2:3], start=True, stop=True), "pe")
    tpe = p.snap()

    @blk.vector
    def _(v):
        p.wait(v, tpe, only=("pe",))
        p.inc(v.tensor_scalar_max(t["sc_s"][0:4, 5:6], t["sc_s"][0:4, 3:4], 1e-12),
              "dve")
        v.wait_ge(p.S["dve"], p.C["dve"])
        p.inc(v.reciprocal(t["sc_s"][0:4, 4:5], t["sc_s"][0:4, 5:6]), "dve")
        v.wait_ge(p.S["dve"], p.C["dve"])
        p.inc(v.tensor_mul(t["rstd_es"][:], t["sc_s"][0:4, 4:5], t["pb6"][0:4, 0:1]),
              "dve")
    p.bank_tok["pb6"] = p.snap()


def emit_l0_weights(p, es):
    p.sb("w_in_b", [128, 8, 128], BF16, es=es)
    p.sb("w_a_f", [128, 8, 128], F32, es=es)
    p.sb("WtT_b", [128, NCH, 1024], BF16, es=es)
    p.sb("WpT_b", [128, NCH, 256], BF16, es=es)
    p.sb("WdallT_b", [128, NCH, 15], BF16, es=es)
    p.sb("WuallT_b", [15, 1024], BF16, es=es)
    p.sb("simc_t", [128, 2, 4], F32, es=es)
    blk, t = p.blk, p.t

    @blk.gpsimd
    def _(g):
        for h in range(8):
            dma(p, g, "qg", t["w_in_b"][:, h, :], p.d["w_in_h"].ap()[h])
            dma(p, g, "qg", t["w_a_f"][:, h, :], p.d["w_a_h"].ap()[h])
        dma(p, g, "qg", t["WtT_b"][:],
            p.d["WtT"].ap().rearrange("(c q) o -> q c o", q=128))
        dma(p, g, "qg", t["WpT_b"][:],
            p.d["WpT"].ap().rearrange("(c q) o -> q c o", q=128))
        dma(p, g, "qg", t["WdallT_b"][:],
            p.d["WdallT"].ap().rearrange("(c q) o -> q c o", q=128))
        dma(p, g, "qg", t["WuallT_b"][:], p.d["WuallT"].ap())
        dma(p, g, "qg", t["simc_t"][:],
            p.d["simc"].ap().rearrange("(c q) o -> q c o", q=128))
    return p.snap()



def alloc_ab(p, es):
    p.sb("sx0", [128, NSUB, 1024], F32, es=es)
    p.sb("sx1", [128, NSUB, 1024], F32, es=es)
    p.sb("tmw", [128, NSUB, 1024], F32, es=es)
    for nm in ["gx", "ga", "a_t", "mlt", "u_t"]:
        p.sb(nm, [128, NCH, GRP], F32, es=es)
    p.t["h_t"] = p.t["gx"]
    p.t["ob"] = p.t["a_t"][:].rearrange("q (s x) g -> q s (x g)", s=2)
    p.sb("xnb", [128, NCH, GRP], BF16, es=es)
    p.sb("xnf", [128, NCH, GRP], F32, es=es)
    p.sb("projf", [128, 2, GRP], F32, es=es)
    p.sb("lg", [4, GRP], F32, es=es)
    p.sb("rnorm", [1, GRP], F32, es=es)
    p.sb("rnorm2", [1, GRP], F32, es=es)
    p.sb("lw_tm", [128, NSUB * 4], F32, es=es)
    p.sb("ret_tm", [128, NSUB * 4], F32, es=es)
    p.sb("ret_fm", [4, GRP], F32, es=es)
    p.sb("tmpf", [15, GRP], F32, es=es)
    p.sb("tmp2b", [15, GRP], BF16, es=es)
    p.sb("tmwB", [128, NSUB, 1024], F32, es=es)
    for nm in ["gx", "ga", "a_t", "mlt", "u_t"]:
        p.sb(nm + "B", [128, NCH, GRP], F32, es=es)
    p.t["h_tB"] = p.t["gxB"]
    p.t["obB"] = p.t["a_tB"][:].rearrange("q (s x) g -> q s (x g)", s=2)
    p.sb("xnbB", [128, NCH, GRP], BF16, es=es)
    p.sb("xnfB", [128, NCH, GRP], F32, es=es)
    p.sb("projfB", [128, 2, GRP], F32, es=es)
    p.sb("lgB", [4, GRP], F32, es=es)
    p.sb("rnormB", [1, GRP], F32, es=es)
    p.sb("rnorm2B", [1, GRP], F32, es=es)
    p.sb("lw_tmB", [128, NSUB * 4], F32, es=es)
    p.sb("ret_tmB", [128, NSUB * 4], F32, es=es)
    p.sb("ret_fmB", [4, GRP], F32, es=es)
    p.sb("tmpfB", [15, GRP], F32, es=es)
    p.sb("tmp2bB", [15, GRP], BF16, es=es)
    p.sb("statsB", [128, 40], F32, es=es)
    p.sb("tinyB", [128, 16], F32, es=es)
    p.sb("sq_scrB", [128, 1, 512], F32, es=es)
    p.sb("state0B", [128, NCH], F32, es=es)


def emit_ab_group(p, g, prev):
    blk, t = p.blk, p.t
    r0, r1 = grows(g)
    sx = t["sx0"] if p.sfx == "" else t["sx1"]
    is_seq_start = (g % GPB == 0)

    @blk.sync
    def _(sync):
        p.wait(sync, prev.get("slot_free"))
        dma(p, sync, "qs", sx[:],
            p.d["x"].ap()[r0:r1, :].rearrange("(s q) d -> q s d", q=128))
    tok_x = {"qs": p.val("qs")}
    w0 = dict(tok_x)
    w0.update(prev.get("xgrp", {}))
    yield

    ta_apply = emit_ln(p, sx, p.tt("tmw"), wait0=w0)
    yield
    dual_f32 = p.tt("xnf")
    if AB_STAGE < 0.5:
        prev["xgrp"] = {"dve": p.val("dve"), "act": p.val("act")}
        prev["slot_free"] = {"dve": p.val("dve")}
        prev["ob_store"] = prev.get("ob_store")
        return
    if g == 0 and "dev_dbg" in p.dev_taps and AB_STAGE != 0.55:
        @blk.gpsimd
        def _(gp):
            p.wait(gp, ta_apply, only=("act",))
            dma(p, gp, "qg", p.d["dbg_xn"].ap(), p.tt("tmw")[:])
            dma(p, gp, "qg", p.d["dbg_sx"].ap(), sx[:])
            dma(p, gp, "qg", p.d["dbg_st"].ap(), t["stats"][:])
    tok_xnb = emit_tr_fm(p, p.tt("tmw"), p.tt("xnb"), ta_apply,
                         evac_eng=TR_EVAC,
                         dual_f32=p.tt("xnf") if TR_DUAL else None)
    yield
    if AB_STAGE < 0.8:
        prev["xgrp"] = {"dve": p.val("dve"), "act": p.val("act")}
        prev["slot_free"] = {"dve": p.val("dve")}
        return

    gate_list = [("w_in_b", "bin_t", "gx"), ("w_a_f", "ba_t", "ga")]
    if AB_STAGE < 0.9:
        gate_list = gate_list[:1]
    for wi, (wt, bt, dst) in enumerate(gate_list):
        for h in range(8 if wi == 0 else AB_GA_N):
            bn = f"pb{2 + 2 * wi + h % 2}"
            bank = t[bn]

            if wi == 0 or AB_GA_MM:
                @blk.tensor
                def _(te, h=h, wt=wt, bank=bank, bn=bn):
                    p.wait(te, tok_xnb, only=("dve", "act"))
                    p.wait(te, prev.get("wtok"), only=("qg",))
                    p.wait(te, p.bank_tok.get(bn))
                    use_f = (wt != "w_in_b") and AB_F32GA
                    p.mmdt(te, "f" if use_f else "b")
                    rhs = p.tt("xnf") if use_f else p.tt("xnb")
                    lhs = t[wt] if use_f or wt == "w_in_b" else t["w_in_b"]
                    if wt == "w_a_f" and not AB_F32GA:
                        lhs = t["w_in_b"]
                    p.inc(te.matmul(bank[:, 0:GRP], lhs[:, h, :], rhs[:, h, :],
                                    start=True, stop=True), "pe")
            tpe = p.snap()

            if wi == 0 or AB_GA_EVAC:
                @blk.scalar
                def _(a, h=h, bt=bt, dst=dst, bank=bank, bn=bn):
                    p.wait(a, tpe, only=("pe",))
                    p.inc(a.activation(p.tt(dst)[:, h, :], bank[:, 0:GRP], AF.Sigmoid,
                                       bias=t[bt][:, h:h + 1]), "act")
            p.bank_tok[bn] = p.snap()
    tok_gates = p.snap()
    yield
    if AB_STAGE < 2:
        prev["xgrp"] = {"dve": p.val("dve"), "act": p.val("act")}
        prev["slot_free"] = {"dve": p.val("dve")}
        return

    @blk.vector
    def _(v):
        p.wait(v, tok_gates, only=("act",))
        msp = t["msp8"][:].unsqueeze(2).broadcast_to([128, 8, GRP])
        p.inc(v.tensor_tensor(p.tt("mlt")[:], p.tt("ga")[:], msp, ALU.mult), "dve")
    tv = p.snap()

    @blk.scalar
    def _(a):
        p.wait(a, tv, only=("dve",))
        p.wait(a, prev.get("ob_store"), only=("qg",))
        a.activation(p.tt("a_t")[:], p.tt("mlt")[:], AF.Exp)
        a.activation(p.tt("ga")[:], p.tt("mlt")[:], AF.Exp, scale=2.0)
        p.inc(a.activation(p.tt("mlt")[:], p.tt("ga")[:], AF.Sqrt, scale=-1.0,
                           bias=t["fconst"][:, 1:2]), "act")
    ta = p.snap()

    @blk.vector
    def _(v):
        p.wait(v, ta, only=("act",))
        v.tensor_mul(p.tt("u_t")[:], p.tt("xnf")[:], p.tt("gx")[:])
        p.inc(v.tensor_mul(p.tt("u_t")[:], p.tt("u_t")[:], p.tt("mlt")[:]), "dve")
        if is_seq_start:
            v.tensor_mul(p.tt("u_t")[:, :, 0:1], p.tt("xnf")[:, :, 0:1], p.tt("gx")[:, :, 0:1])
            p.inc(v.memset(p.tt("state0")[:], 0.0), "dve")
        v.wait_ge(p.S["dve"], p.C["dve"])
        last = None
        for ch in range(NCH):
            last = v.tensor_tensor_scan(p.tt("h_t")[:, ch, :], p.tt("a_t")[:, ch, :],
                                        p.tt("u_t")[:, ch, :], p.tt("state0")[:, ch:ch + 1],
                                        ALU.mult, ALU.add)
        p.inc(last, "dve")
        v.wait_ge(p.S["dve"], p.C["dve"])
        for ch in range(NCH):
            last = v.tensor_copy(p.tt("state0")[:, ch:ch + 1], p.tt("h_t")[:, ch, GRP - 1:GRP])
        p.inc(last, "dve")
    tok_scan = p.snap()
    yield
    if g == 0 and "dev_dbg" in p.dev_taps:
        @blk.gpsimd
        def _(gp):
            p.wait(gp, tok_scan, only=("dve",))
            for nm in ["gx", "ga", "at", "ut", "ht"]:
                key = {"at": "a_t", "ut": "u_t", "ht": "h_t"}.get(nm, nm)
                dma(p, gp, "qg", p.d["dbg_" + nm].ap(), p.tt(key)[:])

    fills = [(s, cq) for s in range(NSUB) for cq in range(2)]
    for fi, (s, cq) in enumerate(fills):
        @blk.tensor
        def _(te, s=s, cq=cq):
            p.wait(te, tok_scan, only=("dve",))
            p.mmdt(te, "f")
            for j in range(4):
                bn = f"pb{j}"
                p.wait(te, p.bank_tok.get(bn))
                ch = cq * 4 + j
                ii = te.transpose(t[bn][:, 0:128],
                                  p.tt("h_t")[:, ch, s * 128:(s + 1) * 128], t["ident"][:])
            p.inc(ii, "pe")
        tpe = p.snap()

        @blk.scalar
        def _(a, s=s, cq=cq):
            p.wait(a, tpe, only=("pe",))
            for j in range(4):
                ii = a.copy(p.tt("tmw")[:, s, cq * 512 + j * 128:cq * 512 + (j + 1) * 128],
                            t[f"pb{j}"][:, 0:128])
            p.inc(ii, "act")
        tokc = p.snap()
        for j in range(4):
            p.bank_tok[f"pb{j}"] = tokc
    tok_y = p.snap()
    yield

    if g == 0 and "dev_dbg" in p.dev_taps:
        @blk.gpsimd
        def _(gp):
            p.wait(gp, tok_y, only=("act",))
            dma(p, gp, "qg", p.d["dbg_y"].ap(), p.tt("tmw")[:])

    ta = emit_ln(p, p.tt("tmw"), p.tt("ob"), wait0={"act": tok_y["act"]},
                 extra_apply_wait=prev.get("ob_store"))

    @blk.vector
    def _(v):
        p.wait(v, ta, only=("act",))
        p.inc(v.tensor_add(sx[:], p.tt("ob")[:], sx[:]), "dve")
    tok_x1 = p.snap()
    yield

    if "dev_x1" in p.dev_taps:
        @blk.gpsimd
        def _(gp):
            p.wait(gp, tok_x1, only=("dve",))
            dma(p, gp, "qg",
                p.d["dev_x1"].ap()[r0:r1, :].rearrange("(s q) d -> q s d", q=128),
                sx[:])

    if AB_STAGE < 3:
        prev["xgrp"] = {"dve": p.val("dve"), "act": p.val("act")}
        prev["slot_free"] = {"dve": p.val("dve")}
        return

    ta_apply2 = emit_ln(p, sx, p.tt("tmw"), wait0={"dve": tok_x1["dve"]})
    yield
    tok_xnb2 = emit_tr_fm(p, p.tt("tmw"), p.tt("xnb"), ta_apply2, evac_eng="dve")
    yield

    for oc in range(2):
        bn = f"pb{2 + oc}"
        bank = t[bn]

        @blk.tensor
        def _(te, oc=oc, bank=bank, bn=bn):
            p.wait(te, tok_xnb2, only=("dve",))
            p.wait(te, p.bank_tok.get(bn))
            p.mmdt(te, "b")
            for kc in range(NCH):
                ii = te.matmul(bank[:, 0:GRP],
                               t["WpT_b"][:, kc, oc * 128:(oc + 1) * 128],
                               p.tt("xnb")[:, kc, :], start=(kc == 0), stop=(kc == NCH - 1))
            p.inc(ii, "pe")
        tpe = p.snap()

        @blk.scalar
        def _(a, oc=oc, bank=bank, bn=bn):
            p.wait(a, tpe, only=("pe",))
            p.inc(a.activation(p.tt("projf")[:, oc, :], bank[:, 0:GRP], AF.Identity,
                               bias=t["bp_t"][:, oc:oc + 1]), "act")
        p.bank_tok[bn] = p.snap()
    tok_proj = p.snap()
    yield

    @blk.scalar
    def _(a):
        p.wait(a, tok_proj, only=("act",))
        p.inc(a.activation(p.tt("sq_scr")[:, 0, 0:2 * GRP]
                           .rearrange("q (c n) -> q c n", c=2),
                           p.tt("projf")[:], AF.Square), "act")
    ta_sq = p.snap()

    @blk.tensor
    def _(te):
        p.wait(te, ta_sq, only=("act",))
        p.wait(te, p.bank_tok.get("pb4"))
        p.mmdt(te, "f")
        bank = t["pb4"]
        for c in range(2):
            ii = te.matmul(bank[0:1, 0:GRP], t["ones_col"][:, 0:1],
                           p.tt("sq_scr")[:, 0, c * GRP:(c + 1) * GRP],
                           start=(c == 0), stop=(c == 1))
        p.inc(ii, "pe")
        te.wait_ge(p.S["pe"], p.C["pe"])
        for c in range(2):
            ii = te.matmul(bank[0:4, 256:256 + GRP], t["simc_t"][:, c, :],
                           p.tt("projf")[:, c, :], start=(c == 0), stop=(c == 1))
        p.inc(ii, "pe")
    tok_nrm = p.snap()

    @blk.scalar
    def _(a):
        p.wait(a, tok_nrm, only=("pe",))
        p.inc(a.activation(p.tt("rnorm")[:], t["pb4"][0:1, 0:GRP], AF.Sqrt), "act")
    ta = p.snap()

    @blk.vector
    def _(v):
        p.wait(v, ta, only=("act",))
        p.inc(v.tensor_scalar_max(p.tt("rnorm2")[:], p.tt("rnorm")[:], 1e-12), "dve")
        v.wait_ge(p.S["dve"], p.C["dve"])
        p.inc(v.reciprocal(p.tt("rnorm")[:], p.tt("rnorm2")[:]), "dve")
        v.wait_ge(p.S["dve"], p.C["dve"])
        p.inc(v.tensor_scalar(p.tt("lg")[:], t["pb4"][0:4, 256:256 + GRP],
                              t["rstd_es"][:], None, ALU.mult), "dve")
    tok_lg = p.snap()

    @blk.tensor
    def _(te):
        p.wait(te, tok_lg, only=("dve",))
        p.mmdt(te, "f")
        p.inc(te.matmul(t["pb4"][0:4, 0:GRP], t["ones_row"][0:1, 0:4],
                        p.tt("rnorm")[:], start=True, stop=True), "pe")
    tpe = p.snap()

    @blk.vector
    def _(v):
        p.wait(v, tpe, only=("pe",))
        p.inc(v.tensor_mul(p.tt("lg")[:], p.tt("lg")[:], t["pb4"][0:4, 0:GRP]), "dve")
    p.bank_tok["pb4"] = p.snap()
    tok_lg2 = p.snap()
    yield

    @blk.tensor
    def _(te):
        p.wait(te, tok_lg2, only=("dve",))
        p.mmdt(te, "f")
        for s in range(NSUB):
            bn = f"pb{s}"
            p.wait(te, p.bank_tok.get(bn))
            ii = te.transpose(t[bn][:, 0:4],
                              p.tt("lg")[0:4, s * 128:(s + 1) * 128], t["ident"][0:4, 0:4])
        p.inc(ii, "pe")
    tpe = p.snap()

    @blk.vector
    def _(v):
        p.wait(v, tpe, only=("pe",))
        for s in range(NSUB):
            ii = v.tensor_copy(p.tt("lw_tm")[:, s * 4:(s + 1) * 4], t[f"pb{s}"][:, 0:4])
        p.inc(ii, "dve")
    tokc = p.snap()
    for s in range(NSUB):
        p.bank_tok[f"pb{s}"] = tokc

    @blk.vector
    def _(v):
        v.wait_ge(p.S["dve"], p.C["dve"])
        lw3 = p.tt("lw_tm")[:].rearrange("q (s e) -> q s e", e=4)
        st = p.tt("tiny")
        p.inc(v.reduce_max(st[:, 0:NSUB], lw3, axis=AX.X), "dve")
        v.wait_ge(p.S["dve"], p.C["dve"])
        p.inc(v.tensor_scalar_mul(st[:, 2:2 + NSUB], st[:, 0:NSUB], -1.0), "dve")
    tv = p.snap()

    @blk.scalar
    def _(a):
        p.wait(a, tv, only=("dve",))
        for s in range(NSUB):
            ii = a.activation(p.tt("lw_tm")[:, s * 4:(s + 1) * 4],
                              p.tt("lw_tm")[:, s * 4:(s + 1) * 4],
                              AF.Exp, bias=p.tt("tiny")[:, 2 + s:3 + s])
        p.inc(ii, "act")
    ta = p.snap()

    @blk.vector
    def _(v):
        p.wait(v, ta, only=("act",))
        lw3 = p.tt("lw_tm")[:].rearrange("q (s e) -> q s e", e=4)
        ret3 = p.tt("ret_tm")[:].rearrange("q (s e) -> q s e", e=4)
        st = p.tt("tiny")
        p.inc(v.reduce_sum(st[:, 4:4 + NSUB], lw3, axis=AX.X), "dve")
        v.wait_ge(p.S["dve"], p.C["dve"])
        p.inc(v.reciprocal(st[:, 6:6 + NSUB], st[:, 4:4 + NSUB]), "dve")
        v.wait_ge(p.S["dve"], p.C["dve"])
        rs = st[:, 6:6 + NSUB].unsqueeze(2).broadcast_to([128, NSUB, 4])
        p.inc(v.tensor_tensor(lw3, lw3, rs, ALU.mult), "dve")
        v.wait_ge(p.S["dve"], p.C["dve"])
        p.inc(v.reduce_max(st[:, 8:8 + NSUB], lw3, axis=AX.X), "dve")
        v.wait_ge(p.S["dve"], p.C["dve"])
        rm = st[:, 8:8 + NSUB].unsqueeze(2).broadcast_to([128, NSUB, 4])
        p.inc(v.tensor_tensor(ret3, lw3, rm, ALU.is_ge), "dve")
        v.wait_ge(p.S["dve"], p.C["dve"])
        p.inc(v.tensor_tensor(ret3, ret3, lw3, ALU.mult), "dve")
    tok_ret = p.snap()

    @blk.tensor
    def _(te):
        p.wait(te, tok_ret, only=("dve",))
        p.mmdt(te, "f")
        for s in range(NSUB):
            bn = f"pb{s}"
            p.wait(te, p.bank_tok.get(bn))
            ii = te.transpose(t[bn][0:4, 0:128],
                              p.tt("ret_tm")[:, s * 4:(s + 1) * 4], t["ident"][:])
        p.inc(ii, "pe")
    tpe = p.snap()

    @blk.vector
    def _(v):
        p.wait(v, tpe, only=("pe",))
        for s in range(NSUB):
            ii = v.tensor_copy(p.tt("ret_fm")[0:4, s * 128:(s + 1) * 128],
                               t[f"pb{s}"][0:4, 0:128])
        p.inc(ii, "dve")
    tokc = p.snap()
    for s in range(NSUB):
        p.bank_tok[f"pb{s}"] = tokc
    tok_retfm = p.snap()
    yield

    @blk.tensor
    def _(te):
        p.wait(te, tok_retfm, only=("dve",))
        p.wait(te, p.bank_tok.get("pb5"))
        p.mmdt(te, "f")
        p.inc(te.matmul(t["pb5"][0:15, 0:GRP], t["Mmap"][:], p.tt("ret_fm")[:],
                        start=True, stop=True), "pe")
        te.wait_ge(p.S["pe"], p.C["pe"])
        p.mmdt(te, "b")
        for kc in range(NCH):
            ii = te.matmul(t["pb5"][0:15, 256:256 + GRP], t["WdallT_b"][:, kc, :],
                           p.tt("xnb")[:, kc, :], start=(kc == 0), stop=(kc == NCH - 1))
        p.inc(ii, "pe")
    tok_exp = p.snap()

    @blk.scalar
    def _(a):
        p.wait(a, tok_exp, only=("pe",))
        p.inc(a.copy(p.tt("tmpf")[:], t["pb5"][0:15, 256:256 + GRP]), "act")
    ta = p.snap()

    @blk.vector
    def _(v):
        p.wait(v, ta, only=("act",))
        p.inc(v.tensor_mul(p.tt("tmp2b")[:], p.tt("tmpf")[:], t["pb5"][0:15, 0:GRP]), "dve")
    p.bank_tok["pb5"] = p.snap()
    tok_tmp2 = p.snap()
    yield

    if g == 0 and "dev_dbg" in p.dev_taps:
        @blk.gpsimd
        def _(gp):
            p.wait(gp, tok_tmp2, only=("dve",))
            dma(p, gp, "qg", p.d["dbg_projf"].ap(), p.tt("projf")[:])
            dma(p, gp, "qg", p.d["dbg_lg"].ap(), p.tt("lg")[:])
            dma(p, gp, "qg", p.d["dbg_ret"].ap(), p.tt("ret_tm")[:])
            dma(p, gp, "qg", p.d["dbg_tmpf"].ap(), p.tt("tmpf")[:])

    if AB_STAGE < 4:
        prev["xgrp"] = {"dve": p.val("dve"), "act": p.val("act")}
        prev["slot_free"] = {"dve": p.val("dve")}
        return

    for s in range(NSUB):
        for oh in range(2):
            i = s * 2 + oh
            bn = f"pb{2 + i % 2}"
            bank = t[bn]

            @blk.tensor
            def _(te, s=s, oh=oh, bank=bank, bn=bn):
                p.wait(te, tok_tmp2, only=("dve",))
                p.wait(te, p.bank_tok.get(bn))
                p.mmdt(te, "b")
                for kc in range(NCH):
                    te.matmul(bank[:], p.tt("xnb")[:, kc, s * 128:(s + 1) * 128],
                              t["WtT_b"][:, kc, oh * 512:(oh + 1) * 512],
                              start=(kc == 0), stop=False)
                p.inc(te.matmul(bank[:], p.tt("tmp2b")[0:15, s * 128:(s + 1) * 128],
                                t["WuallT_b"][0:15, oh * 512:(oh + 1) * 512],
                                start=False, stop=True), "pe")
            tpe = p.snap()

            @blk.vector
            def _(v, s=s, oh=oh, bank=bank, bn=bn):
                p.wait(v, tpe, only=("pe",))
                p.inc(v.tensor_copy(p.tt("tmw")[:, s, oh * 512:(oh + 1) * 512], bank[:]),
                      "dve")
            p.bank_tok[bn] = p.snap()
    tok_y2 = p.snap()
    yield

    if g == 0 and "dev_dbg" in p.dev_taps:
        @blk.gpsimd
        def _(gp):
            p.wait(gp, tok_y2, only=("dve",))
            dma(p, gp, "qg", p.d["dbg_y2"].ap(), p.tt("tmw")[:])

    ta = emit_ln(p, p.tt("tmw"), p.tt("ob"), wait0={"dve": tok_y2["dve"]},
                 extra_apply_wait=prev.get("ob_store"))

    @blk.vector
    def _(v):
        p.wait(v, ta, only=("act",))
        p.inc(v.tensor_add(p.tt("ob")[:], p.tt("ob")[:], sx[:]), "dve")
    tok_x2 = p.snap()
    prev["slot_free"] = {"dve": tok_x2["dve"]}

    @blk.gpsimd
    def _(gp):
        p.wait(gp, tok_x2, only=("dve",))
        dma(p, gp, "qg",
            p.d["x2"].ap()[r0:r1, :].rearrange("(s q) d -> q s d", q=128), p.tt("ob")[:])
    prev["ob_store"] = {"qg": p.val("qg")}
    prev["xgrp"] = {"dve": p.val("dve"), "act": p.val("act")}


def emit_ab(p, es, wtok):
    alloc_ab(p, es)
    prevs = [{"wtok": wtok}, {"wtok": wtok}]
    ngrp = min(NGRP, MAXG_AB)
    glists = [[g for g in range(ngrp) if g < GPB],
              [g for g in range(ngrp) if g >= GPB]]

    def stream(k):
        for g in glists[k]:
            yield from emit_ab_group(p, g, prevs[k])

    gens = [stream(0), stream(1)]
    sfxs = ["", "B"]
    done = [False, False]
    while not all(done):
        for k in (0, 1):
            if done[k]:
                continue
            p.sfx = sfxs[k]
            try:
                next(gens[k])
            except StopIteration:
                done[k] = True
    p.sfx = ""
    return p.snap()



def alloc_c(p, es):
    p.sb("sx0", [128, NSUB, 1024], F32, es=es)
    p.sb("tmw", [128, NSUB, 1024], F32, es=es)
    p.sb("ob", [128, NSUB, 1024], F32, es=es)
    p.sb("xnb", [128, NCH, GRP], BF16, es=es)
    p.sb("xnf", [128, NCH, GRP], F32, es=es)
    p.sb("fmb1", [128, NCH, GRP], BF16, es=es)
    p.sb("fmb2", [128, NCH, GRP], BF16, es=es)
    p.sb("sgf", [128, GRP], F32, es=es)
    p.sb("tmb1", [128, NSUB, 1024], BF16, es=es)
    p.sb("tmb2", [128, NSUB, 1024], BF16, es=es)
    p.sb("dd", [16, GRP], F32, es=es)
    p.sb("lnp", [16, GRP], F32, es=es)
    p.sb("t1d", [16, GRP], F32, es=es)
    p.sb("p_fm", [16, GRP], F32, es=es)
    p.sb("eq_fm", [16, GRP], F32, es=es)
    p.sb("ek_fm", [16, GRP], F32, es=es)
    p.sb("etf", [16, 8], F32, es=es)
    p.sb("eqtm", [128, 16], F32, es=es)
    p.sb("ektm", [128, 16], F32, es=es)
    p.sb("etrow", [1, 16], F32, es=es)
    p.sb("etr", [128, 8], F32, es=es)
    p.sb("prow", [1, 16, 128], F32, es=es)
    p.sb("nprow", [1, 16, 128], F32, es=es)
    p.sb("dt", [128, NH * 128], F32, es=es)
    p.sb("stb", [128, NH * 128], BF16, es=es)
    p.sb("o3f", [128, 8, 128], BF16, es=es)
    p.sb("kgt", [128, 8, 128], BF16, es=es)
    p.sb("sstate", [128, 512], F32, es=es)
    p.sb("ssb", [128, 512], BF16, es=es)


def emit_l1_weights(p, es):
    for nm in ["WqT_b", "WkT_b", "WvT_b", "WgT_b", "WogT_b", "WoT_b"]:
        p.sb(nm, [128, NCH, 1024], BF16, es=es)
    p.sb("WdT_f", [128, NCH, 16], F32, es=es)
    blk, t = p.blk, p.t

    @blk.gpsimd
    def _(g):
        for nm in ["Wq", "Wk", "Wv", "Wg", "Wog", "Wo"]:
            dma(p, g, "qg", t[nm + "T_b"][:],
                p.d[nm + "T"].ap().rearrange("(c q) o -> q c o", q=128))
        dma(p, g, "qg", t["WdT_f"][:],
            p.d["WdT"].ap().rearrange("(c q) o -> q c o", q=128))
    tok_w = p.snap()

    @blk.vector
    def _(v):
        p.wait(v, tok_w, only=("qg",))
        for kc in range(NCH):
            ii = v.tensor_scalar_mul(t["WoT_b"][:, kc, :], t["WoT_b"][:, kc, :],
                                     t["normw_t"][:, kc:kc + 1])
        p.inc(ii, "dve")
    return p.snap()


def emit_c_group(p, g, prev):
    blk, t = p.blk, p.t
    r0, r1 = grows(g)
    sx = t["sx0"]
    is_seq_start = (g % GPB == 0)
    o_s = t["xnf"][:].rearrange("q a b -> q (a b)")[:, 0:NH * DH]
    o3b = t["stb"][:, 0:NH * DH]
    vekb = t["stb"][:, NH * DH:2 * NH * DH]
    stmp = t["sq_scr"][:, 1, 0:512]

    if g > 0:
        pass

    @blk.sync
    def _(sync):
        p.wait(sync, prev.get("slot_free"))
        dma(p, sync, "qs", sx[:],
            p.d["x2"].ap()[r0:r1, :].rearrange("(s q) d -> q s d", q=128))
        prev["load_tok"][g] = {"qs": p.val("qs")}

    tok_x = dict(prev["load_tok"][g])
    tok_x.update(prev.get("xgrp", {}))

    STAGE = C_STAGE
    ta_apply = emit_ln(p, sx, t["tmw"], wait0=tok_x)
    tok_xnb = emit_tr_fm(p, t["tmw"], t["xnb"], ta_apply,
                         evac_eng=TR_EVAC,
                         dual_f32=t["xnf"] if TR_DUAL else None)
    if STAGE < 1:
        prev["xgrp"] = {"dve": p.val("dve"), "act": p.val("act"), "qg": p.val("qg")}
        prev["slot_free"] = {"dve": p.val("dve")}
        return

    for hc in range(NCH):
        bn = f"pb{hc % 2}"
        bank = t[bn]

        @blk.tensor
        def _(te, hc=hc, bank=bank, bn=bn):
            p.wait(te, tok_xnb, only=("dve",))
            p.wait(te, prev.get("wtok"))
            p.wait(te, p.bank_tok.get(bn))
            p.mmdt(te, "b")
            for kc in range(NCH):
                ii = te.matmul(bank[:, 0:GRP],
                               t["WqT_b"][:, kc, hc * 128:(hc + 1) * 128],
                               t["xnb"][:, kc, :], start=(kc == 0), stop=(kc == NCH - 1))
            p.inc(ii, "pe")
        tpe = p.snap()

        @blk.vector
        def _(v, hc=hc, bank=bank, bn=bn):
            p.wait(v, tpe, only=("pe",))
            p.inc(v.tensor_copy(t["fmb1"][:, hc, :], bank[:, 0:GRP]), "dve")
        p.bank_tok[bn] = p.snap()

    for hc in range(NCH):
        bnk, bng = "pb2", "pb3"

        @blk.tensor
        def _(te, hc=hc):
            p.wait(te, tok_xnb, only=("dve",))
            p.wait(te, p.bank_tok.get(bnk))
            p.wait(te, p.bank_tok.get(bng))
            p.mmdt(te, "b")
            for kc in range(NCH):
                te.matmul(t[bnk][:, 0:GRP], t["WkT_b"][:, kc, hc * 128:(hc + 1) * 128],
                          t["xnb"][:, kc, :], start=(kc == 0), stop=(kc == NCH - 1))
            for kc in range(NCH):
                ii = te.matmul(t[bng][:, 0:GRP],
                               t["WgT_b"][:, kc, hc * 128:(hc + 1) * 128],
                               t["xnb"][:, kc, :], start=(kc == 0), stop=(kc == NCH - 1))
            p.inc(ii, "pe")
        tpe = p.snap()

        @blk.scalar
        def _(a, hc=hc):
            p.wait(a, tpe, only=("pe",))
            p.wait(a, prev.get("sg_free"))
            p.inc(a.activation(t["sgf"][:], t[bng][:, 0:GRP], AF.Sigmoid), "act")
        ta = p.snap()

        @blk.vector
        def _(v, hc=hc):
            p.wait(v, ta, only=("act",))
            p.inc(v.tensor_mul(t["fmb2"][:, hc, :], t[bnk][:, 0:GRP],
                               t["sgf"][:]), "dve")
        p.bank_tok[bnk] = p.snap()
        p.bank_tok[bng] = p.snap()
        prev["sg_free"] = {"dve": p.val("dve")}

    for dst, wname, sig in [("tmb1", "WvT_b", False), ("tmb2", "WogT_b", True)]:
        for c in range(NSUB):
            for half in range(2):
                i = c * 2 + half
                bn = f"pb{4 + i % 2}"
                bank = t[bn]

                @blk.tensor
                def _(te, c=c, half=half, bank=bank, bn=bn, wname=wname):
                    p.wait(te, tok_xnb, only=("dve",))
                    p.wait(te, p.bank_tok.get(bn))
                    p.mmdt(te, "b")
                    for kc in range(NCH):
                        ii = te.matmul(bank[:],
                                       t["xnb"][:, kc, c * 128:(c + 1) * 128],
                                       t[wname][:, kc, half * 512:(half + 1) * 512],
                                       start=(kc == 0), stop=(kc == NCH - 1))
                    p.inc(ii, "pe")
                tpe = p.snap()

                if sig:
                    @blk.scalar
                    def _(a, c=c, half=half, bank=bank, dst=dst):
                        p.wait(a, tpe, only=("pe",))
                        p.inc(a.activation(t[dst][:, c, half * 512:(half + 1) * 512],
                                           bank[:], AF.Sigmoid), "act")
                else:
                    @blk.vector
                    def _(v, c=c, half=half, bank=bank, dst=dst):
                        p.wait(v, tpe, only=("pe",))
                        p.inc(v.tensor_copy(t[dst][:, c, half * 512:(half + 1) * 512],
                                            bank[:]), "dve")
                p.bank_tok[bn] = p.snap()
    tok_proj = p.snap()

    if STAGE < 2:
        prev["xgrp"] = {"dve": p.val("dve"), "act": p.val("act"), "qg": p.val("qg")}
        prev["slot_free"] = {"dve": p.val("dve")}
        return

    @blk.tensor
    def _(te):
        p.wait(te, tok_xnb)
        p.wait(te, p.bank_tok.get("pb6"))
        p.mmdt(te, "f")
        for kc in range(NCH):
            ii = te.matmul(t["pb6"][0:16, 0:GRP], t["WdT_f"][:, kc, :],
                           t["xnf"][:, kc, :], start=(kc == 0), stop=(kc == NCH - 1))
        p.inc(ii, "pe")
    tpe = p.snap()

    @blk.scalar
    def _(a):
        p.wait(a, tpe, only=("pe",))
        p.inc(a.activation(t["dd"][:], t["pb6"][0:16, 0:GRP], AF.Identity,
                           bias=t["bd_c"][:]), "act")
        a.wait_ge(p.S["act"], p.C["act"])
        p.inc(a.activation(t["lnp"][:], t["dd"][:], AF.Abs), "act")
        a.wait_ge(p.S["act"], p.C["act"])
        p.inc(a.activation(t["lnp"][:], t["lnp"][:], AF.Exp, scale=-1.0), "act")
        a.wait_ge(p.S["act"], p.C["act"])
        p.inc(a.activation(t["lnp"][:], t["lnp"][:], AF.Ln,
                           bias=t["fconst"][0:16, 1:2]), "act")
    ta_d = p.snap()
    p.bank_tok["pb6"] = ta_d

    @blk.vector
    def _(v):
        p.wait(v, ta_d, only=("act",))
        p.inc(v.tensor_scalar_max(t["t1d"][:], t["dd"][:], 0.0), "dve")
        v.wait_ge(p.S["dve"], p.C["dve"])
        p.inc(v.scalar_tensor_tensor(t["dd"][:], t["t1d"][:], -1.0, t["lnp"][:],
                                     ALU.mult, ALU.subtract), "dve")
        v.wait_ge(p.S["dve"], p.C["dve"])
        for c in range(NCHK):
            ii = v.tensor_tensor_scan(t["p_fm"][:, c * 128:(c + 1) * 128],
                                      t["ones16"][:], t["dd"][:, c * 128:(c + 1) * 128],
                                      0.0, ALU.mult, ALU.add)
        p.inc(ii, "dve")
        v.wait_ge(p.S["dve"], p.C["dve"])
        for c in range(NCHK):
            v.tensor_copy(t["etf"][:, c:c + 1], t["p_fm"][:, (c + 1) * 128 - 1:(c + 1) * 128])
        p.inc(v.tensor_scalar_mul(t["lnp"][:], t["p_fm"][:], -1.0), "dve")
        v.wait_ge(p.S["dve"], p.C["dve"])
        for c in range(NCHK):
            ii = v.tensor_scalar(t["t1d"][:, c * 128:(c + 1) * 128],
                                 t["p_fm"][:, c * 128:(c + 1) * 128],
                                 t["etf"][:, c:c + 1], None, ALU.subtract)
        p.inc(ii, "dve")
    tv_d = p.snap()

    @blk.scalar
    def _(a):
        p.wait(a, tv_d, only=("dve",))
        a.activation(t["ek_fm"][:], t["t1d"][:], AF.Exp, scale=-1.0)
        a.activation(t["eq_fm"][:], t["p_fm"][:], AF.Exp)
        p.inc(a.activation(t["etf"][:, 2:2 + NCHK], t["etf"][:, 0:NCHK], AF.Exp), "act")
    ta_e = p.snap()

    @blk.gpsimd
    def _(gp):
        p.wait(gp, tv_d, only=("dve",))
        p.wait(gp, prev.get("bounce_free"))
        dma(p, gp, "qg", p.d["bounce"].ap()[:, 0:GRP], t["p_fm"][:])
        dma(p, gp, "qg", p.d["bounce"].ap()[:, GRP:2 * GRP], t["lnp"][:])
    tk = p.snap()

    prev["bounce_free"] = tk

    if is_seq_start:
        @blk.vector
        def _(v):
            p.wait(v, prev.get("state_users"))
            v.memset(t["sstate"][:], 0.0)
            p.inc(v.memset(t["ssb"][:], 0.0), "dve")
    tok_sinit = p.snap()

    if STAGE < 3:
        prev["xgrp"] = {"dve": p.val("dve"), "act": p.val("act"), "qg": p.val("qg")}
        prev["slot_free"] = {"dve": p.val("dve")}
        return

    for c in range(NCHK):
        @blk.gpsimd
        def _(gp, c=c):
            p.wait(gp, tk, only=("qg",))
            p.wait(gp, prev.get("prow_free"))
            bap = p.d["bounce"].ap()
            dma(p, gp, "qg", t["prow"][:],
                bap[:, c * 128:(c + 1) * 128].unsqueeze(0))
            dma(p, gp, "qg", t["nprow"][:],
                bap[:, GRP + c * 128:GRP + (c + 1) * 128].unsqueeze(0))
        tok_prow = p.snap()

        @blk.tensor
        def _(te, c=c):
            p.wait(te, ta_e, only=("act",))
            p.wait(te, p.bank_tok.get("pb6"))
            p.wait(te, p.bank_tok.get("pb4"))
            p.wait(te, p.bank_tok.get("pb5"))
            p.mmdt(te, "f")
            te.transpose(t["pb6"][:, 0:16], t["eq_fm"][:, c * 128:(c + 1) * 128],
                         t["ident"][0:16, 0:16])
            te.transpose(t["pb4"][:, 0:16], t["ek_fm"][:, c * 128:(c + 1) * 128],
                         t["ident"][0:16, 0:16])
            p.inc(te.transpose(t["pb5"][0:1, 0:16], t["etf"][:, 2 + c:3 + c],
                               t["ident"][0:16, 0:16]), "pe")
        tpe = p.snap()

        @blk.vector
        def _(v, c=c):
            p.wait(v, tpe, only=("pe",))
            v.tensor_copy(t["eqtm"][:], t["pb6"][:, 0:16])
            v.tensor_copy(t["ektm"][:], t["pb4"][:, 0:16])
            p.inc(v.tensor_copy(t["etrow"][:], t["pb5"][0:1, 0:16]), "dve")
        tv_t = p.snap()
        p.bank_tok["pb4"] = tv_t
        p.bank_tok["pb5"] = tv_t

        @blk.tensor
        def _(te, c=c):
            p.wait(te, tv_t, only=("dve",))
            p.mmdt(te, "f")
            erow2 = t["etrow"][:].rearrange("o (h two) -> o two h", two=2)
            te.matmul(t["pb6"][:, 48:56], t["mtop"][:], erow2[0:1, 0, :],
                      start=True, stop=False)
            p.inc(te.matmul(t["pb6"][:, 48:56], t["mbot"][:], erow2[0:1, 1, :],
                            start=False, stop=True, skip_group_check=True), "pe")
        tpe = p.snap()

        @blk.vector
        def _(v, c=c):
            p.wait(v, tpe, only=("pe",))
            p.inc(v.tensor_copy(t["etr"][:], t["pb6"][:, 48:56]), "dve")
        p.bank_tok["pb6"] = p.snap()
        tok_etr = p.snap()

        @blk.tensor
        def _(te, c=c):
            p.wait(te, tok_prow, only=("qg",))
            p.mmdt(te, "f")
            for q in range(4):
                p.wait(te, p.bank_tok.get(f"pb{q}"))
            for q in range(4):
                te.matmul(t[f"pb{q}"][:], t["ones_row"][:],
                          t["prow"][0:1, 4 * q:4 * q + 4, :]
                          .rearrange("o a b -> o (a b)"),
                          start=True, stop=False)
            for h in range(NH):
                ii = te.matmul(t[f"pb{h // 4}"][:, (h % 4) * 128:(h % 4 + 1) * 128],
                               t["nprow"][0:1, h, :], t["ones_row"][:],
                               start=False, stop=True, skip_group_check=True)
            p.inc(ii, "pe")
        tok_B = p.snap()
        prev["prow_free"] = {"pe": tok_B["pe"]}

        @blk.vector
        def _(v, c=c):
            p.wait(v, tok_B, only=("pe",))
            lt = t["ltneg"][:].unsqueeze(1).broadcast_to([128, 4, 128])
            for q in range(4):
                ii = v.scalar_tensor_tensor(
                    t["dt"][:, q * 512:(q + 1) * 512]
                    .rearrange("j (a i) -> j a i", i=128),
                    t[f"pb{q}"][:].rearrange("j (a i) -> j a i", i=128),
                    0.0, lt, ALU.min, ALU.add)
            p.inc(ii, "dve")
        tok_dt1 = p.snap()

        @blk.scalar
        def _(a, c=c):
            p.wait(a, tok_dt1, only=("dve",))
            p.inc(a.activation(t["dt"][:], t["dt"][:], AF.Exp), "act")
        tok_dt = p.snap()

        if STAGE < 4:
            continue
        @blk.tensor
        def _(te, c=c):
            p.wait(te, tok_dt1, only=("dve",))
            p.wait(te, tok_proj)
            p.mmdt(te, "b")
            for r in range(4):
                if r > 0:
                    te.wait_ge(p.S["pe"], p.C["pe"])
                for q in range(4):
                    h = q * 4 + r
                    base = (h % 2) * 64
                    ii = te.matmul(
                        t[f"pb{q}"][:, r * 128:(r + 1) * 128],
                        t["fmb2"][base:base + 64, h // 2, c * 128:(c + 1) * 128],
                        t["fmb1"][base:base + 64, h // 2, c * 128:(c + 1) * 128],
                        start=True, stop=True)
                p.inc(ii, "pe")
        tok_sc = p.snap()

        if STAGE < 4.6:
            for q in range(4):
                p.bank_tok[f"pb{q}"] = tok_sc
            continue

        @blk.vector
        def _(v, c=c):
            p.wait(v, tok_sc, only=("pe",))
            p.wait(v, tok_dt, only=("act",))
            for q in range(4):
                st3 = t["stb"][:].rearrange("j (q2 r i) -> j q2 r i", q2=4, i=128)
                dt3 = t["dt"][:].rearrange("j (q2 r i) -> j q2 r i", q2=4, i=128)
                ii = v.tensor_mul(st3[:, q, :, :], t[f"pb{q}"][:]
                                  .rearrange("j (r i) -> j r i", i=128),
                                  dt3[:, q, :, :])
            p.inc(ii, "dve")
        tok_st = p.snap()
        for q in range(4):
            p.bank_tok[f"pb{q}"] = tok_st

        if STAGE < 5:
            continue
        @blk.tensor
        def _(te, c=c):
            p.wait(te, tok_st, only=("dve",))
            p.wait(te, tok_sinit, only=("dve",))
            p.wait(te, prev.get("state_ready"))
            p.mmdt(te, "b")
            for r in range(8):
                if r > 0:
                    te.wait_ge(p.S["pe"], p.C["pe"])
                for hb in range(2):
                    h = hb * 8 + r
                    base = (h % 2) * 64
                    te.matmul(t[f"pb{hb}"][:, r * 64:(r + 1) * 64],
                              t["stb"][:, h * 128:(h + 1) * 128],
                              t["tmb1"][:, c, h * 64:(h + 1) * 64],
                              start=True, stop=True)
                    ii = te.matmul(t[f"pb{2 + hb}"][:, r * 64:(r + 1) * 64],
                                   t["fmb1"][base:base + 64, h // 2,
                                             c * 128:(c + 1) * 128],
                                   t["ssb"][base:base + 64,
                                            (h // 2) * 64:(h // 2 + 1) * 64],
                                   start=True, stop=True)
                p.inc(ii, "pe")
        tok_omm = p.snap()

        @blk.vector
        def _(v, c=c):
            p.wait(v, tok_omm, only=("pe",))
            for q in range(2):
                eqv = t["eqtm"][:, q * 8:(q + 1) * 8].unsqueeze(2)                     .broadcast_to([128, 8, 64])
                v.tensor_tensor(o_s[:, q * 512:(q + 1) * 512].rearrange("i (a d) -> i a d", d=64),
                                t[f"pb{2 + q}"][:].rearrange("i (a d) -> i a d", d=64),
                                eqv, ALU.mult)
            p.inc(v.engine_nop(), "dve")
            v.wait_ge(p.S["dve"], p.C["dve"])
            for q in range(2):
                ii = v.tensor_add(o_s[:, q * 512:(q + 1) * 512],
                                  o_s[:, q * 512:(q + 1) * 512], t[f"pb{q}"][:])
            p.inc(ii, "dve")
        tok_o = p.snap()
        for q in range(4):
            p.bank_tok[f"pb{q}"] = tok_o

        if STAGE < 6:
            continue
        @blk.scalar
        def _(a, c=c):
            p.wait(a, tok_o, only=("dve",))
            p.inc(a.activation(t["sq_scr"][:, 0, 0:NH * DH], o_s, AF.Square,
                               accum_out=t["tiny"][:, 12:13]), "act")
        ta_r = p.snap()

        @blk.vector
        def _(v, c=c):
            p.wait(v, ta_r, only=("act",))
            p.inc(v.tensor_scalar_mul(t["tiny"][:, 13:14], t["tiny"][:, 12:13],
                                      1.0 / float(NH * DH)), "dve")
        tv_r = p.snap()

        @blk.scalar
        def _(a, c=c):
            p.wait(a, tv_r, only=("dve",))
            p.inc(a.activation(t["tiny"][:, 14:15], t["tiny"][:, 13:14], AF.Sqrt,
                               bias=t["fconst"][:, 5:6]), "act")
        ta_r2 = p.snap()

        @blk.vector
        def _(v, c=c):
            p.wait(v, ta_r2, only=("act",))
            p.inc(v.reciprocal(t["tiny"][:, 15:16], t["tiny"][:, 14:15]), "dve")
            v.wait_ge(p.S["dve"], p.C["dve"])
            p.inc(v.tensor_scalar(o_s, o_s, t["tiny"][:, 15:16],
                                  None, ALU.mult), "dve")
            v.wait_ge(p.S["dve"], p.C["dve"])
            p.inc(v.tensor_mul(o3b, o_s, t["tmb2"][:, c, :]), "dve")
        tok_o3 = p.snap()

        @blk.tensor
        def _(te, c=c):
            p.wait(te, tok_o3, only=("dve",))
            p.wait(te, p.bank_tok.get("pbt"))
            p.mmdt(te, "b")
            for j in range(8):
                if j > 0:
                    te.wait_ge(p.S["pe"], p.C["pe"])
                p.inc(te.transpose(t["pbt"][:, j * 128:(j + 1) * 128],
                                   o3b[:, j * 128:(j + 1) * 128], t["identb"][:]),
                      "pe")
        tpe = p.snap()

        @blk.vector
        def _(v, c=c):
            p.wait(v, tpe, only=("pe",))
            p.inc(v.tensor_copy(t["o3f"][:], t["pbt"][:]
                                .rearrange("q (j c2) -> q j c2", c2=128)), "dve")
        p.bank_tok["pbt"] = p.snap()
        tok_o3f = p.snap()

        for dh in range(2):
            bn = f"pb{dh % 2}"
            bank = t[bn]

            @blk.tensor
            def _(te, c=c, dh=dh, bank=bank, bn=bn):
                p.wait(te, tok_o3f, only=("dve",))
                p.wait(te, p.bank_tok.get(bn))
                p.mmdt(te, "b")
                for kc in range(NCH):
                    ii = te.matmul(bank[:], t["o3f"][:, kc, :],
                                   t["WoT_b"][:, kc, dh * 512:(dh + 1) * 512],
                                   start=(kc == 0), stop=(kc == NCH - 1))
                p.inc(ii, "pe")
            tpe = p.snap()

            @blk.scalar
            def _(a, c=c, dh=dh, bank=bank, bn=bn):
                p.wait(a, tpe, only=("pe",))
                p.inc(a.copy(t["tmw"][:, c, dh * 512:(dh + 1) * 512], bank[:]), "act")
            p.bank_tok[bn] = p.snap()

        if STAGE < 7:
            continue
        @blk.vector
        def _(v, c=c):
            p.wait(v, tok_etr, only=("dve",))
            p.wait(v, tok_omm, only=("pe",))
            ekv = t["ektm"][:].unsqueeze(2).broadcast_to([128, 16, 64])
            p.inc(v.tensor_tensor(vekb.rearrange("j (h d) -> j h d", d=64),
                                  t["tmb1"][:, c, :].rearrange("j (h d) -> j h d", d=64),
                                  ekv, ALU.mult), "dve")
        tok_vek = p.snap()

        @blk.tensor
        def _(te, c=c):
            p.wait(te, tok_o3f, only=("dve",))
            p.wait(te, p.bank_tok.get("pbt"))
            p.mmdt(te, "b")
            for hc in range(8):
                if hc > 0:
                    te.wait_ge(p.S["pe"], p.C["pe"])
                p.inc(te.transpose(t["pbt"][:, hc * 128:(hc + 1) * 128],
                                   t["fmb2"][:, hc, c * 128:(c + 1) * 128],
                                   t["identb"][:]), "pe")
        tpe = p.snap()

        @blk.vector
        def _(v, c=c):
            p.wait(v, tpe, only=("pe",))
            p.inc(v.tensor_copy(t["kgt"][:], t["pbt"][:]
                                .rearrange("q (j c2) -> q j c2", c2=128)), "dve")
        p.bank_tok["pbt"] = p.snap()
        tok_kgt = p.snap()

        @blk.tensor
        def _(te, c=c):
            p.wait(te, tok_kgt, only=("dve",))
            p.wait(te, tok_vek, only=("dve",))
            p.wait(te, tok_o, only=("dve",))
            p.wait(te, p.bank_tok.get("pb4"))
            p.wait(te, p.bank_tok.get("pb5"))
            p.mmdt(te, "b")
            for r in range(4):
                if r == 2:
                    te.wait_ge(p.S["pe"], p.C["pe"])
                for two in range(2):
                    hc = r * 2 + two
                    bank = t[f"pb{2 + hc % 4}"]
                    col = (hc // 4) * 64
                    for par in range(2):
                        h = hc * 2 + par
                        base = par * 64
                        ii = te.matmul(bank[base:base + 64, col:col + 64],
                                       t["kgt"][:, hc, base:base + 64],
                                       vekb[:, h * 64:(h + 1) * 64],
                                       start=True, stop=True)
                p.inc(ii, "pe")
        tok_kv = p.snap()

        @blk.vector
        def _(v, c=c):
            p.wait(v, tok_kv, only=("pe",))
            etrv = t["etr"][:].unsqueeze(2).broadcast_to([128, 8, 64])
            p.inc(v.tensor_tensor(stmp.rearrange("q (h d) -> q h d", d=64),
                                  t["sstate"][:].rearrange("q (h d) -> q h d", d=64),
                                  etrv, ALU.mult), "dve")
            v.wait_ge(p.S["dve"], p.C["dve"])
            sv = t["sstate"][:].rearrange("q (h4 four d) -> q h4 four d", four=4, d=64)
            sm = stmp.rearrange("q (h4 four d) -> q h4 four d", four=4, d=64)
            for b in range(4):
                ii = v.tensor_add(sv[:, :, b, :], sm[:, :, b, :],
                                  t[f"pb{2 + b}"][:, 0:128]
                                  .rearrange("q (h4 d) -> q h4 d", d=64))
            p.inc(ii, "dve")
            v.wait_ge(p.S["dve"], p.C["dve"])
            p.inc(v.tensor_copy(t["ssb"][:], t["sstate"][:]), "dve")
        tok_state = p.snap()
        p.bank_tok["pb4"] = tok_state
        p.bank_tok["pb5"] = tok_state
        p.bank_tok["pb2"] = tok_state
        p.bank_tok["pb3"] = tok_state
        prev["state_ready"] = {"dve": tok_state["dve"]}
        tok_sinit = {"dve": 0}

    if STAGE < 8:
        prev["xgrp"] = {"dve": p.val("dve"), "act": p.val("act"), "qg": p.val("qg")}
        prev["slot_free"] = {"dve": p.val("dve")}
        return
    ta = emit_ln(p, t["tmw"], t["ob"], wait0={"act": p.val("act"), "dve": p.val("dve")},
                 extra_apply_wait=prev.get("ob_store"))

    @blk.vector
    def _(v):
        p.wait(v, ta, only=("act",))
        p.inc(v.tensor_add(t["ob"][:], t["ob"][:], sx[:]), "dve")
    tok_x3 = p.snap()
    prev["slot_free"] = {"dve": tok_x3["dve"]}
    prev["state_users"] = {"pe": p.val("pe")}

    @blk.gpsimd
    def _(gp):
        p.wait(gp, tok_x3, only=("dve",))
        dma(p, gp, "qg",
            p.d["x3"].ap()[r0:r1, :].rearrange("(s q) d -> q s d", q=128), t["ob"][:])
    prev["ob_store"] = {"qg": p.val("qg")}

    ta2 = emit_ln(p, t["ob"], t["tmw"], wait0={"dve": tok_x3["dve"]})

    @blk.gpsimd
    def _(gp):
        p.wait(gp, ta2, only=("act",))
        dma(p, gp, "qg",
            p.d["xn3"].ap()[r0:r1, :].rearrange("(s q) d -> q s d", q=128), t["tmw"][:])
    prev["ob_store"]["qg"] = p.val("qg")

    @blk.vector
    def _(v):
        p.wait(v, ta2, only=("act",))
        p.inc(v.reduce_max(t["tiny"][:, 11:12],
                           t["tmw"][:].rearrange("q s d -> q (s d)"), axis=AX.X,
                           apply_absolute_value=True), "dve")
        v.wait_ge(p.S["dve"], p.C["dve"])
        p.inc(v.tensor_max(t["gmax"][:, 0:1], t["gmax"][:, 0:1], t["tiny"][:, 11:12]),
              "dve")
    prev["xgrp"] = {"dve": p.val("dve"), "act": p.val("act"), "qg": p.val("qg")}


def emit_c(p, es, wtok):
    alloc_c(p, es)
    blk = p.blk
    prev = {"load_tok": {}, "wtok": wtok}

    @blk.sync
    def _(sync):
        r0, r1 = grows(0)
        dma(p, sync, "qs", p.t["sx0"][:],
            p.d["x2"].ap()[r0:r1, :].rearrange("(s q) d -> q s d", q=128))
        prev["load_tok"][0] = {"qs": p.val("qs")}

    maxg = MAXG_C
    for g in range(min(NGRP, maxg)):
        emit_c_group(p, g, prev)
    return p.snap()




GRPD = 512
NGRPD = NTOK // GRPD
NSUBD = GRPD // 128


def emit_dequant(p, es):
    blk, t = p.blk, p.t
    p.sb("dq_w", [128, 3072], F32, es=es)
    p.sb("dq_s", [128, 96], F32, es=es)
    p.sb("dq_is", [128, 96], F32, es=es)
    p.sb("dq_t", [128, 3072], F32, es=es)
    p.sb("dq_o", [128, 3072], BF16, es=es)
    jobs = []
    for nm, rows, cols in [("w_up", 3072, 1024), ("w_gate", 3072, 1024),
                           ("w_down", 1024, 3072)]:
        for r0 in range(0, rows, 128):
            jobs.append((nm, r0, cols))
    tprev = None
    for nm, r0, cols in jobs:
        nb = cols // 32

        @blk.sync
        def _(sync, nm=nm, r0=r0, cols=cols):
            p.wait(sync, tprev, only=("qg",))
            dma(p, sync, "qs", t["dq_w"][:, 0:cols], p.d[nm].ap()[r0:r0 + 128, :])
        tok_l = {"qs": p.val("qs")}

        @blk.vector
        def _(v, nm=nm, r0=r0, cols=cols, nb=nb):
            p.wait(v, tok_l)
            w3 = t["dq_w"][:, 0:cols].rearrange("q (n k) -> q n k", k=32)
            p.inc(v.reduce_max(t["dq_s"][:, 0:nb], w3, axis=AX.X,
                               apply_absolute_value=True), "dve")
            v.wait_ge(p.S["dve"], p.C["dve"])
            p.inc(v.tensor_scalar(t["dq_s"][:, 0:nb], t["dq_s"][:, 0:nb],
                                  1.0 / 7.0, 1e-12, ALU.mult, ALU.max), "dve")
            v.wait_ge(p.S["dve"], p.C["dve"])
            p.inc(v.reciprocal(t["dq_is"][:, 0:nb], t["dq_s"][:, 0:nb]), "dve")
            v.wait_ge(p.S["dve"], p.C["dve"])
            isv = t["dq_is"][:, 0:nb].unsqueeze(2).broadcast_to([128, nb, 32])
            p.inc(v.tensor_tensor(t["dq_t"][:, 0:cols].rearrange(
                "q (n k) -> q n k", k=32), w3, isv, ALU.mult), "dve")
        tok_v = p.snap()

        @blk.scalar
        def _(a, cols=cols):
            p.wait(a, tok_v, only=("dve",))
            p.inc(a.activation(t["dq_t"][:, 0:cols], t["dq_t"][:, 0:cols],
                               AF.Identity, bias=t["fconst"][:, 2:3]), "act")
            a.wait_ge(p.S["act"], p.C["act"])
            p.inc(a.activation(t["dq_t"][:, 0:cols], t["dq_t"][:, 0:cols],
                               AF.Identity, bias=t["fconst"][:, 3:4]), "act")
        tok_a = p.snap()

        @blk.vector
        def _(v, cols=cols, nb=nb):
            p.wait(v, tok_a, only=("act",))
            sv = t["dq_s"][:, 0:nb].unsqueeze(2).broadcast_to([128, nb, 32])
            p.inc(v.tensor_tensor(t["dq_o"][:, 0:cols].rearrange(
                "q (n k) -> q n k", k=32),
                t["dq_t"][:, 0:cols].rearrange("q (n k) -> q n k", k=32),
                sv, ALU.mult), "dve")
        tok_v2 = p.snap()

        @blk.gpsimd
        def _(gp, nm=nm, r0=r0, cols=cols):
            p.wait(gp, tok_v2, only=("dve",))
            dma(p, gp, "qg", p.d["wq_" + nm].ap()[r0:r0 + 128, :],
                t["dq_o"][:, 0:cols])
        tprev = {"qg": p.val("qg")}
    return p.snap()


def emit_cc(p, j, n_vals, gcol0, prev):
    blk, t = p.blk, p.t

    @blk.tensor
    def _(te):
        p.wait(te, prev["gmax_tok"], only=("dve",))
        p.wait(te, p.bank_tok.get("pb6"))
        p.mmdt(te, "f")
        p.inc(te.transpose(t["pb6"][0:n_vals, 0:128],
                           t["gmax"][:, gcol0:gcol0 + n_vals],
                           t["ident"][:]), "pe")
    tpe = p.snap()

    @blk.vector
    def _(v):
        p.wait(v, tpe, only=("pe",))
        p.inc(v.tensor_copy(t["glob2"][0:n_vals, 0:128], t["pb6"][0:n_vals, 0:128]),
              "dve")
        v.wait_ge(p.S["dve"], p.C["dve"])
        p.inc(v.reduce_max(t["glob"][0:n_vals, 0:1], t["glob2"][0:n_vals, 0:128],
                           axis=AX.X), "dve")
    p.bank_tok["pb6"] = p.snap()
    tv = p.snap()

    @blk.gpsimd
    def _(gp):
        p.wait(gp, tv, only=("dve",))
        dma(p, gp, "qg", p.d[f"ccin{j}"].ap(), t["glob"][:])
        gp.wait_ge(p.S["qg"], p.C["qg"])
        if DEV_NO_CC:
            dma(p, gp, "qg", p.d[f"ccout{j}"].ap(), p.d[f"ccin{j}"].ap())
            gp.wait_ge(p.S["qg"], p.C["qg"])
        else:
            p.inc(gp.collective_compute(
                "AllReduce", ALU.max, replica_groups=[list(range(N_CORES))],
                ins=[p.d[f"ccin{j}"].ap().opt()], outs=[p.d[f"ccout{j}"].ap().opt()]),
                "cc" + str(j), 1)
            gp.wait_ge(p.sem("cc" + str(j)), 1)
        dma(p, gp, "qg", t["gres"][:], p.d[f"ccout{j}"].ap())
    tok_g = {"qg": p.val("qg")}

    @blk.tensor
    def _(te):
        p.wait(te, tok_g)
        p.wait(te, p.bank_tok.get("pb6"))
        p.mmdt(te, "f")
        p.inc(te.transpose(t["pb6"][0:1, 56:56 + 16], t["gres"][0:16, 0:1],
                           t["ident"][0:16, 0:16]), "pe")
    tpe2 = p.snap()

    @blk.vector
    def _(v):
        p.wait(v, tpe2, only=("pe",))
        p.inc(v.tensor_copy(t["gr_row"][:], t["pb6"][0:1, 56:56 + 16]), "dve")
    p.bank_tok["pb6"] = p.snap()
    return p.snap()


def bcast_col(p, te_tok, src_cell, dst_col):
    blk, t = p.blk, p.t

    @blk.tensor
    def _(te):
        p.wait(te, te_tok, only=("dve",))
        p.wait(te, p.bank_tok.get("pb6"))
        p.mmdt(te, "f")
        p.inc(te.matmul(t["pb6"][:, 60:61], t["ones_row"][:], src_cell,
                        start=True, stop=True), "pe")
    tpe = p.snap()

    @blk.vector
    def _(v):
        p.wait(v, tpe, only=("pe",))
        p.inc(v.tensor_copy(dst_col, t["pb6"][:, 60:61]), "dve")
    p.bank_tok["pb6"] = p.snap()
    return p.snap()




def emit_d(p, es, prev_c_tok, dqtok=None):
    blk, t = p.blk, p.t
    p.sb("sx0", [128, NSUBD, 1024], F32, es=es)
    p.sb("tmw", [128, NSUBD, 1024], F32, es=es)
    p.sb("ob", [128, NSUBD, 1024], F32, es=es)
    p.sb("scl", [128, 8], F32, es=es)
    p.sb("glob2", [16, 128], F32, es=es)
    p.sb("gres", [16, 128], F32, es=es)
    p.sb("gr_row", [1, 16], F32, es=es)
    sx = t["sx0"]
    prev = {"gmax_tok": prev_c_tok, "dq_done": dqtok}

    if "C" not in p.phases:
        for g in range(NGRPD):
            @blk.sync
            def _(sync, g=g):
                p.wait(sync, prev.get("pre_done"))
                dma(p, sync, "qs", sx[:],
                    p.d["xn3"].ap()[g * GRPD:(g + 1) * GRPD, :]
                    .rearrange("(s q) d -> q s d", q=128))
            tok_l = {"qs": p.val("qs")}

            @blk.vector
            def _(v, g=g):
                p.wait(v, tok_l)
                p.inc(v.reduce_max(t["stats"][:, g:g + 1],
                                   sx[:].rearrange("q s d -> q (s d)"), axis=AX.X,
                                   apply_absolute_value=True), "dve")
            prev["pre_done"] = {"dve": p.val("dve")}
        @blk.vector
        def _(v):
            v.wait_ge(p.S["dve"], p.C["dve"])
            p.inc(v.reduce_max(t["gmax"][:, 0:1], t["stats"][:, 0:NGRPD], axis=AX.X),
                  "dve")
        if "dev_dbg" in p.dev_taps:
            @blk.gpsimd
            def _(gp):
                p.wait(gp, {"dve": p.val("dve")})
                dma(p, gp, "qg", p.d["dbg_pre"].ap(), t["stats"][:])
        prev["gmax_tok"] = {"dve": p.val("dve")}

    emit_cc(p, 1, 1, 0, prev)

    @blk.vector
    def _(v):
        p.wait(v, {"dve": p.val("dve")})
        p.inc(v.tensor_scalar(t["tiny"][0:1, 0:1], t["gr_row"][0:1, 0:1],
                              1.0 / QMAX, 1e-12, ALU.mult, ALU.max), "dve")
        v.wait_ge(p.S["dve"], p.C["dve"])
        p.inc(v.reciprocal(t["tiny"][0:1, 1:2], t["tiny"][0:1, 0:1]), "dve")
    tvc = p.snap()
    bcast_col(p, tvc, t["tiny"][0:1, 1:2], t["scl"][:, 0:1])
    tok_sc1 = bcast_col(p, tvc, t["tiny"][0:1, 0:1], t["scl"][:, 1:2])

    emit_barrier(p)
    with ExitStack() as es_w:
        p.sb("wupT_b", [128, NCH, DFF], BF16, es=es_w)
        p.sb("wgT_b", [128, NCH, DFF], BF16, es=es_w)
        p.sb("stage0", [128, 12, GRPD], BF16, es=es_w)
        p.sb("stage1", [128, 12, GRPD], BF16, es=es_w)
        p.sb("xqf", [128, NCH, GRPD], F32, es=es_w)
        p.sb("xqb", [128, NCH, GRPD], BF16, es=es_w)

        @blk.sync
        def _(sync):
            p.wait(sync, prev.get("dq_done"), only=("qg",))
            for kc in range(NCH):
                p.inc(sync.dma_start(out=t["wupT_b"][:, kc, :],
                                     in_=p.d["wq_w_up"].ap()[:, kc * 128:(kc + 1) * 128],
                                     transpose=True), "qs", 16)
                p.inc(sync.dma_start(out=t["wgT_b"][:, kc, :],
                                     in_=p.d["wq_w_gate"].ap()[:, kc * 128:(kc + 1) * 128],
                                     transpose=True), "qs", 16)
        tok_w = {"qs": p.val("qs")}

        if "dev_dbg" in p.dev_taps:
            @blk.gpsimd
            def _(gp):
                p.wait(gp, tok_w)
                dma(p, gp, "qg", p.d["dbg_wupT"].ap(), t["wupT_b"][:])

        for g in range(NGRPD):
            @blk.sync
            def _(sync, g=g):
                p.wait(sync, prev.get("d1_sx_free"))
                dma(p, sync, "qs", sx[:],
                    p.d["xn3"].ap()[g * GRPD:(g + 1) * GRPD, :]
                    .rearrange("(s q) d -> q s d", q=128))
            tok_l = {"qs": p.val("qs")}

            tok_tr = emit_tr_fm(p, sx, t["xqf"], tok_l, nsub=NSUBD)
            prev["d1_sx_free"] = {"pe": p.val("pe")}

            @blk.vector
            def _(v, g=g):
                p.wait(v, tok_tr, only=("dve",))
                p.wait(v, tok_sc1, only=("dve",))
                p.inc(v.tensor_scalar(t["xqf"][:], t["xqf"][:], t["scl"][:, 0:1],
                                      t["fconst"][:, 2:3], ALU.mult, ALU.add), "dve")
                v.wait_ge(p.S["dve"], p.C["dve"])
                p.inc(v.tensor_scalar(t["xqb"][:], t["xqf"][:], t["fconst"][:, 2:3],
                                      t["scl"][:, 1:2], ALU.subtract, ALU.mult), "dve")
            tok_xq = p.snap()
            if g == 0 and "dev_dbg" in p.dev_taps:
                @blk.gpsimd
                def _(gp):
                    p.wait(gp, tok_xq, only=("dve",))
                    dma(p, gp, "qg", p.d["dbg_xq"].ap(), t["xqb"][:])
                    dma(p, gp, "qg", p.d["dbg_scl"].ap(), t["scl"][:])
                    dma(p, gp, "qg", p.d["dbg_tiny"].ap(), t["tiny"][:])
                    dma(p, gp, "qg", p.d["dbg_grow"].ap(), t["gr_row"][:])
                    dma(p, gp, "qg", p.d["dbg_gmax"].ap(), t["gmax"][:])

            for wi, (wn, spn, gcol) in enumerate(
                    [("wupT_b", "up_sp", 1), ("wgT_b", "gate_sp", 2)]):
                for half in range(2):
                    hb = (2 * wi + half) % 2
                    stg = t[f"stage{hb}"]
                    for oc12 in range(12):
                        oc = half * 12 + oc12
                        bn = f"pb{2 + oc % 2}"
                        bank = t[bn]

                        @blk.tensor
                        def _(te, oc=oc, bank=bank, bn=bn, wn=wn):
                            p.wait(te, tok_xq, only=("dve",))
                            p.wait(te, tok_w)
                            p.wait(te, p.bank_tok.get(bn))
                            p.mmdt(te, "b")
                            for kc in range(NCH):
                                ii = te.matmul(bank[:],
                                               t[wn][:, kc, oc * 128:(oc + 1) * 128],
                                               t["xqb"][:, kc, :],
                                               start=(kc == 0), stop=(kc == NCH - 1))
                            p.inc(ii, "pe")
                        tpe = p.snap()

                        @blk.scalar
                        def _(a, oc12=oc12, bank=bank, bn=bn, hb=hb, stg=stg):
                            p.wait(a, tpe, only=("pe",))
                            p.wait(a, prev.get(f"stage_free{hb}"))
                            p.inc(a.copy(stg[:, oc12, :], bank[:]), "act")
                        p.bank_tok[bn] = p.snap()
                    tok_half = p.snap()

                    @blk.vector
                    def _(v, stg=stg):
                        p.wait(v, tok_half, only=("act",))
                        p.inc(v.reduce_max(t["tiny"][:, 2:3],
                                           stg[:].rearrange("q a b -> q (a b)"),
                                           axis=AX.X, apply_absolute_value=True),
                              "dve")
                        v.wait_ge(p.S["dve"], p.C["dve"])
                        p.inc(v.tensor_max(t["gmax"][:, gcol:gcol + 1],
                                           t["gmax"][:, gcol:gcol + 1],
                                           t["tiny"][:, 2:3]), "dve")
                    tok_red = p.snap()

                    @blk.gpsimd
                    def _(gp, g=g, spn=spn, half=half, stg=stg):
                        p.wait(gp, tok_red, only=("act", "dve"))
                        dma(p, gp, "qg",
                            p.d[spn].ap()[g, half * 1536:(half + 1) * 1536, :]
                            .rearrange("(a q) b -> q a b", q=128), stg[:])
                    prev[f"stage_free{hb}"] = {"qg": p.val("qg")}
        prev["gmax_tok"] = {"dve": p.val("dve")}

    emit_cc(p, 2, 2, 1, prev)

    @blk.vector
    def _(v):
        p.wait(v, {"dve": p.val("dve")})
        p.inc(v.tensor_scalar(t["tiny"][0:1, 0:1], t["gr_row"][0:1, 0:1],
                              1.0 / QMAX, 1e-12, ALU.mult, ALU.max), "dve")
        p.inc(v.tensor_scalar(t["tiny"][0:1, 2:3], t["gr_row"][0:1, 1:2],
                              1.0 / QMAX, 1e-12, ALU.mult, ALU.max), "dve")
        v.wait_ge(p.S["dve"], p.C["dve"])
        v.reciprocal(t["tiny"][0:1, 1:2], t["tiny"][0:1, 0:1])
        p.inc(v.reciprocal(t["tiny"][0:1, 3:4], t["tiny"][0:1, 2:3]), "dve")
    tvc2 = p.snap()
    bcast_col(p, tvc2, t["tiny"][0:1, 1:2], t["scl"][:, 2:3])
    bcast_col(p, tvc2, t["tiny"][0:1, 2:3], t["scl"][:, 3:4])
    tok_sc2 = bcast_col(p, tvc2, t["tiny"][0:1, 3:4], t["scl"][:, 4:5])

    emit_barrier(p)
    with ExitStack() as es_d2:
        for b in range(2):
            p.sb(f"gsp{b}", [128, 12, GRPD], BF16, es=es_d2)
            p.sb(f"g2b{b}", [128, 12, GRPD], BF16, es=es_d2)
        p.sb("rgf", [128, 12, GRPD], F32, es=es_d2)
        p.sb("sgv", [128, 12, GRPD], F32, es=es_d2)
        for g in range(NGRPD):
            for half in range(2):
                hb = half
                gsp = t[f"gsp{hb}"]
                g2b = t[f"g2b{hb}"]

                @blk.sync
                def _(sync, g=g, half=half, hb=hb, gsp=gsp):
                    p.wait(sync, prev.get(f"d2_buf_free{hb}"))
                    dma(p, sync, "qs", gsp[:],
                        p.d["gate_sp"].ap()[g, half * 1536:(half + 1) * 1536, :]
                        .rearrange("(a q) b -> q a b", q=128))
                tok_l = {"qs": p.val("qs")}

                @blk.vector
                def _(v, half=half, hb=hb, gsp=gsp):
                    p.wait(v, tok_l)
                    p.wait(v, tok_sc2, only=("dve",))
                    n = p.inc(v.tensor_scalar(t["rgf"][:], gsp[:],
                                              t["scl"][:, 4:5],
                                              t["fconst"][:, 2:3],
                                              ALU.mult, ALU.add), "dve")
                    prev[f"d2_buf_free{hb}"] = {"dve": n}
                    v.wait_ge(p.S["dve"], p.C["dve"])
                    p.inc(v.tensor_scalar(t["rgf"][:], t["rgf"][:],
                                          t["fconst"][:, 2:3], None, ALU.subtract),
                          "dve")
                tv1 = p.snap()

                @blk.scalar
                def _(a, half=half):
                    p.wait(a, tv1, only=("dve",))
                    p.inc(a.activation(t["sgv"][:], t["rgf"][:], AF.Sigmoid,
                                       scale=t["scl"][:, 3:4]), "act")
                ta1 = p.snap()

                @blk.vector
                def _(v, half=half, g=g, hb=hb, g2b=g2b):
                    p.wait(v, ta1, only=("act",))
                    p.wait(v, prev.get(f"d2_st_free{hb}"))
                    p.inc(v.tensor_scalar(t["sgv"][:], t["sgv"][:], 65536.0,
                                          t["fconst"][:, 2:3], ALU.mult, ALU.add),
                          "dve")
                    v.wait_ge(p.S["dve"], p.C["dve"])
                    p.inc(v.tensor_scalar(t["sgv"][:], t["sgv"][:],
                                          t["fconst"][:, 2:3], 65535.0,
                                          ALU.subtract, ALU.min), "dve")
                    v.wait_ge(p.S["dve"], p.C["dve"])
                    p.inc(v.scalar_tensor_tensor(g2b[:], t["sgv"][:],
                                                 1.0 / 65536.0, t["rgf"][:],
                                                 ALU.mult, ALU.mult), "dve")
                    v.wait_ge(p.S["dve"], p.C["dve"])
                    p.inc(v.reduce_max(t["tiny"][:, 2:3],
                                       g2b[:].rearrange("q a b -> q (a b)"),
                                       axis=AX.X, apply_absolute_value=True), "dve")
                    v.wait_ge(p.S["dve"], p.C["dve"])
                    p.inc(v.tensor_max(t["gmax"][:, 3:4], t["gmax"][:, 3:4],
                                       t["tiny"][:, 2:3]), "dve")
                tok_g2 = p.snap()

                @blk.gpsimd
                def _(gp, g=g, half=half, g2b=g2b):
                    p.wait(gp, tok_g2, only=("dve",))
                    dma(p, gp, "qg",
                        p.d["gate_sp"].ap()[g, half * 1536:(half + 1) * 1536, :]
                        .rearrange("(a q) b -> q a b", q=128), g2b[:])
                prev[f"d2_st_free{hb}"] = {"qg": p.val("qg")}
        prev["gmax_tok"] = {"dve": p.val("dve")}

    emit_cc(p, 3, 1, 3, prev)

    @blk.vector
    def _(v):
        p.wait(v, {"dve": p.val("dve")})
        p.inc(v.tensor_mul(t["tiny"][0:1, 4:5], t["gr_row"][0:1, 0:1],
                           t["tiny"][0:1, 2:3]), "dve")
        v.wait_ge(p.S["dve"], p.C["dve"])
        p.inc(v.tensor_scalar(t["tiny"][0:1, 4:5], t["tiny"][0:1, 4:5],
                              1.0 / QMAX, 1e-12, ALU.mult, ALU.max), "dve")
        v.wait_ge(p.S["dve"], p.C["dve"])
        p.inc(v.reciprocal(t["tiny"][0:1, 5:6], t["tiny"][0:1, 4:5]), "dve")
        v.wait_ge(p.S["dve"], p.C["dve"])
        p.inc(v.tensor_mul(t["tiny"][0:1, 5:6], t["tiny"][0:1, 5:6],
                           t["tiny"][0:1, 2:3]), "dve")
    tvc3 = p.snap()
    tok_sc3 = bcast_col(p, tvc3, t["tiny"][0:1, 5:6], t["scl"][:, 5:6])

    emit_barrier(p)
    with ExitStack() as es_d3:
        for b in range(2):
            p.sb(f"usp{b}", [128, 12, GRPD], BF16, es=es_d3)
            p.sb(f"gsp{b}", [128, 12, GRPD], BF16, es=es_d3)
            p.sb(f"ob16{b}", [128, 12, GRPD], BF16, es=es_d3)
        p.sb("ruf", [128, 12, GRPD], F32, es=es_d3)
        p.sb("rg2f", [128, 12, GRPD], F32, es=es_d3)
        for g in range(NGRPD):
            for half in range(2):
                hb = half
                usp, gsp, ob16 = t[f"usp{hb}"], t[f"gsp{hb}"], t[f"ob16{hb}"]

                @blk.sync
                def _(sync, g=g, half=half, hb=hb, usp=usp, gsp=gsp):
                    p.wait(sync, prev.get(f"d3_buf_free{hb}"))
                    sl = p.d["up_sp"].ap()[g, half * 1536:(half + 1) * 1536, :]
                    dma(p, sync, "qs", usp[:],
                        sl.rearrange("(a q) b -> q a b", q=128))
                    sl = p.d["gate_sp"].ap()[g, half * 1536:(half + 1) * 1536, :]
                    dma(p, sync, "qs", gsp[:],
                        sl.rearrange("(a q) b -> q a b", q=128))
                tok_l = {"qs": p.val("qs")}

                @blk.vector
                def _(v, half=half, g=g, hb=hb, usp=usp, gsp=gsp, ob16=ob16):
                    p.wait(v, tok_l)
                    p.wait(v, tok_sc3, only=("dve",))
                    p.wait(v, prev.get(f"d3_st_free{hb}"))
                    p.inc(v.tensor_scalar(t["ruf"][:], usp[:], t["scl"][:, 2:3],
                                          t["fconst"][:, 2:3], ALU.mult, ALU.add),
                          "dve")
                    n = p.inc(v.tensor_scalar(t["rg2f"][:], gsp[:],
                                              t["scl"][:, 5:6],
                                              t["fconst"][:, 2:3],
                                              ALU.mult, ALU.add), "dve")
                    prev[f"d3_buf_free{hb}"] = {"dve": n}
                    v.wait_ge(p.S["dve"], p.C["dve"])
                    p.inc(v.tensor_scalar(t["ruf"][:], t["ruf"][:],
                                          t["fconst"][:, 2:3], None, ALU.subtract),
                          "dve")
                    p.inc(v.tensor_scalar(t["rg2f"][:], t["rg2f"][:],
                                          t["fconst"][:, 2:3], None, ALU.subtract),
                          "dve")
                    v.wait_ge(p.S["dve"], p.C["dve"])
                    p.inc(v.tensor_mul(t["ruf"][:], t["ruf"][:], t["rg2f"][:]), "dve")
                    v.wait_ge(p.S["dve"], p.C["dve"])
                    p.inc(v.reduce_max(t["tiny"][:, 2:3],
                                       t["ruf"][:].rearrange("q a b -> q (a b)"),
                                       axis=AX.X, apply_absolute_value=True), "dve")
                    v.wait_ge(p.S["dve"], p.C["dve"])
                    p.inc(v.tensor_max(t["gmax"][:, 4:5], t["gmax"][:, 4:5],
                                       t["tiny"][:, 2:3]), "dve")
                    p.inc(v.tensor_copy(ob16[:], t["ruf"][:]), "dve")
                tok_o2 = p.snap()

                @blk.gpsimd
                def _(gp, g=g, half=half, ob16=ob16):
                    p.wait(gp, tok_o2, only=("dve",))
                    dma(p, gp, "qg",
                        p.d["up_sp"].ap()[g, half * 1536:(half + 1) * 1536, :]
                        .rearrange("(a q) b -> q a b", q=128), ob16[:])
                prev[f"d3_st_free{hb}"] = {"qg": p.val("qg")}
        prev["gmax_tok"] = {"dve": p.val("dve")}

    emit_cc(p, 4, 1, 4, prev)

    @blk.vector
    def _(v):
        p.wait(v, {"dve": p.val("dve")})
        p.inc(v.tensor_mul(t["tiny"][0:1, 6:7], t["tiny"][0:1, 0:1],
                           t["tiny"][0:1, 4:5]), "dve")
        v.wait_ge(p.S["dve"], p.C["dve"])
        p.inc(v.tensor_mul(t["tiny"][0:1, 7:8], t["tiny"][0:1, 6:7],
                           t["gr_row"][0:1, 0:1]), "dve")
        v.wait_ge(p.S["dve"], p.C["dve"])
        p.inc(v.tensor_scalar(t["tiny"][0:1, 7:8], t["tiny"][0:1, 7:8],
                              1.0 / QMAX, 1e-12, ALU.mult, ALU.max), "dve")
        v.wait_ge(p.S["dve"], p.C["dve"])
        p.inc(v.reciprocal(t["tiny"][0:1, 8:9], t["tiny"][0:1, 7:8]), "dve")
        v.wait_ge(p.S["dve"], p.C["dve"])
        p.inc(v.tensor_mul(t["tiny"][0:1, 6:7], t["tiny"][0:1, 6:7],
                           t["tiny"][0:1, 8:9]), "dve")
    tvc4 = p.snap()
    bcast_col(p, tvc4, t["tiny"][0:1, 6:7], t["scl"][:, 6:7])
    tok_sc4 = bcast_col(p, tvc4, t["tiny"][0:1, 7:8], t["scl"][:, 7:8])

    emit_barrier(p)
    with ExitStack() as es_d4:
        p.sb("wdT_b", [128, NFF, 1024], BF16, es=es_d4)
        p.sb("osp0", [128, NFF, GRPD], BF16, es=es_d4)
        p.sb("osp1", [128, NFF, GRPD], BF16, es=es_d4)
        p.sb("rof", [128, 12, GRPD], F32, es=es_d4)
        p.sb("rob", [128, NFF, GRPD], BF16, es=es_d4)
        sxv = (t["rof"][:].rearrange("q a b -> q (a b)")[:, 0:4 * 1024]
               .rearrange("q (s d) -> q s d", d=1024))

        @blk.sync
        def _(sync):
            for ic in range(NFF):
                p.inc(sync.dma_start(out=t["wdT_b"][:, ic, :],
                                     in_=p.d["wq_w_down"].ap()
                                     [:, ic * 128:(ic + 1) * 128],
                                     transpose=True), "qs", 16)
        tok_wd = {"qs": p.val("qs")}

        for g in range(NGRPD):
            gb = g % 2
            osp = t[f"osp{gb}"]

            @blk.sync
            def _(sync, g=g, gb=gb, osp=osp):
                p.wait(sync, prev.get(f"d4_free{gb}"))
                dma(p, sync, "qs", osp[:],
                    p.d["up_sp"].ap()[g].rearrange("(a q) b -> q a b", q=128))
            tok_l = {"qs": p.val("qs")}

            for half in range(2):
                @blk.vector
                def _(v, half=half, osp=osp):
                    p.wait(v, tok_l)
                    p.wait(v, tok_sc4, only=("dve",))
                    p.wait(v, prev.get("d4_mm_done"))
                    osl = osp[:, half * 12:(half + 1) * 12, :]
                    p.inc(v.tensor_scalar(t["rof"][:], osl, t["scl"][:, 6:7],
                                          t["fconst"][:, 2:3], ALU.mult, ALU.add),
                          "dve")
                    v.wait_ge(p.S["dve"], p.C["dve"])
                    p.inc(v.tensor_scalar(t["rob"][:, half * 12:(half + 1) * 12, :],
                                          t["rof"][:], t["fconst"][:, 2:3], None,
                                          ALU.subtract), "dve")
            tok_ro = p.snap()
            prev[f"d4_free{gb}"] = {"dve": tok_ro["dve"]}

            @blk.sync
            def _(sync, g=g):
                p.wait(sync, tok_ro, only=("dve",))
                dma(p, sync, "qs", sxv,
                    p.d["x3"].ap()[g * GRPD:(g + 1) * GRPD, :]
                    .rearrange("(s q) d -> q s d", q=128))
            tok_lsx = {"qs": p.val("qs")}

            for s in range(NSUBD):
                for dh in range(2):
                    i = s * 2 + dh
                    bn = f"pb{2 + i % 2}"
                    bank = t[bn]

                    @blk.tensor
                    def _(te, s=s, dh=dh, bank=bank, bn=bn):
                        p.wait(te, tok_ro, only=("dve",))
                        p.wait(te, tok_wd)
                        p.wait(te, p.bank_tok.get(bn))
                        p.mmdt(te, "b")
                        for ic in range(NFF):
                            ii = te.matmul(bank[:],
                                           t["rob"][:, ic, s * 128:(s + 1) * 128],
                                           t["wdT_b"][:, ic, dh * 512:(dh + 1) * 512],
                                           start=(ic == 0), stop=(ic == NFF - 1))
                        p.inc(ii, "pe")
                    tpe = p.snap()

                    @blk.scalar
                    def _(a, s=s, dh=dh, bank=bank, bn=bn):
                        p.wait(a, tpe, only=("pe",))
                        p.wait(a, prev.get("tmw_free"), only=("dve",))
                        p.inc(a.activation(t["tmw"][:, s, dh * 512:(dh + 1) * 512],
                                           bank[:], AF.Identity,
                                           scale=t["scl"][:, 7:8]), "act")
                    p.bank_tok[bn] = p.snap()
            tok_y = p.snap()
            prev["d4_mm_done"] = {"pe": p.val("pe")}

            ta = emit_ln(p, t["tmw"], t["ob"],
                         wait0={"act": tok_y["act"]},
                         extra_apply_wait=prev.get("out_store"), nsub=NSUBD)
            prev["tmw_free"] = {"dve": p.val("dve")}

            @blk.vector
            def _(v, g=g):
                p.wait(v, ta, only=("act",))
                p.wait(v, tok_lsx)
                p.inc(v.tensor_add(t["ob"][:], t["ob"][:], sxv), "dve")
            tok_out = p.snap()

            @blk.gpsimd
            def _(gp, g=g):
                p.wait(gp, tok_out, only=("dve",))
                dma(p, gp, "qg",
                    p.d["out"].ap()[g * GRPD:(g + 1) * GRPD, :]
                    .rearrange("(s q) d -> q s d", q=128), t["ob"][:])
            prev["out_store"] = {"qg": p.val("qg")}
    return p.snap()



def build_program(phases, dev_taps=()):
    nc = bass.Bass("TRN2", target_bir_lowering=False)
    es = ExitStack()
    p = Prog(nc, es, phases, dev_taps)
    with es:
        p.dram_in("x", [NTOK, D])
        for nm, shape in SHARED_SPECS.items():
            p.dram_in(nm, shape)
        if "C" in phases and "AB" not in phases:
            p.dram_in("x2", [NTOK, D])
        else:
            p.dram_tmp("x2", [NTOK, D])
        if "D" in phases and "C" not in phases:
            p.dram_in("x3", [NTOK, D])
            p.dram_in("xn3", [NTOK, D])
        else:
            p.dram_tmp("x3", [NTOK, D])
            p.dram_tmp("xn3", [NTOK, D])
        p.dram_tmp("bounce", [16, 2 * GRP])
        if "D" in phases:
            for nm in ["w_up", "w_gate"]:
                p.dram_tmp("wq_" + nm, [3072, 1024], BF16)
            p.dram_tmp("wq_w_down", [1024, 3072], BF16)
            p.dram_tmp("up_sp", [NTOK // 512, 3072, 512], BF16)
            p.dram_tmp("gate_sp", [NTOK // 512, 3072, 512], BF16)
            for j in range(1, 5):
                p.dram_tmp(f"ccin{j}", [16, 128])
                p.dram_tmp(f"ccout{j}", [16, 128])
        if "dev_x1" in p.dev_taps:
            p.dram_out("dev_x1", [NTOK, D])
        if "dev_dbg" in p.dev_taps:
            if "D" in p.phases:
                p.dram_out("dbg_wupT", [128, NCH, DFF], BF16)
                p.dram_out("dbg_xq", [128, NCH, GRPD], BF16)
                p.dram_out("dbg_scl", [128, 8])
                p.dram_out("dbg_tiny", [128, 16])
                p.dram_out("dbg_grow", [1, 16])
                p.dram_out("dbg_gmax", [128, 8])
                p.dram_out("dbg_pre", [128, 40])
            p.dram_out("dbg_xn", [128, NSUB, 1024])
            p.dram_out("dbg_sx", [128, NSUB, 1024])
            p.dram_out("dbg_st", [128, 40])
            p.dram_out("dbg_gx", [128, NCH, GRP])
            p.dram_out("dbg_ga", [128, NCH, GRP])
            p.dram_out("dbg_at", [128, NCH, GRP])
            p.dram_out("dbg_ut", [128, NCH, GRP])
            p.dram_out("dbg_ht", [128, NCH, GRP])
            p.dram_out("dbg_y", [128, NSUB, 1024])
            p.dram_out("dbg_projf", [128, 2, GRP])
            p.dram_out("dbg_lg", [4, GRP])
            p.dram_out("dbg_ret", [128, NSUB * 4])
            p.dram_out("dbg_tmpf", [15, GRP])
            p.dram_out("dbg_y2", [128, NSUB, 1024])
        p.dram_out("out", [NTOK, D])

        es.enter_context(nc.Block())
        p.blk = FlatBlk(nc)
        alloc_common(p)
        emit_setup(p)
        if "AB" in p.phases:
            with ExitStack() as es_ab:
                wtok = emit_l0_weights(p, es_ab)
                emit_ab(p, es_ab, {"qg": wtok["qg"]})
        ctok = {"dve": 0}
        if "C" in p.phases:
            emit_barrier(p)
            with ExitStack() as es_c:
                wtok_c = emit_l1_weights(p, es_c)
                emit_c(p, es_c, {"dve": wtok_c["dve"]})
                ctok = {"dve": p.val("dve")}
        if "D" in p.phases:
            emit_barrier(p)
            with ExitStack() as es_d:
                dqtok = emit_dequant(p, es_d)
            emit_barrier(p)
            with ExitStack() as es_d:
                emit_d(p, es_d, ctok, dqtok)

        @p.blk.gpsimd
        def _(gp):
            gp.wait_ge(p.sem("qg"), p.val("qg"))
            gp.wait_ge(p.sem("qs"), p.val("qs"))
    return p


def kernel(**inputs):
    shared = prepare_shared_inputs(inputs)
    x = np.ascontiguousarray(np.asarray(inputs["L0_attn_x"], np.float32)
                             .reshape(16, SEQ, D))
    p = build_program(("AB", "C", "D"))
    in_maps = []
    for c in range(N_CORES):
        m = dict(shared)
        m["x"] = np.ascontiguousarray(x[2 * c:2 * c + 2].reshape(NTOK, D))
        in_maps.append(m)
    res = run_bass_kernel_spmd(p.nc, in_maps, core_ids=list(range(N_CORES)))
    out = np.concatenate([r["out"].reshape(2, SEQ, D) for r in res.results], 0)
    return out
